# version 13
# baseline (speedup 1.0000x reference)
import numpy as np
import ml_dtypes
from contextlib import ExitStack
import concourse.bass as bass
import concourse.mybir as mybir
from concourse.bass_utils import run_bass_kernel_spmd

F32 = mybir.dt.float32
BF16 = mybir.dt.bfloat16
AF = mybir.ActivationFunctionType
ALU = mybir.AluOpType
AX = mybir.AxisListType

T = 2048
D = 1024
H = 8
DH = 128
E = 8
HID = 2048
NCH = T // 128
MC = D // 128
HC = HID // 128
NSLAB = 4
ENGINES = ("sync", "gpsimd", "tensor", "vector", "scalar")


class Sem:
    __slots__ = ("h", "n")
    def __init__(self, h):
        self.h = h
        self.n = 0


class Plan:
    def __init__(self):
        self.prog = {e: [] for e in ENGINES}
        self.seen = {e: {} for e in ENGINES}

    def op(self, eng, fn, waits=(), inc=None, tag=""):
        ws = []
        seen = self.seen[eng]
        for w in waits:
            if w is None:
                continue
            s, v = w
            if seen.get(s, 0) >= v:
                continue
            seen[s] = v
            ws.append((s, v))
        mil = None
        if inc is not None:
            s, k = inc
            s.n += k
            mil = (s, s.n)
        self.prog[eng].append((fn, ws, inc, tag))
        return mil

    def replay(self, eng_name, eng):
        for fn, ws, inc, tag in self.prog[eng_name]:
            for s, v in ws:
                eng.wait_ge(s.h, v)
            ins = fn(eng)
            if inc is not None:
                s, k = inc
                ins.then_inc(s.h, k)


def _rope_tables():
    half = DH // 2
    pos = np.arange(T, dtype=np.float64)[None, :]
    freqs = np.power(10000.0, -np.arange(half, dtype=np.float64) / half)[:, None]
    rad = freqs * pos
    s = 1.0 / np.sqrt(float(DH))
    cos = np.cos(rad) * s
    sin = np.sin(rad) * s
    cos2 = np.concatenate([cos, cos], axis=0).astype(ml_dtypes.bfloat16)
    ssin2 = np.concatenate([-sin, sin], axis=0).astype(ml_dtypes.bfloat16)
    return np.ascontiguousarray(cos2), np.ascontiguousarray(ssin2)


def _swap_matrix():
    sw = np.zeros((128, 128), dtype=ml_dtypes.bfloat16)
    for d in range(128):
        sw[(d + 64) % 128, d] = 1.0
    return sw


def _trimask():
    m = np.zeros((128, 128), dtype=ml_dtypes.bfloat16)
    i = np.arange(128)
    m[i[:, None] < i[None, :]] = -1e30
    return m


def _build(debug=()):
    nc = bass.Bass("TRN2", target_bir_lowering=False)

    x_d = nc.dram_tensor("x", [T, D], F32, kind="ExternalInput").ap()
    wq_d = nc.dram_tensor("wq", [D, H, DH], F32, kind="ExternalInput").ap()
    wk_d = nc.dram_tensor("wk", [D, H, DH], F32, kind="ExternalInput").ap()
    wv_d = nc.dram_tensor("wv", [D, H, DH], F32, kind="ExternalInput").ap()
    wo_d = nc.dram_tensor("wo", [H, DH, D], F32, kind="ExternalInput").ap()
    wg_d = nc.dram_tensor("wg", [D, E], F32, kind="ExternalInput").ap()
    w1_d = nc.dram_tensor("w1", [E, D, HID], F32, kind="ExternalInput").ap()
    w2_d = nc.dram_tensor("w2", [E, HID, D], F32, kind="ExternalInput").ap()
    cos_d = nc.dram_tensor("cos2", [128, T], BF16, kind="ExternalInput").ap()
    sin_d = nc.dram_tensor("ssin2", [128, T], BF16, kind="ExternalInput").ap()
    idn_d = nc.dram_tensor("idn", [128, 128], BF16, kind="ExternalInput").ap()
    swp_d = nc.dram_tensor("swp", [128, 128], BF16, kind="ExternalInput").ap()
    tri_d = nc.dram_tensor("tri", [128, 128], BF16, kind="ExternalInput").ap()
    y_d = nc.dram_tensor("y", [T, D], F32, kind="ExternalOutput").ap()

    taps = {}
    def tapdecl(name, shape, dtype):
        if name in debug:
            taps[name] = nc.dram_tensor("tap_" + name, shape, dtype,
                                        kind="ExternalOutput").ap()
    tapdecl("hT", [128, MC, T], BF16)
    tapdecl("lnst", [128, 8 * NCH], F32)
    tapdecl("lg", [128, NCH * E], F32)
    tapdecl("pr", [128, NCH * E], F32)
    tapdecl("m8", [128, NCH * E], F32)
    tapdecl("v", [128, NCH, D], BF16)
    tapdecl("q0", [128, T], BF16)
    tapdecl("k0", [128, T], BF16)
    tapdecl("pT0", [128, 16, 512], BF16)
    tapdecl("p15", [128, T], BF16)
    tapdecl("smst", [128, 48], F32)
    tapdecl("xattn", [128, NCH, D], F32)
    tapdecl("h2T", [128, MC, T], BF16)
    tapdecl("c", [128, NCH, E], F32)

    with ExitStack() as ctx:
        En = ctx.enter_context

        x_sb = En(nc.sbuf_tensor("x_sb", [128, NCH * D], F32))
        hT_sb = En(nc.sbuf_tensor("hT_sb", [128, MC * T], BF16))
        v_sb = En(nc.sbuf_tensor("v_sb", [128, NCH * D], BF16))
        arenaA = En(nc.sbuf_tensor("arenaA", [128, 16384], BF16))
        arenaB = En(nc.sbuf_tensor("arenaB", [128, 16384], BF16))
        cos_sb = En(nc.sbuf_tensor("cos_sb", [128, T], BF16))
        sin_sb = En(nc.sbuf_tensor("sin_sb", [128, T], BF16))
        o_sb = [En(nc.sbuf_tensor(f"o_sb{i}", [128, 512], BF16)) for i in range(2)]
        idn_sb = En(nc.sbuf_tensor("idn_sb", [128, 128], BF16))
        swp_sb = En(nc.sbuf_tensor("swp_sb", [128, 128], BF16))
        tri_sb = En(nc.sbuf_tensor("tri_sb", [128, 128], BF16))
        diag4_sb = En(nc.sbuf_tensor("diag4_sb", [128, 256], BF16))
        sqtrash = En(nc.sbuf_tensor("sqtrash", [128, 1024], BF16))
        wg_sb = En(nc.sbuf_tensor("wg_sb", [128, MC * E], BF16))
        ln_st = En(nc.sbuf_tensor("ln_st", [128, 8 * NCH], F32))
        sm_st = En(nc.sbuf_tensor("sm_st", [128, 12], F32))
        lg_sb = En(nc.sbuf_tensor("lg_sb", [128, NCH * E], F32))
        pr_sb = En(nc.sbuf_tensor("pr_sb", [128, NCH * E], F32))
        c_sb = En(nc.sbuf_tensor("c_sb", [128, NCH * E], F32))
        rt_sb = En(nc.sbuf_tensor("rt_sb", [128, 4 * NCH], F32))

        aA = arenaA[:]
        pT3 = aA[:, 0:8192].rearrange("p (jc i) -> p jc i", jc=16)
        wv3 = aA[:, 8192:16384].rearrange("p (mc d) -> p mc d", mc=MC)
        qraw = [aA[:, 8192 + i * 512:8704 + i * 512] for i in range(2)]
        swb = [aA[:, 9216 + i * 512:9728 + i * 512] for i in range(2)]
        rt1 = aA[:, 10240:10752]
        rt2 = aA[:, 10752:11264]
        w13 = aA.rearrange("p (mc hh) -> p mc hh", mc=MC)

        aB = arenaB[:]
        q_sb = aB[:, 0:2048]
        k_sb = aB[:, 2048:4096]
        p_sb = [aB[:, 4096:6144], aB[:, 6144:8192], aA[:, 8192:10240]]
        htok = [aB[:, 8192 + i * 1024:9216 + i * 1024] for i in range(2)]
        wqh = [aB[:, 10240 + i * 1024:11264 + i * 1024] for i in range(2)]
        wkh = [aB[:, 12288 + i * 1024:13312 + i * 1024] for i in range(2)]
        woh = [aB[:, 14336 + i * 1024:15360 + i * 1024] for i in range(2)]
        w23 = aB.rearrange("p (hc d) -> p hc d", hc=HC)

        x2 = x_sb[:].rearrange("p (c f) -> p c f", c=NCH)
        hT3 = hT_sb[:].rearrange("p (mc t) -> p mc t", mc=MC)
        v3 = v_sb[:].rearrange("p (c f) -> p c f", c=NCH)
        act3 = v3
        wg3 = wg_sb[:].rearrange("p (mc e) -> p mc e", mc=MC)
        lg3 = lg_sb[:].rearrange("p (c e) -> p c e", c=NCH)
        pr3 = pr_sb[:].rearrange("p (c e) -> p c e", c=NCH)
        m83 = lg_sb[:].rearrange("p (c e) -> p c e", c=NCH)
        c3 = c_sb[:].rearrange("p (c e) -> p c e", c=NCH)

        s_ps = En(nc.psum_tensor("s_ps", [128, 2048], F32))
        mi_ps = En(nc.psum_tensor("mi_ps", [128, 512], F32))
        tp_ps = En(nc.psum_tensor("tp_ps", [128, 512], F32))
        pp_ps = [En(nc.psum_tensor(f"pp{i}", [128, 512], F32)) for i in range(2)]

        dIN = Sem(En(nc.semaphore("dIN")))
        dX = [Sem(En(nc.semaphore(f"dX{i}"))) for i in range(4)]
        dWV = Sem(En(nc.semaphore("dWV")))
        dWG = Sem(En(nc.semaphore("dWG")))
        dWH = [Sem(En(nc.semaphore(f"dWH{i}"))) for i in range(2)]
        dW1 = Sem(En(nc.semaphore("dW1")))
        dW2 = Sem(En(nc.semaphore("dW2")))
        dOUT = Sem(En(nc.semaphore("dOUT")))
        sT = Sem(En(nc.semaphore("sT")))
        sV = Sem(En(nc.semaphore("sV")))
        sA = Sem(En(nc.semaphore("sA")))
        sG = Sem(En(nc.semaphore("sG")))

        P = Plan()
        tp_par = {0: None, 1: None}

        xr = x_d.rearrange("(c p) f -> p c f", p=128)
        m_xg = []
        for g in range(4):
            m_xg.append(P.op("sync", lambda e, g=g: e.dma_start(
                out=x2[:, 4 * g:4 * (g + 1), :], in_=xr[:, 4 * g:4 * (g + 1), :]),
                inc=(dX[g], 16), tag=f"ld_x{g}"))
        m_x = m_xg[3]
        P.op("sync", lambda e: e.dma_start(out=cos_sb[:], in_=cos_d), inc=(dIN, 16))
        P.op("sync", lambda e: e.dma_start(out=sin_sb[:], in_=sin_d), inc=(dIN, 16))
        P.op("sync", lambda e: e.dma_start(out=idn_sb[:], in_=idn_d), inc=(dIN, 16))
        P.op("sync", lambda e: e.dma_start(out=swp_sb[:], in_=swp_d), inc=(dIN, 16))
        m_tbl = P.op("sync", lambda e: e.dma_start(out=tri_sb[:], in_=tri_d), inc=(dIN, 16))
        m_cos = m_sin = m_idn = m_swp = m_tri = m_tbl

        m_wv = P.op("gpsimd", lambda e: e.dma_start(
            out=wv3, in_=wv_d.rearrange("(mc p) h d -> p mc (h d)", p=128)),
            inc=(dWV, 16), tag="ld_wv")
        m_wg = P.op("gpsimd", lambda e: e.dma_start(
            out=wg3, in_=wg_d.rearrange("(mc p) e -> p mc e", p=128)),
            inc=(dWG, 16), tag="ld_wg")

        def load_head_weights(h, waits):
            b = h % 2
            P.op("gpsimd", lambda e, h=h, b=b: e.dma_start(
                out=wqh[b].rearrange("p (mc d) -> p mc d", mc=MC),
                in_=wq_d[:, h, :].rearrange("(mc p) d -> p mc d", p=128)),
                waits=waits, inc=(dWH[b], 16), tag=f"ld_wq{h}")
            P.op("gpsimd", lambda e, h=h, b=b: e.dma_start(
                out=wkh[b].rearrange("p (mc d) -> p mc d", mc=MC),
                in_=wk_d[:, h, :].rearrange("(mc p) d -> p mc d", p=128)),
                inc=(dWH[b], 16), tag=f"ld_wk{h}")
            mo = P.op("gpsimd", lambda e, h=h, b=b: e.dma_start(
                out=woh[b], in_=wo_d[h]), inc=(dWH[b], 16), tag=f"ld_wo{h}")
            return mo

        m_wh = [None] * (H + 1)
        m_wh[0] = load_head_weights(0, [])

        def slot(i, c):
            return ln_st[:, i * NCH + c:i * NCH + c + 1]

        def SS(i):
            return ln_st[:, i * NCH:(i + 1) * NCH]

        def emit_layernorm(wait_src_fn, phase):
            m_sum = m_sq = None
            for c in range(NCH):
                m_sum = P.op("vector", lambda e, c=c: e.reduce_sum(
                    slot(0, c), x2[:, c, :], axis=AX.X),
                    waits=[wait_src_fn(c)], inc=(sV, 1) if c == NCH - 1 else None,
                    tag=f"{phase}_sum{c}")
                m_sq = P.op("scalar", lambda e, c=c: e.activation(
                    sqtrash[:], x2[:, c, :], AF.Square, accum_out=slot(1, c)),
                    waits=[wait_src_fn(c), m_sq], inc=(sA, 1),
                    tag=f"{phase}_sq{c}")
            mm1 = P.op("vector", lambda e: e.tensor_scalar_mul(SS(2), SS(0), 1.0 / D),
                       waits=[m_sum, m_sq], inc=(sV, 1), tag=f"{phase}_m")
            mm2 = P.op("vector", lambda e: e.tensor_scalar_mul(SS(3), SS(2), -1.0),
                       waits=[mm1], inc=(sV, 1))
            mm3 = P.op("vector", lambda e: e.tensor_scalar_mul(SS(4), SS(1), 1.0 / D),
                       inc=(sV, 1))
            mm4 = P.op("vector", lambda e: e.tensor_tensor(SS(7), SS(2), SS(2), ALU.mult),
                       waits=[mm1], inc=(sV, 1))
            mm5 = P.op("vector", lambda e: e.tensor_sub(SS(4), SS(4), SS(7)),
                       waits=[mm3, mm4], inc=(sV, 1))
            m_veps = P.op("vector", lambda e: e.tensor_scalar_add(SS(4), SS(4), 1e-5),
                          waits=[mm5], inc=(sV, 1), tag=f"{phase}_veps")
            m_ln = P.op("scalar", lambda e: e.activation(SS(7), SS(4), AF.Ln),
                        waits=[m_veps], inc=(sA, 1), tag=f"{phase}_ln")
            m_rstd = P.op("scalar", lambda e: e.activation(SS(5), SS(7), AF.Exp, scale=-0.5),
                          waits=[m_ln], inc=(sA, 1), tag=f"{phase}_rstd")
            m_bias = P.op("vector", lambda e: e.tensor_tensor(SS(6), SS(3), SS(5), ALU.mult),
                          waits=[m_rstd], inc=(sV, 1), tag=f"{phase}_bias")
            m_tev = [None] * (2 * NCH)
            m_ttr = [None] * (2 * NCH)
            for c in range(NCH):
                b = c % 2
                m_ap = P.op("vector", lambda e, c=c, b=b: e.tensor_scalar(
                    htok[b], x2[:, c, :], slot(5, c), slot(6, c), ALU.mult, ALU.add),
                    waits=[m_bias, m_rstd, m_ttr[2 * c - 4] if c >= 2 else None],
                    inc=(sV, 1), tag=f"{phase}_apply{c}")
                for g in range(2):
                    half = g
                    mm = None
                    for q in range(4):
                        mc = g * 4 + q
                        mm = P.op("tensor", lambda e, b=b, mc=mc, q=q, half=half: e.matmul(
                            (tp_ps if half == 0 else mi_ps)[:, q * 128:(q + 1) * 128],
                            htok[b][:, mc * 128:(mc + 1) * 128], idn_sb[:],
                            start=True, stop=True),
                            waits=[m_ap if q == 0 else None,
                                   m_idn if c == 0 and g == 0 and q == 0 else None,
                                   tp_par[half] if q == 0 else None],
                            inc=(sT, 1) if q == 3 else None, tag=f"{phase}_tr{c}_{mc}")
                    m_ttr[2 * c + g] = mm
                    m_tev[2 * c + g] = P.op("vector", lambda e, c=c, g=g, half=half: e.tensor_copy(
                        hT3[:, g * 4:(g + 1) * 4, c * 128:(c + 1) * 128],
                        (tp_ps if half == 0 else mi_ps)[:].rearrange(
                            "p (q t) -> p q t", q=4)),
                        waits=[mm], inc=(sV, 1), tag=f"{phase}_tev{c}_{g}")
                    tp_par[half] = m_tev[2 * c + g]
            return m_tev[2 * NCH - 1], m_ttr[2 * NCH - 1], [m_tev[2 * c + 1] for c in range(NCH)]

        mil_h1T, mil_h1T_tr, tev1 = emit_layernorm(lambda c: m_xg[c // 4], "ln1")
        if "lnst" in debug:
            P.op("sync", lambda e: e.dma_start(out=taps["lnst"], in_=ln_st[:]),
                 waits=[mil_h1T], inc=(dOUT, 16))
        if "hT" in debug:
            P.op("sync", lambda e: e.dma_start(out=taps["hT"], in_=hT3),
                 waits=[mil_h1T], inc=(dOUT, 16))

        m_vev = [None, None]
        last_v_mm = None
        for c in range(NCH):
            for nh in range(2):
                pp = (c * 2 + nh) % 2
                mms = None
                for mc in range(MC):
                    mms = P.op("tensor", lambda e, c=c, nh=nh, mc=mc, pp=pp: e.matmul(
                        pp_ps[pp][:], hT3[:, mc, c * 128:(c + 1) * 128],
                        wv3[:, mc, nh * 512:(nh + 1) * 512],
                        start=(mc == 0), stop=(mc == MC - 1)),
                        waits=[tev1[c] if mc == 0 else None,
                               m_wv if mc == 0 and c == 0 else None,
                               m_vev[pp] if mc == 0 else None],
                        inc=(sT, 1) if mc == MC - 1 else None, tag=f"v_mm{c}_{nh}")
                last_v_mm = mms
                m_vev[pp] = P.op("scalar", lambda e, c=c, nh=nh, pp=pp: e.activation(
                    v3[:, c, nh * 512:(nh + 1) * 512], pp_ps[pp][:], AF.Copy),
                    waits=[mms], inc=(sA, 1), tag=f"v_ev{c}_{nh}")
        mil_v_all = (m_vev[0], m_vev[1])
        if "v" in debug:
            P.op("sync", lambda e: e.dma_start(out=taps["v"], in_=v3),
                 waits=[m_vev[0], m_vev[1]], inc=(dOUT, 16))

        m_phB_done = [None] * H
        m_exp_bank = {}
        m_xadd_slab = {}
        m_diag_mm = {"last": None, 0: None, 1: None, 2: None}
        m_xadd = None
        m_xadd_bank = [None, None, None]
        m_oev = [None, None]
        m_proj_mm = [None, None]
        m_rope_done = [None] * H

        def SUM(par, k=None):
            o = par * 4
            return sm_st[:, o + k:o + k + 1] if k is not None else sm_st[:, o:o + 4]
        def SSUM(par):
            return sm_st[:, 8 + par:9 + par]
        def RR(par):
            return sm_st[:, 10 + par:11 + par]

        for h in range(H):
            b = h % 2
            if h + 1 < H:
                m_wh[h + 1] = load_head_weights(
                    h + 1, [m_rope_done[h - 1] if h >= 1 else None,
                            m_proj_mm[0] if h >= 1 else None,
                            m_proj_mm[1] if h >= 1 else None])

            wq3 = wqh[b].rearrange("p (mc d) -> p mc d", mc=MC)
            wk3 = wkh[b].rearrange("p (mc d) -> p mc d", mc=MC)
            raw_slots = [pp_ps[0][:], s_ps[:, 0:512], s_ps[:, 512:1024],
                         s_ps[:, 1024:1536]]
            swp_slots = [pp_ps[1][:], tp_ps[:]]
            m_raw_ev = [None] * 4
            m_swp_ev = [None] * 2
            m_rt1 = [None] * 2
            m_rope = None

            def emit_qk_proj(idx):
                mat, tt = divmod(idx, 4)
                w3 = wq3 if mat == 0 else wk3
                rs = idx % 4
                mms = None
                for mc in range(MC):
                    mms = P.op("tensor", lambda e, w3=w3, tt=tt, mc=mc, rs=rs: e.matmul(
                        raw_slots[rs], w3[:, mc, :], hT3[:, mc, tt * 512:(tt + 1) * 512],
                        start=(mc == 0), stop=(mc == MC - 1)),
                        waits=[m_wh[h] if mc == 0 else None,
                               tev1[tt * 4 + 3] if mc == 0 and h == 0 else None,
                               m_raw_ev[rs] if mc == 0 else None,
                               m_exp_bank.get(rs - 1) if mc == 0 and rs >= 1 else None,
                               m_oev[0] if mc == 0 and idx == 0 else None,
                               m_oev[1] if mc == 0 and idx == 0 else None,
                               m_xadd if mc == 0 and idx == 0 else None,
                               m_phB_done[h - 1] if mc == 0 and idx == 0 and h >= 1 else None,
                               mil_v_all[0] if mc == 0 and idx == 0 and h == 0 else None,
                               mil_v_all[1] if mc == 0 and idx == 0 and h == 0 else None],
                        inc=(sT, 1) if mc == MC - 1 else None, tag=f"h{h}qk{idx}")
                m_raw_ev[rs] = P.op("scalar", lambda e, idx=idx, rs=rs: e.activation(
                    qraw[idx % 2], raw_slots[rs], AF.Copy),
                    waits=[mms, m_rt1[idx % 2],
                           last_v_mm if h == 0 and idx == 0 else None],
                    inc=(sA, 1), tag=f"h{h}qraw{idx}")

            def emit_qk_rope(idx):
                nonlocal m_rope
                mat, tt = divmod(idx, 4)
                dst = q_sb if mat == 0 else k_sb
                rs = idx % 4
                ss = idx % 2
                msw = P.op("tensor", lambda e, ss=ss: e.matmul(
                    swp_slots[ss], swp_sb[:], qraw[ss], start=True, stop=True),
                    waits=[m_raw_ev[rs], m_swp, m_swp_ev[ss],
                           tp_par[0] if ss == 1 and idx == 1 else None,
                           m_xadd if idx == 0 else None],
                    inc=(sT, 1), tag=f"h{h}sw{idx}")
                m_swp_ev[ss] = P.op("vector", lambda e, ss=ss: e.tensor_copy(
                    swb[ss], swp_slots[ss]),
                    waits=[msw], inc=(sV, 1), tag=f"h{h}swev{idx}")
                m_rt1[ss] = P.op("vector", lambda e, ss=ss, tt=tt: e.tensor_mul(
                    rt1, qraw[ss], cos_sb[:, tt * 512:(tt + 1) * 512]),
                    waits=[m_cos, m_sin, m_raw_ev[rs],
                           m_phB_done[h - 1] if h >= 1 and idx == 0 else None,
                           last_v_mm if h == 0 and idx == 0 else None],
                    inc=(sV, 1), tag=f"h{h}rp1_{idx}")
                P.op("vector", lambda e, ss=ss, tt=tt: e.tensor_mul(
                    rt2, swb[ss], sin_sb[:, tt * 512:(tt + 1) * 512]))
                m_rope = P.op("vector", lambda e, dst=dst, tt=tt: e.tensor_add(
                    dst[:, tt * 512:(tt + 1) * 512], rt1, rt2),
                    inc=(sV, 1), tag=f"h{h}rp3_{idx}")

            for idx in range(8):
                emit_qk_proj(idx)
                if idx >= 1:
                    emit_qk_rope(idx - 1)
            emit_qk_rope(7)
            m_rope_done[h] = m_rope
            m_sbank_free = {0: m_raw_ev[1], 1: m_raw_ev[2], 2: m_raw_ev[3]}

            if h == 0 and "q0" in debug:
                P.op("sync", lambda e: e.dma_start(out=taps["q0"], in_=q_sb),
                     waits=[m_rope_done[0]], inc=(dOUT, 16))
            if h == 0 and "k0" in debug:
                P.op("sync", lambda e: e.dma_start(out=taps["k0"], in_=k_sb),
                     waits=[m_rope_done[0]], inc=(dOUT, 16))

            m_phB_local = None
            m_nmx = {}
            m_expm = {}
            m_tail = {}
            m_sumfree = {}
            m_trp_par = {0: None, 1: None}
            m_tgrp = {}
            m_ptev = {}
            m_ptev_last = None

            def emit_scores_bank(ci, k):
                par = ci % 2
                pbuf = ci % 3
                ph = k * 512
                jlen = (ci + 1) * 128
                j0 = k * 512
                j1 = min(j0 + 512, jlen)
                w = j1 - j0
                msc = P.op("tensor", lambda e, ci=ci, j0=j0, j1=j1, ph=ph, w=w: e.matmul(
                    s_ps[:, ph:ph + w], q_sb[:, ci * 128:(ci + 1) * 128],
                    k_sb[:, j0:j1], start=True, stop=True),
                    waits=[m_rope_done[h] if ci == 0 and k == 0 else None,
                           m_exp_bank.get(k),
                           m_sbank_free.get(k)],
                    inc=(sT, 1), tag=f"h{h}s{ci}_{k}")
                last = (j1 == jlen)
                m_msk = None
                if last:
                    lo = ph + (w - 128)
                    m_msk = P.op("vector", lambda e, lo=lo: e.tensor_tensor(
                        s_ps[:, lo:lo + 128], s_ps[:, lo:lo + 128],
                        tri_sb[:], ALU.add),
                        waits=[msc, m_tri], inc=(sV, 1), tag=f"h{h}msk{ci}")
                m_expm[(ci, k)] = P.op("scalar", lambda e, k=k, j0=j0, ph=ph, w=w, par=par, pbuf=pbuf: e.activation(
                    p_sb[pbuf][:, j0:j0 + w], s_ps[:, ph:ph + w], AF.Exp,
                    accum_out=SUM(par, k)),
                    waits=[msc, m_msk, m_diag_mm[pbuf],
                           m_sumfree.get(par),
                           m_rope_done[h] if pbuf == 2 else None],
                    inc=(sA, 1), tag=f"h{h}exp{ci}_{k}")
                m_exp_bank[k] = m_expm[(ci, k)]

            def emit_tail(ci):
                par = ci % 2
                dpar = ci % 2
                K = (ci + 1 + 3) // 4
                t5 = P.op("vector", lambda e, par=par, K=K: e.reduce_sum(
                    SSUM(par), SUM(par)[:, 0:K], axis=AX.X),
                    waits=[m_expm[(ci, K - 1)]], inc=(sV, 1), tag=f"h{h}tl_s{ci}")
                m_sumfree[par] = t5
                t6 = P.op("vector", lambda e, par=par: e.reciprocal(RR(par), SSUM(par)),
                          waits=[t5], inc=(sV, 1), tag=f"h{h}tl_r{ci}")
                m_tail[ci] = P.op("vector", lambda e, dpar=dpar: e.tensor_scalar_mul(
                    diag4_sb[:, dpar * 128:(dpar + 1) * 128], idn_sb[:], RR(dpar)),
                    waits=[t6, m_trp_par[dpar], m_idn],
                    inc=(sV, 1), tag=f"h{h}tl_dg{ci}")

            def emit_transposes(ci):
                nonlocal m_ptev_last
                pbuf = ci % 3
                cl = ci % 4
                sl = ci // 4
                njc = ci + 1
                for g0 in range(0, njc, 4):
                    g1 = min(g0 + 4, njc)
                    k = g0 // 4
                    half = k % 2
                    dst_ps = tp_ps if half == 0 else mi_ps
                    mmg = None
                    for jc in range(g0, g1):
                        mmg = P.op("tensor", lambda e, pbuf=pbuf, jc=jc, g0=g0, k=k, dst_ps=dst_ps: e.matmul(
                            dst_ps[:, (jc - g0) * 128:(jc - g0 + 1) * 128],
                            p_sb[pbuf][:, jc * 128:(jc + 1) * 128],
                            diag4_sb[:, (ci % 2) * 128:(ci % 2 + 1) * 128],
                            start=True, stop=True),
                            waits=[m_tail[ci] if jc == g0 else None,
                                   tp_par[half] if jc == g0 else None,
                                   m_swp_ev[1] if jc == g0 and half == 0 else None,
                                   m_phB_local if jc == g0 and g0 == 0 and cl == 0 else None,
                                   m_oev[0] if jc == g0 and half == 1 else None,
                                   m_oev[1] if jc == g0 and half == 1 else None,
                                   m_phB_done[h - 1] if jc == g0 and g0 == 0 and ci == 0 and h >= 1 else None,
                                   last_v_mm if jc == g0 and g0 == 0 and ci == 0 and h == 0 else None],
                            inc=(sT, 1) if jc == g1 - 1 else None,
                            tag=f"h{h}pt{ci}_{jc}")
                    m_diag_mm[pbuf] = mmg
                    m_diag_mm["last"] = mmg
                    m_trp_par[ci % 2] = mmg
                    if half == 0:
                        m_ptev_last = P.op("scalar", lambda e, cl=cl, g0=g0, g1=g1: e.activation(
                            pT3[:, g0:g1, cl * 128:(cl + 1) * 128],
                            tp_ps[:, 0:(g1 - g0) * 128].rearrange(
                                "p (g i) -> p g i", g=g1 - g0), AF.Copy),
                            waits=[mmg], inc=(sA, 1), tag=f"h{h}ptev{ci}_{g0}")
                    else:
                        m_ptev_last = P.op("vector", lambda e, cl=cl, g0=g0, g1=g1: e.tensor_copy(
                            pT3[:, g0:g1, cl * 128:(cl + 1) * 128],
                            mi_ps[:, 0:(g1 - g0) * 128].rearrange(
                                "p (g i) -> p g i", g=g1 - g0)),
                            waits=[mmg], inc=(sV, 1), tag=f"h{h}ptev{ci}_{g0}")
                    tp_par[half] = m_ptev_last

            def emit_phaseB(sl):
                nonlocal m_phB_local, m_xadd
                par = sl % 2
                m_omm = None
                for jc in range(sl * 4 + 4):
                    lo = max(0, jc - sl * 4) * 128
                    m_omm = P.op("tensor", lambda e, jc=jc, lo=lo, h=h: e.matmul(
                        mi_ps[:, lo:512] if lo else mi_ps[:],
                        v3[:, jc, h * 128:(h + 1) * 128],
                        pT3[:, jc, lo:512] if lo else pT3[:, jc, :],
                        start=(jc == 0), stop=(jc == sl * 4 + 3)),
                        waits=[m_ptev_last if jc == 0 else None,
                               m_oev[1 - par] if jc == 0 else None,
                               tp_par[0] if jc == 0 else None,
                               tp_par[1] if jc == 0 else None],
                        inc=(sT, 1) if jc == sl * 4 + 3 else None, tag=f"h{h}o{sl}_{jc}")
                m_phB_local = m_omm
                m_phB_done[h] = m_omm
                m_oev[par] = P.op("vector", lambda e, par=par: e.tensor_copy(
                    o_sb[par][:], mi_ps[:]),
                    waits=[m_omm, m_proj_mm[par]], inc=(sV, 1), tag=f"h{h}oev{sl}")
                tp_par[1] = m_oev[par]
                pbanks = [pp_ps[1], pp_ps[0], tp_ps]
                for cl in range(4):
                    for mh in range(2):
                        bank = (cl * 2 + mh) % 3
                        mpj = P.op("tensor", lambda e, par=par, cl=cl, mh=mh, b=b, bank=bank, pbanks=pbanks: e.matmul(
                            pbanks[bank][:], o_sb[par][:, cl * 128:(cl + 1) * 128],
                            woh[b][:, mh * 512:(mh + 1) * 512],
                            start=True, stop=True),
                            waits=[m_oev[par], m_xadd_bank[bank],
                                   tp_par[0] if bank == 2 else None,
                                   m_raw_ev[0] if cl == 0 and mh == 1 else None,
                                   m_swp_ev[0] if cl == 0 and mh == 0 else None],
                            inc=(sT, 1), tag=f"h{h}pj{sl}_{cl}_{mh}")
                        m_proj_mm[par] = mpj
                        m_xadd = P.op("vector", lambda e, sl=sl, cl=cl, mh=mh, bank=bank, pbanks=pbanks: e.tensor_tensor(
                            x2[:, sl * 4 + cl, mh * 512:(mh + 1) * 512],
                            x2[:, sl * 4 + cl, mh * 512:(mh + 1) * 512],
                            pbanks[bank][:], ALU.add),
                            waits=[mpj], inc=(sV, 1), tag=f"h{h}xadd{sl}_{cl}_{mh}")
                        m_xadd_bank[bank] = m_xadd
                        if bank == 2:
                            tp_par[0] = m_xadd
                if h == H - 1:
                    m_xadd_slab[sl] = m_xadd

            for step in range(18):
                if step < 16:
                    ci = step
                    K = ((ci + 1) * 128 + 511) // 512
                    for k in range(K):
                        emit_scores_bank(ci, k)
                if 1 <= step <= 16:
                    emit_tail(step - 1)
                if step >= 6 and (step - 6) % 4 == 0 and step < 18:
                    emit_phaseB((step - 6) // 4)
                if step >= 2:
                    emit_transposes(step - 2)
            emit_phaseB(3)

            if h == 0 and "pT0" in debug:
                P.op("sync", lambda e: e.dma_start(out=taps["pT0"], in_=pT3),
                     waits=[m_ptev_last], inc=(dOUT, 16))
            if h == 0 and "p15" in debug:
                P.op("sync", lambda e: e.dma_start(out=taps["p15"], in_=p_sb[1]),
                     waits=[m_expm[(15, 3)], m_tail[15]], inc=(dOUT, 16))
            if h == 0 and "smst" in debug:
                P.op("sync", lambda e: e.dma_start(out=taps["smst"], in_=sm_st[:]),
                     waits=[m_tail[15]], inc=(dOUT, 16))

        if "xattn" in debug:
            P.op("sync", lambda e: e.dma_start(out=taps["xattn"], in_=x2),
                 waits=[m_xadd], inc=(dOUT, 16))

        mil_h2T, mil_h2T_tr, tev2 = emit_layernorm(lambda c: m_xadd_slab.get(c // 4, m_xadd), "ln2")
        if "h2T" in debug:
            P.op("sync", lambda e: e.dma_start(out=taps["h2T"], in_=hT3),
                 waits=[mil_h2T], inc=(dOUT, 16))

        m_rmm = None
        for c in range(NCH):
            for mc in range(MC):
                m_rmm = P.op("tensor", lambda e, c=c, mc=mc: e.matmul(
                    s_ps[:, c * 8:(c + 1) * 8], hT3[:, mc, c * 128:(c + 1) * 128],
                    wg3[:, mc, :], start=(mc == 0), stop=(mc == MC - 1)),
                    waits=[tev2[c] if mc == 0 else None,
                           m_exp_bank.get(0) if c == 0 and mc == 0 else None,
                           m_wg if c == 0 and mc == 0 else None],
                    inc=(sT, 1) if mc == MC - 1 else None, tag=f"rt_mm{c}")
        m_lg = P.op("vector", lambda e: e.tensor_copy(lg_sb[:], s_ps[:, 0:NCH * E]),
                    waits=[m_rmm], inc=(sV, 1), tag="rt_ev")
        R = lambda i, c: rt_sb[:, i * NCH + c:i * NCH + c + 1]
        m_rnm = [None] * NCH
        for c in range(NCH):
            m_rnm[c] = P.op("vector", lambda e, c=c: e.reduce_max(
                R(0, c), lg3[:, c, :], axis=AX.X, negate=True),
                waits=[m_lg], inc=(sV, 1), tag=f"rt_mx{c}")
        m_rex = [None] * NCH
        for c in range(NCH):
            m_rex[c] = P.op("scalar", lambda e, c=c: e.activation(
                pr3[:, c, :], lg3[:, c, :], AF.Exp, bias=R(0, c), accum_out=R(1, c)),
                waits=[m_rnm[c]], inc=(sA, 1), tag=f"rt_exp{c}")
        m_rrc = P.op("vector", lambda e: e.reciprocal(
            rt_sb[:, 2 * NCH:3 * NCH], rt_sb[:, NCH:2 * NCH]),
            waits=[m_rex[NCH - 1]], inc=(sV, 1), tag="rt_recip")
        m_cw = None
        for c in range(NCH):
            e1 = P.op("vector", lambda e, c=c: e.tensor_scalar_mul(
                pr3[:, c, :], pr3[:, c, :], R(2, c)),
                waits=[m_rrc], inc=(sV, 1), tag=f"rt_pr{c}")
            m_m8 = P.op("vector", lambda e, c=c: e.max(out=m83[:, c, :], in_=pr3[:, c, :]),
                        waits=[e1], inc=(sV, 1), tag=f"rt_m8{c}")
            e2 = P.op("vector", lambda e, c=c: e.tensor_tensor(
                R(3, c), m83[:, c, 0:1], m83[:, c, 1:2], ALU.add),
                waits=[m_m8], inc=(sV, 1), tag=f"rt_dn{c}")
            e3 = P.op("vector", lambda e, c=c: e.tensor_scalar_add(R(3, c), R(3, c), 1e-8),
                      waits=[e2], inc=(sV, 1))
            m_rd = P.op("vector", lambda e, c=c: e.reciprocal(R(3, c), R(3, c)),
                        waits=[e3], inc=(sV, 1), tag=f"rt_rd{c}")
            e4 = P.op("vector", lambda e, c=c: e.tensor_single_scalar(
                c3[:, c, :], pr3[:, c, :], m83[:, c, 1:2], ALU.is_ge),
                waits=[m_rd], inc=(sV, 1), tag=f"rt_ind{c}")
            e5 = P.op("vector", lambda e, c=c: e.tensor_mul(
                c3[:, c, :], c3[:, c, :], pr3[:, c, :]),
                waits=[e4], inc=(sV, 1), tag=f"rt_ct{c}")
            m_cw = P.op("vector", lambda e, c=c: e.tensor_scalar_mul(
                c3[:, c, :], c3[:, c, :], R(3, c)),
                waits=[e5], inc=(sV, 1), tag=f"rt_cw{c}")
        if "c" in debug:
            P.op("sync", lambda e: e.dma_start(out=taps["c"], in_=c3),
                 waits=[m_cw], inc=(dOUT, 16))
        if "lg" in debug:
            P.op("sync", lambda e: e.dma_start(out=taps["lg"], in_=lg_sb[:]),
                 waits=[m_cw], inc=(dOUT, 16))
        if "pr" in debug:
            P.op("sync", lambda e: e.dma_start(out=taps["pr"], in_=pr_sb[:]),
                 waits=[m_cw], inc=(dOUT, 16))
        if "m8" in debug:
            P.op("sync", lambda e: e.dma_start(out=taps["m8"], in_=lg_sb[:]),
                 waits=[m_cw], inc=(dOUT, 16))

        m_w1 = [None] * (E + 1)
        m_w2 = [None] * (E + 1)
        m_w1[0] = P.op("gpsimd", lambda e: e.dma_start(
            out=w13, in_=w1_d[0].rearrange("(mc p) hh -> p mc hh", p=128)),
            waits=[m_phB_done[H - 1], m_rope_done[H - 1]], inc=(dW1, 16), tag="ld_w1_0")
        m_w2[0] = P.op("gpsimd", lambda e: e.dma_start(
            out=w23, in_=w2_d[0].rearrange("(hc p) d -> p hc d", p=128)),
            waits=[mil_h2T_tr, m_xadd], inc=(dW2, 16), tag="ld_w2_0")

        fbanks = [pp_ps[0], pp_ps[1], mi_ps]
        m_relu = [None, None, None]
        m_yev = [None, None, None]
        m_u_last = None
        m_y_last = None
        m_y_slab = None
        for ex in range(E):
            m_sq_last = None
            for ts in range(2):
                for hc in range(HC):
                    for tt in range(2):
                        pp = (hc * 2 + tt) % 3
                        t0 = ts * 1024 + tt * 512
                        mmu = None
                        for mc in range(MC):
                            mmu = P.op("tensor", lambda e, hc=hc, t0=t0, mc=mc, pp=pp: e.matmul(
                                fbanks[pp][:], w13[:, mc, hc * 128:(hc + 1) * 128],
                                hT3[:, mc, t0:t0 + 512],
                                start=(mc == 0), stop=(mc == MC - 1)),
                                waits=[m_w1[ex] if mc == 0 else None,
                                       tev2[ts * 8 + 7] if mc == 0 and ex == 0 and hc == 0 and tt == 0 else None,
                                       m_relu[pp] if mc == 0 else None,
                                       m_rmm if mc == 0 and ex == 0 and ts == 0 and hc == 0 and tt == 0 else None,
                                       m_xadd_bank[0] if mc == 0 and ex == 0 and ts == 0 and hc == 0 and tt == 0 else None,
                                       m_xadd_bank[1] if mc == 0 and ex == 0 and ts == 0 and hc == 0 and tt == 0 else None,
                                       m_phB_done[H - 1] if mc == 0 and ex == 0 and ts == 0 and hc == 0 and tt == 0 else None,
                                       m_lg if mc == 0 and ex == 0 and ts == 0 and hc <= 1 else None,
                                       tp_par[0] if mc == 0 and ex == 0 and ts == 0 and hc <= 1 else None,
                                       m_yev[pp] if mc == 0 else None],
                                inc=(sT, 1) if mc == MC - 1 else None,
                                tag=f"e{ex}u{ts}_{hc}_{tt}")
                        m_u_last = mmu
                        m_relu[pp] = P.op("scalar", lambda e, pp=pp: e.activation(
                            o_sb[pp % 2][:], fbanks[pp][:], AF.Relu),
                            waits=[mmu, m_sq_last], inc=(sA, 1),
                            tag=f"e{ex}relu{ts}_{hc}_{tt}")
                        m_sq_last = P.op("vector", lambda e, pp=pp, hc=hc, tt=tt: e.tensor_mul(
                            act3[:, hc, tt * 512:(tt + 1) * 512],
                            o_sb[pp % 2][:], o_sb[pp % 2][:]),
                            waits=[m_relu[pp],
                                   m_y_slab if hc == 0 and tt == 0 else None],
                            inc=(sV, 1), tag=f"e{ex}sq{ts}_{hc}_{tt}")
                for tc in range(8):
                    ch = ts * 8 + tc
                    for dh in range(2):
                        pp = (tc * 2 + dh) % 3
                        mmy = None
                        for hc in range(HC):
                            mmy = P.op("tensor", lambda e, hc=hc, tc=tc, dh=dh, pp=pp: e.matmul(
                                fbanks[pp][:],
                                act3[:, hc, tc * 128:(tc + 1) * 128],
                                w23[:, hc, dh * 512:(dh + 1) * 512],
                                start=(hc == 0), stop=(hc == HC - 1)),
                                waits=[m_sq_last if hc == 0 else None,
                                       m_w2[ex] if hc == 0 else None,
                                       m_relu[pp] if hc == 0 else None,
                                       m_yev[pp] if hc == 0 else None],
                                inc=(sT, 1) if hc == HC - 1 else None,
                                tag=f"e{ex}y{ts}_{tc}_{dh}")
                        m_y_last = mmy
                        m_yev[pp] = P.op("vector", lambda e, ch=ch, dh=dh, ex=ex, pp=pp: e.scalar_tensor_tensor(
                            x2[:, ch, dh * 512:(dh + 1) * 512],
                            fbanks[pp][:], c3[:, ch, ex:ex + 1],
                            x2[:, ch, dh * 512:(dh + 1) * 512],
                            ALU.mult, ALU.add),
                            waits=[mmy, m_cw if ex == 0 else None],
                            inc=(sV, 1), tag=f"e{ex}yev{ts}_{tc}_{dh}")
                m_y_slab = m_y_last
            if ex + 1 < E:
                m_w1[ex + 1] = P.op("gpsimd", lambda e, ex=ex: e.dma_start(
                    out=w13, in_=w1_d[ex + 1].rearrange("(mc p) hh -> p mc hh", p=128)),
                    waits=[m_u_last], inc=(dW1, 16), tag=f"ld_w1_{ex + 1}")
                m_w2[ex + 1] = P.op("gpsimd", lambda e, ex=ex: e.dma_start(
                    out=w23, in_=w2_d[ex + 1].rearrange("(hc p) d -> p hc d", p=128)),
                    waits=[m_y_last], inc=(dW2, 16), tag=f"ld_w2_{ex + 1}")

        P.op("sync", lambda e: e.dma_start(
            out=y_d.rearrange("(c p) f -> p c f", p=128), in_=x2),
            waits=[m_yev[0], m_yev[1]], inc=(dOUT, 16), tag="store_y")
        P.op("sync", lambda e: e.wait_ge(dOUT.h, dOUT.n), tag="final_wait")

        block = En(nc.Block())
        block.sync(lambda e: P.replay("sync", e))
        block.gpsimd(lambda e: P.replay("gpsimd", e))
        block.tensor(lambda e: P.replay("tensor", e))
        block.vector(lambda e: P.replay("vector", e))
        block.scalar(lambda e: P.replay("scalar", e))

    return nc


_CACHE = {}


def _get_program(debug=()):
    key = tuple(sorted(debug))
    if key not in _CACHE:
        _CACHE[key] = _build(debug=key)
    return _CACHE[key]


def _make_in_maps(inputs, n_cores=8):
    x = np.ascontiguousarray(np.asarray(inputs["L0_attn_x"], dtype=np.float32))
    cos2, ssin2 = _rope_tables()
    common = {
        "wq": np.ascontiguousarray(np.asarray(inputs["L0_attn_wq"], dtype=np.float32)),
        "wk": np.ascontiguousarray(np.asarray(inputs["L0_attn_wk"], dtype=np.float32)),
        "wv": np.ascontiguousarray(np.asarray(inputs["L0_attn_wv"], dtype=np.float32)),
        "wo": np.ascontiguousarray(np.asarray(inputs["L0_attn_wo"], dtype=np.float32)),
        "wg": np.ascontiguousarray(np.asarray(inputs["L0_ffn_Wg"], dtype=np.float32)),
        "w1": np.ascontiguousarray(np.asarray(inputs["L0_ffn_W1"], dtype=np.float32)),
        "w2": np.ascontiguousarray(np.asarray(inputs["L0_ffn_W2"], dtype=np.float32)),
        "cos2": cos2, "ssin2": ssin2,
        "idn": np.eye(128, dtype=ml_dtypes.bfloat16),
        "swp": _swap_matrix(),
        "tri": _trimask(),
    }
    return [dict(common, x=np.ascontiguousarray(x[b])) for b in range(n_cores)]


def kernel(**inputs):
    nc = _get_program()
    in_maps = _make_in_maps(inputs)
    res = run_bass_kernel_spmd(nc, in_maps, core_ids=list(range(8)))
    return np.stack([np.asarray(r["y"], dtype=np.float32) for r in res.results], axis=0)


# version 14
# speedup vs baseline: 1.0029x; 1.0029x over previous
import numpy as np
import ml_dtypes
from contextlib import ExitStack
import concourse.bass as bass
import concourse.mybir as mybir
from concourse.bass_utils import run_bass_kernel_spmd

F32 = mybir.dt.float32
BF16 = mybir.dt.bfloat16
AF = mybir.ActivationFunctionType
ALU = mybir.AluOpType
AX = mybir.AxisListType

T = 2048
D = 1024
H = 8
DH = 128
E = 8
HID = 2048
NCH = T // 128
MC = D // 128
HC = HID // 128
NSLAB = 4
ENGINES = ("sync", "gpsimd", "tensor", "vector", "scalar")


class Sem:
    __slots__ = ("h", "n")
    def __init__(self, h):
        self.h = h
        self.n = 0


class Plan:
    def __init__(self):
        self.prog = {e: [] for e in ENGINES}
        self.seen = {e: {} for e in ENGINES}

    def op(self, eng, fn, waits=(), inc=None, tag=""):
        ws = []
        seen = self.seen[eng]
        for w in waits:
            if w is None:
                continue
            s, v = w
            if seen.get(s, 0) >= v:
                continue
            seen[s] = v
            ws.append((s, v))
        mil = None
        if inc is not None:
            s, k = inc
            s.n += k
            mil = (s, s.n)
        self.prog[eng].append((fn, ws, inc, tag))
        return mil

    def replay(self, eng_name, eng):
        for fn, ws, inc, tag in self.prog[eng_name]:
            for s, v in ws:
                eng.wait_ge(s.h, v)
            ins = fn(eng)
            if inc is not None:
                s, k = inc
                ins.then_inc(s.h, k)


def _rope_tables():
    half = DH // 2
    pos = np.arange(T, dtype=np.float64)[None, :]
    freqs = np.power(10000.0, -np.arange(half, dtype=np.float64) / half)[:, None]
    rad = freqs * pos
    s = 1.0 / np.sqrt(float(DH))
    cos = np.cos(rad) * s
    sin = np.sin(rad) * s
    cos2 = np.concatenate([cos, cos], axis=0).astype(ml_dtypes.bfloat16)
    ssin2 = np.concatenate([-sin, sin], axis=0).astype(ml_dtypes.bfloat16)
    return np.ascontiguousarray(cos2), np.ascontiguousarray(ssin2)


def _swap_matrix():
    sw = np.zeros((128, 128), dtype=ml_dtypes.bfloat16)
    for d in range(128):
        sw[(d + 64) % 128, d] = 1.0
    return sw


def _trimask():
    m = np.zeros((128, 128), dtype=ml_dtypes.bfloat16)
    i = np.arange(128)
    m[i[:, None] < i[None, :]] = -1e30
    return m


def _build(debug=()):
    nc = bass.Bass("TRN2", target_bir_lowering=False)

    x_d = nc.dram_tensor("x", [T, D], F32, kind="ExternalInput").ap()
    wq_d = nc.dram_tensor("wq", [D, H, DH], F32, kind="ExternalInput").ap()
    wk_d = nc.dram_tensor("wk", [D, H, DH], F32, kind="ExternalInput").ap()
    wv_d = nc.dram_tensor("wv", [D, H, DH], F32, kind="ExternalInput").ap()
    wo_d = nc.dram_tensor("wo", [H, DH, D], F32, kind="ExternalInput").ap()
    wg_d = nc.dram_tensor("wg", [D, E], F32, kind="ExternalInput").ap()
    w1_d = nc.dram_tensor("w1", [E, D, HID], F32, kind="ExternalInput").ap()
    w2_d = nc.dram_tensor("w2", [E, HID, D], F32, kind="ExternalInput").ap()
    cos_d = nc.dram_tensor("cos2", [128, T], BF16, kind="ExternalInput").ap()
    sin_d = nc.dram_tensor("ssin2", [128, T], BF16, kind="ExternalInput").ap()
    idn_d = nc.dram_tensor("idn", [128, 128], BF16, kind="ExternalInput").ap()
    swp_d = nc.dram_tensor("swp", [128, 128], BF16, kind="ExternalInput").ap()
    tri_d = nc.dram_tensor("tri", [128, 128], BF16, kind="ExternalInput").ap()
    y_d = nc.dram_tensor("y", [T, D], F32, kind="ExternalOutput").ap()

    taps = {}
    def tapdecl(name, shape, dtype):
        if name in debug:
            taps[name] = nc.dram_tensor("tap_" + name, shape, dtype,
                                        kind="ExternalOutput").ap()
    tapdecl("hT", [128, MC, T], BF16)
    tapdecl("lnst", [128, 8 * NCH], F32)
    tapdecl("lg", [128, NCH * E], F32)
    tapdecl("pr", [128, NCH * E], F32)
    tapdecl("m8", [128, NCH * E], F32)
    tapdecl("v", [128, NCH, D], BF16)
    tapdecl("q0", [128, T], BF16)
    tapdecl("k0", [128, T], BF16)
    tapdecl("pT0", [128, 16, 512], BF16)
    tapdecl("p15", [128, T], BF16)
    tapdecl("smst", [128, 48], F32)
    tapdecl("xattn", [128, NCH, D], F32)
    tapdecl("h2T", [128, MC, T], BF16)
    tapdecl("c", [128, NCH, E], F32)

    with ExitStack() as ctx:
        En = ctx.enter_context

        x_sb = En(nc.sbuf_tensor("x_sb", [128, NCH * D], F32))
        hT_sb = En(nc.sbuf_tensor("hT_sb", [128, MC * T], BF16))
        v_sb = En(nc.sbuf_tensor("v_sb", [128, NCH * D], BF16))
        arenaA = En(nc.sbuf_tensor("arenaA", [128, 16384], BF16))
        arenaB = En(nc.sbuf_tensor("arenaB", [128, 16384], BF16))
        cos_sb = En(nc.sbuf_tensor("cos_sb", [128, T], BF16))
        sin_sb = En(nc.sbuf_tensor("sin_sb", [128, T], BF16))
        o_sb = [En(nc.sbuf_tensor(f"o_sb{i}", [128, 512], BF16)) for i in range(2)]
        idn_sb = En(nc.sbuf_tensor("idn_sb", [128, 128], BF16))
        swp_sb = En(nc.sbuf_tensor("swp_sb", [128, 128], BF16))
        tri_sb = En(nc.sbuf_tensor("tri_sb", [128, 128], BF16))
        diag4_sb = En(nc.sbuf_tensor("diag4_sb", [128, 256], BF16))
        sqtrash = En(nc.sbuf_tensor("sqtrash", [128, 1024], BF16))
        wg_sb = En(nc.sbuf_tensor("wg_sb", [128, MC * E], BF16))
        ln_st = En(nc.sbuf_tensor("ln_st", [128, 8 * NCH], F32))
        sm_st = En(nc.sbuf_tensor("sm_st", [128, 12], F32))
        lg_sb = En(nc.sbuf_tensor("lg_sb", [128, NCH * E], F32))
        pr_sb = En(nc.sbuf_tensor("pr_sb", [128, NCH * E], F32))
        c_sb = En(nc.sbuf_tensor("c_sb", [128, NCH * E], F32))
        rt_sb = En(nc.sbuf_tensor("rt_sb", [128, 4 * NCH], F32))

        aA = arenaA[:]
        pT3 = aA[:, 0:8192].rearrange("p (jc i) -> p jc i", jc=16)
        wv3 = aA[:, 8192:16384].rearrange("p (mc d) -> p mc d", mc=MC)
        qraw = [aA[:, 8192 + i * 512:8704 + i * 512] for i in range(2)]
        swb = [aA[:, 9216 + i * 512:9728 + i * 512] for i in range(2)]
        rt1 = aA[:, 10240:10752]
        rt2 = aA[:, 10752:11264]
        w13 = aA.rearrange("p (mc hh) -> p mc hh", mc=MC)

        aB = arenaB[:]
        q_sb = aB[:, 0:2048]
        k_sb = aB[:, 2048:4096]
        p_sb = [aB[:, 4096:6144], aB[:, 6144:8192], aA[:, 8192:10240]]
        htok = [aB[:, 8192 + i * 1024:9216 + i * 1024] for i in range(2)]
        wqh = [aB[:, 10240 + i * 1024:11264 + i * 1024] for i in range(2)]
        wkh = [aB[:, 12288 + i * 1024:13312 + i * 1024] for i in range(2)]
        woh = [aB[:, 14336 + i * 1024:15360 + i * 1024] for i in range(2)]
        w23 = aB.rearrange("p (hc d) -> p hc d", hc=HC)

        x2 = x_sb[:].rearrange("p (c f) -> p c f", c=NCH)
        hT3 = hT_sb[:].rearrange("p (mc t) -> p mc t", mc=MC)
        v3 = v_sb[:].rearrange("p (c f) -> p c f", c=NCH)
        act3 = v3
        wg3 = wg_sb[:].rearrange("p (mc e) -> p mc e", mc=MC)
        lg3 = lg_sb[:].rearrange("p (c e) -> p c e", c=NCH)
        pr3 = pr_sb[:].rearrange("p (c e) -> p c e", c=NCH)
        m83 = lg_sb[:].rearrange("p (c e) -> p c e", c=NCH)
        c3 = c_sb[:].rearrange("p (c e) -> p c e", c=NCH)

        s_ps = En(nc.psum_tensor("s_ps", [128, 2048], F32))
        mi_ps = En(nc.psum_tensor("mi_ps", [128, 512], F32))
        tp_ps = En(nc.psum_tensor("tp_ps", [128, 512], F32))
        pp_ps = [En(nc.psum_tensor(f"pp{i}", [128, 512], F32)) for i in range(2)]

        dIN = Sem(En(nc.semaphore("dIN")))
        dX = [Sem(En(nc.semaphore(f"dX{i}"))) for i in range(4)]
        dWV = Sem(En(nc.semaphore("dWV")))
        dWG = Sem(En(nc.semaphore("dWG")))
        dWH = [Sem(En(nc.semaphore(f"dWH{i}"))) for i in range(2)]
        dW1 = Sem(En(nc.semaphore("dW1")))
        dW2 = Sem(En(nc.semaphore("dW2")))
        dOUT = Sem(En(nc.semaphore("dOUT")))
        sT = Sem(En(nc.semaphore("sT")))
        sV = Sem(En(nc.semaphore("sV")))
        sA = Sem(En(nc.semaphore("sA")))
        sG = Sem(En(nc.semaphore("sG")))

        P = Plan()
        tp_par = {0: None, 1: None}

        xr = x_d.rearrange("(c p) f -> p c f", p=128)
        m_xg = []
        for g in range(4):
            m_xg.append(P.op("sync", lambda e, g=g: e.dma_start(
                out=x2[:, 4 * g:4 * (g + 1), :], in_=xr[:, 4 * g:4 * (g + 1), :]),
                inc=(dX[g], 16), tag=f"ld_x{g}"))
        m_x = m_xg[3]
        P.op("sync", lambda e: e.dma_start(out=cos_sb[:], in_=cos_d), inc=(dIN, 16))
        P.op("sync", lambda e: e.dma_start(out=sin_sb[:], in_=sin_d), inc=(dIN, 16))
        P.op("sync", lambda e: e.dma_start(out=idn_sb[:], in_=idn_d), inc=(dIN, 16))
        P.op("sync", lambda e: e.dma_start(out=swp_sb[:], in_=swp_d), inc=(dIN, 16))
        m_tbl = P.op("sync", lambda e: e.dma_start(out=tri_sb[:], in_=tri_d), inc=(dIN, 16))
        m_cos = m_sin = m_idn = m_swp = m_tri = m_tbl

        m_wv = P.op("gpsimd", lambda e: e.dma_start(
            out=wv3, in_=wv_d.rearrange("(mc p) h d -> p mc (h d)", p=128)),
            inc=(dWV, 16), tag="ld_wv")
        m_wg = P.op("gpsimd", lambda e: e.dma_start(
            out=wg3, in_=wg_d.rearrange("(mc p) e -> p mc e", p=128)),
            inc=(dWG, 16), tag="ld_wg")

        def load_head_weights(h, waits):
            b = h % 2
            P.op("gpsimd", lambda e, h=h, b=b: e.dma_start(
                out=wqh[b].rearrange("p (mc d) -> p mc d", mc=MC),
                in_=wq_d[:, h, :].rearrange("(mc p) d -> p mc d", p=128)),
                waits=waits, inc=(dWH[b], 16), tag=f"ld_wq{h}")
            P.op("gpsimd", lambda e, h=h, b=b: e.dma_start(
                out=wkh[b].rearrange("p (mc d) -> p mc d", mc=MC),
                in_=wk_d[:, h, :].rearrange("(mc p) d -> p mc d", p=128)),
                inc=(dWH[b], 16), tag=f"ld_wk{h}")
            mo = P.op("gpsimd", lambda e, h=h, b=b: e.dma_start(
                out=woh[b], in_=wo_d[h]), inc=(dWH[b], 16), tag=f"ld_wo{h}")
            return mo

        m_wh = [None] * (H + 1)
        m_wh[0] = load_head_weights(0, [])

        def slot(i, c):
            return ln_st[:, i * NCH + c:i * NCH + c + 1]

        def SS(i):
            return ln_st[:, i * NCH:(i + 1) * NCH]

        def emit_layernorm(wait_src_fn, phase):
            m_sum = m_sq = None
            for c in range(NCH):
                m_sum = P.op("vector", lambda e, c=c: e.reduce_sum(
                    slot(0, c), x2[:, c, :], axis=AX.X),
                    waits=[wait_src_fn(c)], inc=(sV, 1) if c == NCH - 1 else None,
                    tag=f"{phase}_sum{c}")
                m_sq = P.op("scalar", lambda e, c=c: e.activation(
                    sqtrash[:], x2[:, c, :], AF.Square, accum_out=slot(1, c)),
                    waits=[wait_src_fn(c), m_sq], inc=(sA, 1),
                    tag=f"{phase}_sq{c}")
            mm1 = P.op("vector", lambda e: e.tensor_scalar_mul(SS(2), SS(0), 1.0 / D),
                       waits=[m_sum, m_sq], inc=(sV, 1), tag=f"{phase}_m")
            mm2 = P.op("vector", lambda e: e.tensor_scalar_mul(SS(3), SS(2), -1.0),
                       waits=[mm1], inc=(sV, 1))
            mm3 = P.op("vector", lambda e: e.tensor_scalar_mul(SS(4), SS(1), 1.0 / D),
                       inc=(sV, 1))
            mm4 = P.op("vector", lambda e: e.tensor_tensor(SS(7), SS(2), SS(2), ALU.mult),
                       waits=[mm1], inc=(sV, 1))
            mm5 = P.op("vector", lambda e: e.tensor_sub(SS(4), SS(4), SS(7)),
                       waits=[mm3, mm4], inc=(sV, 1))
            m_veps = P.op("vector", lambda e: e.tensor_scalar_add(SS(4), SS(4), 1e-5),
                          waits=[mm5], inc=(sV, 1), tag=f"{phase}_veps")
            m_ln = P.op("scalar", lambda e: e.activation(SS(7), SS(4), AF.Ln),
                        waits=[m_veps], inc=(sA, 1), tag=f"{phase}_ln")
            m_rstd = P.op("scalar", lambda e: e.activation(SS(5), SS(7), AF.Exp, scale=-0.5),
                          waits=[m_ln], inc=(sA, 1), tag=f"{phase}_rstd")
            m_bias = P.op("vector", lambda e: e.tensor_tensor(SS(6), SS(3), SS(5), ALU.mult),
                          waits=[m_rstd], inc=(sV, 1), tag=f"{phase}_bias")
            m_tev = [None] * (2 * NCH)
            m_ttr = [None] * (2 * NCH)
            for c in range(NCH):
                b = c % 2
                m_ap = P.op("vector", lambda e, c=c, b=b: e.tensor_scalar(
                    htok[b], x2[:, c, :], slot(5, c), slot(6, c), ALU.mult, ALU.add),
                    waits=[m_bias, m_rstd, m_ttr[2 * c - 4] if c >= 2 else None],
                    inc=(sV, 1), tag=f"{phase}_apply{c}")
                for g in range(2):
                    half = g
                    mm = None
                    for q in range(4):
                        mc = g * 4 + q
                        mm = P.op("tensor", lambda e, b=b, mc=mc, q=q, half=half: e.matmul(
                            (tp_ps if half == 0 else mi_ps)[:, q * 128:(q + 1) * 128],
                            htok[b][:, mc * 128:(mc + 1) * 128], idn_sb[:],
                            start=True, stop=True),
                            waits=[m_ap if q == 0 else None,
                                   m_idn if c == 0 and g == 0 and q == 0 else None,
                                   tp_par[half] if q == 0 else None],
                            inc=(sT, 1) if q == 3 else None, tag=f"{phase}_tr{c}_{mc}")
                    m_ttr[2 * c + g] = mm
                    m_tev[2 * c + g] = P.op("vector", lambda e, c=c, g=g, half=half: e.tensor_copy(
                        hT3[:, g * 4:(g + 1) * 4, c * 128:(c + 1) * 128],
                        (tp_ps if half == 0 else mi_ps)[:].rearrange(
                            "p (q t) -> p q t", q=4)),
                        waits=[mm], inc=(sV, 1), tag=f"{phase}_tev{c}_{g}")
                    tp_par[half] = m_tev[2 * c + g]
            return m_tev[2 * NCH - 1], m_ttr[2 * NCH - 1], [m_tev[2 * c + 1] for c in range(NCH)]

        mil_h1T, mil_h1T_tr, tev1 = emit_layernorm(lambda c: m_xg[c // 4], "ln1")
        if "lnst" in debug:
            P.op("sync", lambda e: e.dma_start(out=taps["lnst"], in_=ln_st[:]),
                 waits=[mil_h1T], inc=(dOUT, 16))
        if "hT" in debug:
            P.op("sync", lambda e: e.dma_start(out=taps["hT"], in_=hT3),
                 waits=[mil_h1T], inc=(dOUT, 16))

        m_vev = [None, None]
        last_v_mm = None
        for c in range(NCH):
            for nh in range(2):
                pp = (c * 2 + nh) % 2
                mms = None
                for mc in range(MC):
                    mms = P.op("tensor", lambda e, c=c, nh=nh, mc=mc, pp=pp: e.matmul(
                        pp_ps[pp][:], hT3[:, mc, c * 128:(c + 1) * 128],
                        wv3[:, mc, nh * 512:(nh + 1) * 512],
                        start=(mc == 0), stop=(mc == MC - 1)),
                        waits=[tev1[c] if mc == 0 else None,
                               m_wv if mc == 0 and c == 0 else None,
                               m_vev[pp] if mc == 0 else None],
                        inc=(sT, 1) if mc == MC - 1 else None, tag=f"v_mm{c}_{nh}")
                last_v_mm = mms
                m_vev[pp] = P.op("scalar", lambda e, c=c, nh=nh, pp=pp: e.activation(
                    v3[:, c, nh * 512:(nh + 1) * 512], pp_ps[pp][:], AF.Copy),
                    waits=[mms], inc=(sA, 1), tag=f"v_ev{c}_{nh}")
        mil_v_all = (m_vev[0], m_vev[1])
        if "v" in debug:
            P.op("sync", lambda e: e.dma_start(out=taps["v"], in_=v3),
                 waits=[m_vev[0], m_vev[1]], inc=(dOUT, 16))

        m_phB_done = [None] * H
        m_exp_bank = {}
        m_xadd_slab = {}
        m_diag_mm = {"last": None, 0: None, 1: None, 2: None}
        m_xadd = None
        m_xadd_bank = [None, None, None]
        m_oev = [None, None]
        m_proj_mm = [None, None]
        m_rope_done = [None] * H

        def SUM(par, k=None):
            o = par * 4
            return sm_st[:, o + k:o + k + 1] if k is not None else sm_st[:, o:o + 4]
        def SSUM(par):
            return sm_st[:, 8 + par:9 + par]
        def RR(par):
            return sm_st[:, 10 + par:11 + par]

        for h in range(H):
            b = h % 2
            if h + 1 < H:
                m_wh[h + 1] = load_head_weights(
                    h + 1, [m_rope_done[h - 1] if h >= 1 else None,
                            m_proj_mm[0] if h >= 1 else None,
                            m_proj_mm[1] if h >= 1 else None])

            wq3 = wqh[b].rearrange("p (mc d) -> p mc d", mc=MC)
            wk3 = wkh[b].rearrange("p (mc d) -> p mc d", mc=MC)
            raw_slots = [pp_ps[0][:], s_ps[:, 0:512], s_ps[:, 512:1024],
                         s_ps[:, 1024:1536]]
            swp_slots = [pp_ps[1][:], tp_ps[:]]
            m_raw_ev = [None] * 4
            m_swp_ev = [None] * 2
            m_rt1 = [None] * 2
            m_rope = None

            def emit_qk_proj(idx):
                mat, tt = divmod(idx, 4)
                w3 = wq3 if mat == 0 else wk3
                rs = idx % 4
                mms = None
                for mc in range(MC):
                    mms = P.op("tensor", lambda e, w3=w3, tt=tt, mc=mc, rs=rs: e.matmul(
                        raw_slots[rs], w3[:, mc, :], hT3[:, mc, tt * 512:(tt + 1) * 512],
                        start=(mc == 0), stop=(mc == MC - 1)),
                        waits=[m_wh[h] if mc == 0 else None,
                               tev1[tt * 4 + 3] if mc == 0 and h == 0 else None,
                               m_raw_ev[rs] if mc == 0 else None,
                               m_exp_bank.get(rs - 1) if mc == 0 and rs >= 1 else None,
                               m_oev[0] if mc == 0 and idx == 0 else None,
                               m_oev[1] if mc == 0 and idx == 0 else None,
                               m_xadd if mc == 0 and idx == 0 else None,
                               m_phB_done[h - 1] if mc == 0 and idx == 0 and h >= 1 else None,
                               mil_v_all[0] if mc == 0 and idx == 0 and h == 0 else None,
                               mil_v_all[1] if mc == 0 and idx == 0 and h == 0 else None],
                        inc=(sT, 1) if mc == MC - 1 else None, tag=f"h{h}qk{idx}")
                m_raw_ev[rs] = P.op("scalar", lambda e, idx=idx, rs=rs: e.activation(
                    qraw[idx % 2], raw_slots[rs], AF.Copy),
                    waits=[mms, m_rt1[idx % 2],
                           last_v_mm if h == 0 and idx == 0 else None],
                    inc=(sA, 1), tag=f"h{h}qraw{idx}")

            def emit_qk_rope(idx):
                nonlocal m_rope
                mat, tt = divmod(idx, 4)
                dst = q_sb if mat == 0 else k_sb
                rs = idx % 4
                ss = idx % 2
                msw = P.op("tensor", lambda e, ss=ss: e.matmul(
                    swp_slots[ss], swp_sb[:], qraw[ss], start=True, stop=True),
                    waits=[m_raw_ev[rs], m_swp, m_swp_ev[ss],
                           tp_par[0] if ss == 1 and idx == 1 else None,
                           m_xadd if idx == 0 else None],
                    inc=(sT, 1), tag=f"h{h}sw{idx}")
                m_swp_ev[ss] = P.op("vector", lambda e, ss=ss: e.tensor_copy(
                    swb[ss], swp_slots[ss]),
                    waits=[msw], inc=(sV, 1), tag=f"h{h}swev{idx}")
                m_rt1[ss] = P.op("vector", lambda e, ss=ss, tt=tt: e.tensor_mul(
                    rt1, qraw[ss], cos_sb[:, tt * 512:(tt + 1) * 512]),
                    waits=[m_cos, m_sin, m_raw_ev[rs],
                           m_phB_done[h - 1] if h >= 1 and idx == 0 else None,
                           last_v_mm if h == 0 and idx == 0 else None],
                    inc=(sV, 1), tag=f"h{h}rp1_{idx}")
                P.op("vector", lambda e, ss=ss, tt=tt: e.tensor_mul(
                    rt2, swb[ss], sin_sb[:, tt * 512:(tt + 1) * 512]))
                m_rope = P.op("vector", lambda e, dst=dst, tt=tt: e.tensor_add(
                    dst[:, tt * 512:(tt + 1) * 512], rt1, rt2),
                    inc=(sV, 1), tag=f"h{h}rp3_{idx}")

            for idx in range(8):
                emit_qk_proj(idx)
                if idx >= 1:
                    emit_qk_rope(idx - 1)
            emit_qk_rope(7)
            m_rope_done[h] = m_rope
            m_sbank_free = {0: m_raw_ev[1], 1: m_raw_ev[2], 2: m_raw_ev[3]}

            if h == 0 and "q0" in debug:
                P.op("sync", lambda e: e.dma_start(out=taps["q0"], in_=q_sb),
                     waits=[m_rope_done[0]], inc=(dOUT, 16))
            if h == 0 and "k0" in debug:
                P.op("sync", lambda e: e.dma_start(out=taps["k0"], in_=k_sb),
                     waits=[m_rope_done[0]], inc=(dOUT, 16))

            m_phB_local = None
            m_nmx = {}
            m_expm = {}
            m_tail = {}
            m_sumfree = {}
            m_trp_par = {0: None, 1: None}
            m_tgrp = {}
            m_ptev = {}
            m_ptev_last = None

            def emit_scores_bank(ci, k):
                par = ci % 2
                pbuf = ci % 3
                ph = k * 512
                jlen = (ci + 1) * 128
                j0 = k * 512
                j1 = min(j0 + 512, jlen)
                w = j1 - j0
                msc = P.op("tensor", lambda e, ci=ci, j0=j0, j1=j1, ph=ph, w=w: e.matmul(
                    s_ps[:, ph:ph + w], q_sb[:, ci * 128:(ci + 1) * 128],
                    k_sb[:, j0:j1], start=True, stop=True),
                    waits=[m_rope_done[h] if ci == 0 and k == 0 else None,
                           m_exp_bank.get(k),
                           m_sbank_free.get(k)],
                    inc=(sT, 1), tag=f"h{h}s{ci}_{k}")
                last = (j1 == jlen)
                m_msk = None
                if last:
                    lo = ph + (w - 128)
                    m_msk = P.op("vector", lambda e, lo=lo: e.tensor_tensor(
                        s_ps[:, lo:lo + 128], s_ps[:, lo:lo + 128],
                        tri_sb[:], ALU.add),
                        waits=[msc, m_tri], inc=(sV, 1), tag=f"h{h}msk{ci}")
                m_expm[(ci, k)] = P.op("scalar", lambda e, k=k, j0=j0, ph=ph, w=w, par=par, pbuf=pbuf: e.activation(
                    p_sb[pbuf][:, j0:j0 + w], s_ps[:, ph:ph + w], AF.Exp,
                    accum_out=SUM(par, k)),
                    waits=[msc, m_msk, m_diag_mm[pbuf],
                           m_sumfree.get(par),
                           m_rope_done[h] if pbuf == 2 else None],
                    inc=(sA, 1), tag=f"h{h}exp{ci}_{k}")
                m_exp_bank[k] = m_expm[(ci, k)]

            def emit_tail(ci):
                par = ci % 2
                dpar = ci % 2
                K = (ci + 1 + 3) // 4
                t5 = P.op("vector", lambda e, par=par, K=K: e.reduce_sum(
                    SSUM(par), SUM(par)[:, 0:K], axis=AX.X),
                    waits=[m_expm[(ci, K - 1)]], inc=(sV, 1), tag=f"h{h}tl_s{ci}")
                m_sumfree[par] = t5
                t6 = P.op("vector", lambda e, par=par: e.reciprocal(RR(par), SSUM(par)),
                          waits=[t5], inc=(sV, 1), tag=f"h{h}tl_r{ci}")
                m_tail[ci] = P.op("vector", lambda e, dpar=dpar: e.tensor_scalar_mul(
                    diag4_sb[:, dpar * 128:(dpar + 1) * 128], idn_sb[:], RR(dpar)),
                    waits=[t6, m_trp_par[dpar], m_idn],
                    inc=(sV, 1), tag=f"h{h}tl_dg{ci}")

            def emit_transposes(ci):
                nonlocal m_ptev_last
                pbuf = ci % 3
                cl = ci % 4
                sl = ci // 4
                njc = ci + 1
                for g0 in range(0, njc, 4):
                    g1 = min(g0 + 4, njc)
                    k = g0 // 4
                    half = k % 2
                    dst_ps = tp_ps if half == 0 else mi_ps
                    mmg = None
                    for jc in range(g0, g1):
                        mmg = P.op("tensor", lambda e, pbuf=pbuf, jc=jc, g0=g0, k=k, dst_ps=dst_ps: e.matmul(
                            dst_ps[:, (jc - g0) * 128:(jc - g0 + 1) * 128],
                            p_sb[pbuf][:, jc * 128:(jc + 1) * 128],
                            diag4_sb[:, (ci % 2) * 128:(ci % 2 + 1) * 128],
                            start=True, stop=True),
                            waits=[m_tail[ci] if jc == g0 else None,
                                   tp_par[half] if jc == g0 else None,
                                   m_swp_ev[1] if jc == g0 and half == 0 else None,
                                   m_phB_local if jc == g0 and g0 == 0 and cl == 0 else None,
                                   m_oev[0] if jc == g0 and half == 1 else None,
                                   m_oev[1] if jc == g0 and half == 1 else None,
                                   m_phB_done[h - 1] if jc == g0 and g0 == 0 and ci == 0 and h >= 1 else None,
                                   last_v_mm if jc == g0 and g0 == 0 and ci == 0 and h == 0 else None],
                            inc=(sT, 1) if jc == g1 - 1 else None,
                            tag=f"h{h}pt{ci}_{jc}")
                    m_diag_mm[pbuf] = mmg
                    m_diag_mm["last"] = mmg
                    m_trp_par[ci % 2] = mmg
                    if half == 0:
                        m_ptev_last = P.op("scalar", lambda e, cl=cl, g0=g0, g1=g1: e.activation(
                            pT3[:, g0:g1, cl * 128:(cl + 1) * 128],
                            tp_ps[:, 0:(g1 - g0) * 128].rearrange(
                                "p (g i) -> p g i", g=g1 - g0), AF.Copy),
                            waits=[mmg], inc=(sA, 1), tag=f"h{h}ptev{ci}_{g0}")
                    else:
                        m_ptev_last = P.op("vector", lambda e, cl=cl, g0=g0, g1=g1: e.tensor_copy(
                            pT3[:, g0:g1, cl * 128:(cl + 1) * 128],
                            mi_ps[:, 0:(g1 - g0) * 128].rearrange(
                                "p (g i) -> p g i", g=g1 - g0)),
                            waits=[mmg], inc=(sV, 1), tag=f"h{h}ptev{ci}_{g0}")
                    tp_par[half] = m_ptev_last

            def emit_phaseB(sl):
                nonlocal m_phB_local, m_xadd
                par = sl % 2
                m_omm = None
                for jc in range(sl * 4 + 4):
                    lo = max(0, jc - sl * 4) * 128
                    m_omm = P.op("tensor", lambda e, jc=jc, lo=lo, h=h: e.matmul(
                        mi_ps[:, lo:512] if lo else mi_ps[:],
                        v3[:, jc, h * 128:(h + 1) * 128],
                        pT3[:, jc, lo:512] if lo else pT3[:, jc, :],
                        start=(jc == 0), stop=(jc == sl * 4 + 3)),
                        waits=[m_ptev_last if jc == 0 else None,
                               m_oev[1 - par] if jc == 0 else None,
                               tp_par[0] if jc == 0 else None,
                               tp_par[1] if jc == 0 else None],
                        inc=(sT, 1) if jc == sl * 4 + 3 else None, tag=f"h{h}o{sl}_{jc}")
                m_phB_local = m_omm
                m_phB_done[h] = m_omm
                m_oev[par] = P.op("vector", lambda e, par=par: e.tensor_copy(
                    o_sb[par][:], mi_ps[:]),
                    waits=[m_omm, m_proj_mm[par]], inc=(sV, 1), tag=f"h{h}oev{sl}")
                tp_par[1] = m_oev[par]
                pbanks = [pp_ps[1], pp_ps[0], tp_ps]
                for cl in range(4):
                    for mh in range(2):
                        bank = (cl * 2 + mh) % 3
                        mpj = P.op("tensor", lambda e, par=par, cl=cl, mh=mh, b=b, bank=bank, pbanks=pbanks: e.matmul(
                            pbanks[bank][:], o_sb[par][:, cl * 128:(cl + 1) * 128],
                            woh[b][:, mh * 512:(mh + 1) * 512],
                            start=True, stop=True),
                            waits=[m_oev[par], m_xadd_bank[bank],
                                   tp_par[0] if bank == 2 else None,
                                   m_raw_ev[0] if cl == 0 and mh == 1 else None,
                                   m_swp_ev[0] if cl == 0 and mh == 0 else None],
                            inc=(sT, 1), tag=f"h{h}pj{sl}_{cl}_{mh}")
                        m_proj_mm[par] = mpj
                        m_xadd = P.op("vector", lambda e, sl=sl, cl=cl, mh=mh, bank=bank, pbanks=pbanks: e.tensor_tensor(
                            x2[:, sl * 4 + cl, mh * 512:(mh + 1) * 512],
                            x2[:, sl * 4 + cl, mh * 512:(mh + 1) * 512],
                            pbanks[bank][:], ALU.add),
                            waits=[mpj], inc=(sV, 1), tag=f"h{h}xadd{sl}_{cl}_{mh}")
                        m_xadd_bank[bank] = m_xadd
                        if bank == 2:
                            tp_par[0] = m_xadd
                if h == H - 1:
                    m_xadd_slab[sl] = m_xadd

            for step in range(18):
                if 1 <= step <= 16:
                    emit_tail(step - 1)
                if step < 16:
                    ci = step
                    K = ((ci + 1) * 128 + 511) // 512
                    for k in range(K):
                        emit_scores_bank(ci, k)
                if step >= 6 and (step - 6) % 4 == 0 and step < 18:
                    emit_phaseB((step - 6) // 4)
                if step >= 2:
                    emit_transposes(step - 2)
            emit_phaseB(3)

            if h == 0 and "pT0" in debug:
                P.op("sync", lambda e: e.dma_start(out=taps["pT0"], in_=pT3),
                     waits=[m_ptev_last], inc=(dOUT, 16))
            if h == 0 and "p15" in debug:
                P.op("sync", lambda e: e.dma_start(out=taps["p15"], in_=p_sb[1]),
                     waits=[m_expm[(15, 3)], m_tail[15]], inc=(dOUT, 16))
            if h == 0 and "smst" in debug:
                P.op("sync", lambda e: e.dma_start(out=taps["smst"], in_=sm_st[:]),
                     waits=[m_tail[15]], inc=(dOUT, 16))

        if "xattn" in debug:
            P.op("sync", lambda e: e.dma_start(out=taps["xattn"], in_=x2),
                 waits=[m_xadd], inc=(dOUT, 16))

        mil_h2T, mil_h2T_tr, tev2 = emit_layernorm(lambda c: m_xadd_slab.get(c // 4, m_xadd), "ln2")
        if "h2T" in debug:
            P.op("sync", lambda e: e.dma_start(out=taps["h2T"], in_=hT3),
                 waits=[mil_h2T], inc=(dOUT, 16))

        m_rmm = None
        for c in range(NCH):
            for mc in range(MC):
                m_rmm = P.op("tensor", lambda e, c=c, mc=mc: e.matmul(
                    s_ps[:, c * 8:(c + 1) * 8], hT3[:, mc, c * 128:(c + 1) * 128],
                    wg3[:, mc, :], start=(mc == 0), stop=(mc == MC - 1)),
                    waits=[tev2[c] if mc == 0 else None,
                           m_exp_bank.get(0) if c == 0 and mc == 0 else None,
                           m_wg if c == 0 and mc == 0 else None],
                    inc=(sT, 1) if mc == MC - 1 else None, tag=f"rt_mm{c}")
        m_lg = P.op("vector", lambda e: e.tensor_copy(lg_sb[:], s_ps[:, 0:NCH * E]),
                    waits=[m_rmm], inc=(sV, 1), tag="rt_ev")
        R = lambda i, c: rt_sb[:, i * NCH + c:i * NCH + c + 1]
        m_rnm = [None] * NCH
        for c in range(NCH):
            m_rnm[c] = P.op("vector", lambda e, c=c: e.reduce_max(
                R(0, c), lg3[:, c, :], axis=AX.X, negate=True),
                waits=[m_lg], inc=(sV, 1), tag=f"rt_mx{c}")
        m_rex = [None] * NCH
        for c in range(NCH):
            m_rex[c] = P.op("scalar", lambda e, c=c: e.activation(
                pr3[:, c, :], lg3[:, c, :], AF.Exp, bias=R(0, c), accum_out=R(1, c)),
                waits=[m_rnm[c]], inc=(sA, 1), tag=f"rt_exp{c}")
        m_rrc = P.op("vector", lambda e: e.reciprocal(
            rt_sb[:, 2 * NCH:3 * NCH], rt_sb[:, NCH:2 * NCH]),
            waits=[m_rex[NCH - 1]], inc=(sV, 1), tag="rt_recip")
        m_cw = None
        for c in range(NCH):
            e1 = P.op("vector", lambda e, c=c: e.tensor_scalar_mul(
                pr3[:, c, :], pr3[:, c, :], R(2, c)),
                waits=[m_rrc], inc=(sV, 1), tag=f"rt_pr{c}")
            m_m8 = P.op("vector", lambda e, c=c: e.max(out=m83[:, c, :], in_=pr3[:, c, :]),
                        waits=[e1], inc=(sV, 1), tag=f"rt_m8{c}")
            e2 = P.op("vector", lambda e, c=c: e.tensor_tensor(
                R(3, c), m83[:, c, 0:1], m83[:, c, 1:2], ALU.add),
                waits=[m_m8], inc=(sV, 1), tag=f"rt_dn{c}")
            e3 = P.op("vector", lambda e, c=c: e.tensor_scalar_add(R(3, c), R(3, c), 1e-8),
                      waits=[e2], inc=(sV, 1))
            m_rd = P.op("vector", lambda e, c=c: e.reciprocal(R(3, c), R(3, c)),
                        waits=[e3], inc=(sV, 1), tag=f"rt_rd{c}")
            e4 = P.op("vector", lambda e, c=c: e.tensor_single_scalar(
                c3[:, c, :], pr3[:, c, :], m83[:, c, 1:2], ALU.is_ge),
                waits=[m_rd], inc=(sV, 1), tag=f"rt_ind{c}")
            e5 = P.op("vector", lambda e, c=c: e.tensor_mul(
                c3[:, c, :], c3[:, c, :], pr3[:, c, :]),
                waits=[e4], inc=(sV, 1), tag=f"rt_ct{c}")
            m_cw = P.op("vector", lambda e, c=c: e.tensor_scalar_mul(
                c3[:, c, :], c3[:, c, :], R(3, c)),
                waits=[e5], inc=(sV, 1), tag=f"rt_cw{c}")
        if "c" in debug:
            P.op("sync", lambda e: e.dma_start(out=taps["c"], in_=c3),
                 waits=[m_cw], inc=(dOUT, 16))
        if "lg" in debug:
            P.op("sync", lambda e: e.dma_start(out=taps["lg"], in_=lg_sb[:]),
                 waits=[m_cw], inc=(dOUT, 16))
        if "pr" in debug:
            P.op("sync", lambda e: e.dma_start(out=taps["pr"], in_=pr_sb[:]),
                 waits=[m_cw], inc=(dOUT, 16))
        if "m8" in debug:
            P.op("sync", lambda e: e.dma_start(out=taps["m8"], in_=lg_sb[:]),
                 waits=[m_cw], inc=(dOUT, 16))

        m_w1 = [None] * (E + 1)
        m_w2 = [None] * (E + 1)
        m_w1[0] = P.op("gpsimd", lambda e: e.dma_start(
            out=w13, in_=w1_d[0].rearrange("(mc p) hh -> p mc hh", p=128)),
            waits=[m_phB_done[H - 1], m_rope_done[H - 1]], inc=(dW1, 16), tag="ld_w1_0")
        m_w2[0] = P.op("gpsimd", lambda e: e.dma_start(
            out=w23, in_=w2_d[0].rearrange("(hc p) d -> p hc d", p=128)),
            waits=[mil_h2T_tr, m_xadd], inc=(dW2, 16), tag="ld_w2_0")

        fbanks = [pp_ps[0], pp_ps[1], mi_ps]
        m_relu = [None, None, None]
        m_yev = [None, None, None]
        m_u_last = None
        m_y_last = None
        m_y_slab = None
        for ex in range(E):
            m_sq_last = None
            for ts in range(2):
                for hc in range(HC):
                    for tt in range(2):
                        pp = (hc * 2 + tt) % 3
                        t0 = ts * 1024 + tt * 512
                        mmu = None
                        for mc in range(MC):
                            mmu = P.op("tensor", lambda e, hc=hc, t0=t0, mc=mc, pp=pp: e.matmul(
                                fbanks[pp][:], w13[:, mc, hc * 128:(hc + 1) * 128],
                                hT3[:, mc, t0:t0 + 512],
                                start=(mc == 0), stop=(mc == MC - 1)),
                                waits=[m_w1[ex] if mc == 0 else None,
                                       tev2[ts * 8 + 7] if mc == 0 and ex == 0 and hc == 0 and tt == 0 else None,
                                       m_relu[pp] if mc == 0 else None,
                                       m_rmm if mc == 0 and ex == 0 and ts == 0 and hc == 0 and tt == 0 else None,
                                       m_xadd_bank[0] if mc == 0 and ex == 0 and ts == 0 and hc == 0 and tt == 0 else None,
                                       m_xadd_bank[1] if mc == 0 and ex == 0 and ts == 0 and hc == 0 and tt == 0 else None,
                                       m_phB_done[H - 1] if mc == 0 and ex == 0 and ts == 0 and hc == 0 and tt == 0 else None,
                                       m_lg if mc == 0 and ex == 0 and ts == 0 and hc <= 1 else None,
                                       tp_par[0] if mc == 0 and ex == 0 and ts == 0 and hc <= 1 else None,
                                       m_yev[pp] if mc == 0 else None],
                                inc=(sT, 1) if mc == MC - 1 else None,
                                tag=f"e{ex}u{ts}_{hc}_{tt}")
                        m_u_last = mmu
                        m_relu[pp] = P.op("scalar", lambda e, pp=pp: e.activation(
                            o_sb[pp % 2][:], fbanks[pp][:], AF.Relu),
                            waits=[mmu, m_sq_last], inc=(sA, 1),
                            tag=f"e{ex}relu{ts}_{hc}_{tt}")
                        m_sq_last = P.op("vector", lambda e, pp=pp, hc=hc, tt=tt: e.tensor_mul(
                            act3[:, hc, tt * 512:(tt + 1) * 512],
                            o_sb[pp % 2][:], o_sb[pp % 2][:]),
                            waits=[m_relu[pp],
                                   m_y_slab if hc == 0 and tt == 0 else None],
                            inc=(sV, 1), tag=f"e{ex}sq{ts}_{hc}_{tt}")
                for tc in range(8):
                    ch = ts * 8 + tc
                    for dh in range(2):
                        pp = (tc * 2 + dh) % 3
                        mmy = None
                        for hc in range(HC):
                            mmy = P.op("tensor", lambda e, hc=hc, tc=tc, dh=dh, pp=pp: e.matmul(
                                fbanks[pp][:],
                                act3[:, hc, tc * 128:(tc + 1) * 128],
                                w23[:, hc, dh * 512:(dh + 1) * 512],
                                start=(hc == 0), stop=(hc == HC - 1)),
                                waits=[m_sq_last if hc == 0 else None,
                                       m_w2[ex] if hc == 0 else None,
                                       m_relu[pp] if hc == 0 else None,
                                       m_yev[pp] if hc == 0 else None],
                                inc=(sT, 1) if hc == HC - 1 else None,
                                tag=f"e{ex}y{ts}_{tc}_{dh}")
                        m_y_last = mmy
                        m_yev[pp] = P.op("vector", lambda e, ch=ch, dh=dh, ex=ex, pp=pp: e.scalar_tensor_tensor(
                            x2[:, ch, dh * 512:(dh + 1) * 512],
                            fbanks[pp][:], c3[:, ch, ex:ex + 1],
                            x2[:, ch, dh * 512:(dh + 1) * 512],
                            ALU.mult, ALU.add),
                            waits=[mmy, m_cw if ex == 0 else None],
                            inc=(sV, 1), tag=f"e{ex}yev{ts}_{tc}_{dh}")
                m_y_slab = m_y_last
            if ex + 1 < E:
                m_w1[ex + 1] = P.op("gpsimd", lambda e, ex=ex: e.dma_start(
                    out=w13, in_=w1_d[ex + 1].rearrange("(mc p) hh -> p mc hh", p=128)),
                    waits=[m_u_last], inc=(dW1, 16), tag=f"ld_w1_{ex + 1}")
                m_w2[ex + 1] = P.op("gpsimd", lambda e, ex=ex: e.dma_start(
                    out=w23, in_=w2_d[ex + 1].rearrange("(hc p) d -> p hc d", p=128)),
                    waits=[m_y_last], inc=(dW2, 16), tag=f"ld_w2_{ex + 1}")

        P.op("sync", lambda e: e.dma_start(
            out=y_d.rearrange("(c p) f -> p c f", p=128), in_=x2),
            waits=[m_yev[0], m_yev[1]], inc=(dOUT, 16), tag="store_y")
        P.op("sync", lambda e: e.wait_ge(dOUT.h, dOUT.n), tag="final_wait")

        block = En(nc.Block())
        block.sync(lambda e: P.replay("sync", e))
        block.gpsimd(lambda e: P.replay("gpsimd", e))
        block.tensor(lambda e: P.replay("tensor", e))
        block.vector(lambda e: P.replay("vector", e))
        block.scalar(lambda e: P.replay("scalar", e))

    return nc


_CACHE = {}


def _get_program(debug=()):
    key = tuple(sorted(debug))
    if key not in _CACHE:
        _CACHE[key] = _build(debug=key)
    return _CACHE[key]


def _make_in_maps(inputs, n_cores=8):
    x = np.ascontiguousarray(np.asarray(inputs["L0_attn_x"], dtype=np.float32))
    cos2, ssin2 = _rope_tables()
    common = {
        "wq": np.ascontiguousarray(np.asarray(inputs["L0_attn_wq"], dtype=np.float32)),
        "wk": np.ascontiguousarray(np.asarray(inputs["L0_attn_wk"], dtype=np.float32)),
        "wv": np.ascontiguousarray(np.asarray(inputs["L0_attn_wv"], dtype=np.float32)),
        "wo": np.ascontiguousarray(np.asarray(inputs["L0_attn_wo"], dtype=np.float32)),
        "wg": np.ascontiguousarray(np.asarray(inputs["L0_ffn_Wg"], dtype=np.float32)),
        "w1": np.ascontiguousarray(np.asarray(inputs["L0_ffn_W1"], dtype=np.float32)),
        "w2": np.ascontiguousarray(np.asarray(inputs["L0_ffn_W2"], dtype=np.float32)),
        "cos2": cos2, "ssin2": ssin2,
        "idn": np.eye(128, dtype=ml_dtypes.bfloat16),
        "swp": _swap_matrix(),
        "tri": _trimask(),
    }
    return [dict(common, x=np.ascontiguousarray(x[b])) for b in range(n_cores)]


def kernel(**inputs):
    nc = _get_program()
    in_maps = _make_in_maps(inputs)
    res = run_bass_kernel_spmd(nc, in_maps, core_ids=list(range(8)))
    return np.stack([np.asarray(r["y"], dtype=np.float32) for r in res.results], axis=0)


# version 15
# speedup vs baseline: 1.1969x; 1.1934x over previous
import numpy as np
import ml_dtypes
from contextlib import ExitStack
import concourse.bass as bass
import concourse.mybir as mybir
from concourse.bass_utils import run_bass_kernel_spmd

F32 = mybir.dt.float32
BF16 = mybir.dt.bfloat16
AF = mybir.ActivationFunctionType
ALU = mybir.AluOpType
AX = mybir.AxisListType

T = 2048
D = 1024
H = 8
DH = 128
E = 8
HID = 2048
NCH = T // 128
MC = D // 128
HC = HID // 128
NSLAB = 4
ENGINES = ("sync", "gpsimd", "tensor", "vector", "scalar")


class Sem:
    __slots__ = ("h", "n")
    def __init__(self, h):
        self.h = h
        self.n = 0


class Plan:
    def __init__(self):
        self.prog = {e: [] for e in ENGINES}
        self.seen = {e: {} for e in ENGINES}

    def op(self, eng, fn, waits=(), inc=None, tag=""):
        ws = []
        seen = self.seen[eng]
        for w in waits:
            if w is None:
                continue
            s, v = w
            if seen.get(s, 0) >= v:
                continue
            seen[s] = v
            ws.append((s, v))
        mil = None
        if inc is not None:
            s, k = inc
            s.n += k
            mil = (s, s.n)
        self.prog[eng].append((fn, ws, inc, tag))
        return mil

    def replay(self, eng_name, eng):
        for fn, ws, inc, tag in self.prog[eng_name]:
            for s, v in ws:
                eng.wait_ge(s.h, v)
            ins = fn(eng)
            if inc is not None:
                s, k = inc
                ins.then_inc(s.h, k)


def _rope_tables():
    half = DH // 2
    pos = np.arange(T, dtype=np.float64)[None, :]
    freqs = np.power(10000.0, -np.arange(half, dtype=np.float64) / half)[:, None]
    rad = freqs * pos
    s = 1.0 / np.sqrt(float(DH))
    cos = np.cos(rad) * s
    sin = np.sin(rad) * s
    cos2 = np.concatenate([cos, cos], axis=0).astype(ml_dtypes.bfloat16)
    ssin2 = np.concatenate([-sin, sin], axis=0).astype(ml_dtypes.bfloat16)
    return np.ascontiguousarray(cos2), np.ascontiguousarray(ssin2)


def _swap_matrix():
    sw = np.zeros((128, 128), dtype=ml_dtypes.bfloat16)
    for d in range(128):
        sw[(d + 64) % 128, d] = 1.0
    return sw


def _trimask():
    m = np.zeros((128, 128), dtype=ml_dtypes.bfloat16)
    i = np.arange(128)
    m[i[:, None] < i[None, :]] = -1e30
    return m


def _build(debug=()):
    nc = bass.Bass("TRN2", target_bir_lowering=False)

    x_d = nc.dram_tensor("x", [T, D], F32, kind="ExternalInput").ap()
    wq_d = nc.dram_tensor("wq", [D, H, DH], F32, kind="ExternalInput").ap()
    wk_d = nc.dram_tensor("wk", [D, H, DH], F32, kind="ExternalInput").ap()
    wv_d = nc.dram_tensor("wv", [D, H, DH], F32, kind="ExternalInput").ap()
    wo_d = nc.dram_tensor("wo", [H, DH, D], F32, kind="ExternalInput").ap()
    wg_d = nc.dram_tensor("wg", [D, E], F32, kind="ExternalInput").ap()
    w1_d = nc.dram_tensor("w1", [E, D, HID], F32, kind="ExternalInput").ap()
    w2_d = nc.dram_tensor("w2", [E, HID, D], F32, kind="ExternalInput").ap()
    cos_d = nc.dram_tensor("cos2", [128, T], BF16, kind="ExternalInput").ap()
    sin_d = nc.dram_tensor("ssin2", [128, T], BF16, kind="ExternalInput").ap()
    idn_d = nc.dram_tensor("idn", [128, 128], BF16, kind="ExternalInput").ap()
    swp_d = nc.dram_tensor("swp", [128, 128], BF16, kind="ExternalInput").ap()
    tri_d = nc.dram_tensor("tri", [128, 128], BF16, kind="ExternalInput").ap()
    y_d = nc.dram_tensor("y", [T, D], F32, kind="ExternalOutput").ap()

    taps = {}
    def tapdecl(name, shape, dtype):
        if name in debug:
            taps[name] = nc.dram_tensor("tap_" + name, shape, dtype,
                                        kind="ExternalOutput").ap()
    tapdecl("hT", [128, MC, T], BF16)
    tapdecl("lnst", [128, 8 * NCH], F32)
    tapdecl("lg", [128, NCH * E], F32)
    tapdecl("pr", [128, NCH * E], F32)
    tapdecl("m8", [128, NCH * E], F32)
    tapdecl("v", [128, NCH, D], BF16)
    tapdecl("q0", [128, T], BF16)
    tapdecl("k0", [128, T], BF16)
    tapdecl("pT0", [128, 16, 512], BF16)
    tapdecl("p15", [128, T], BF16)
    tapdecl("smst", [128, 48], F32)
    tapdecl("xattn", [128, NCH, D], F32)
    tapdecl("h2T", [128, MC, T], BF16)
    tapdecl("c", [128, NCH, E], F32)

    with ExitStack() as ctx:
        En = ctx.enter_context

        x_sb = En(nc.sbuf_tensor("x_sb", [128, NCH * D], F32))
        hT_sb = En(nc.sbuf_tensor("hT_sb", [128, MC * T], BF16))
        v_sb = En(nc.sbuf_tensor("v_sb", [128, NCH * D], BF16))
        arenaA = En(nc.sbuf_tensor("arenaA", [128, 16384], BF16))
        arenaB = En(nc.sbuf_tensor("arenaB", [128, 16384], BF16))
        cos_sb = En(nc.sbuf_tensor("cos_sb", [128, T], BF16))
        sin_sb = En(nc.sbuf_tensor("sin_sb", [128, T], BF16))
        o_sb = [En(nc.sbuf_tensor(f"o_sb{i}", [128, 512], BF16)) for i in range(2)]
        idn_sb = En(nc.sbuf_tensor("idn_sb", [128, 128], BF16))
        swp_sb = En(nc.sbuf_tensor("swp_sb", [128, 128], BF16))
        tri_sb = En(nc.sbuf_tensor("tri_sb", [128, 128], BF16))
        diag4_sb = En(nc.sbuf_tensor("diag4_sb", [128, 256], BF16))
        sqtrash = En(nc.sbuf_tensor("sqtrash", [128, 1024], BF16))
        wg_sb = En(nc.sbuf_tensor("wg_sb", [128, MC * E], BF16))
        ln_st = En(nc.sbuf_tensor("ln_st", [128, 8 * NCH], F32))
        sm_st = En(nc.sbuf_tensor("sm_st", [128, 12], F32))
        lg_sb = En(nc.sbuf_tensor("lg_sb", [128, NCH * E], F32))
        pr_sb = En(nc.sbuf_tensor("pr_sb", [128, NCH * E], F32))
        c_sb = En(nc.sbuf_tensor("c_sb", [128, NCH * E], F32))
        rt_sb = En(nc.sbuf_tensor("rt_sb", [128, 4 * NCH], F32))

        aA = arenaA[:]
        pT3 = aA[:, 0:8192].rearrange("p (jc i) -> p jc i", jc=16)
        wv3 = aA[:, 8192:16384].rearrange("p (mc d) -> p mc d", mc=MC)
        qraw = [aA[:, 8192 + i * 512:8704 + i * 512] for i in range(2)]
        swb = [aA[:, 9216 + i * 512:9728 + i * 512] for i in range(2)]
        rt1 = aA[:, 10240:10752]
        rt2 = aA[:, 10752:11264]
        w13 = aA.rearrange("p (mc hh) -> p mc hh", mc=MC)

        aB = arenaB[:]
        q_sb = aB[:, 0:2048]
        k_sb = aB[:, 2048:4096]
        p_sb = [aB[:, 4096:6144], aB[:, 6144:8192], aA[:, 8192:10240]]
        htok = [aB[:, 8192 + i * 1024:9216 + i * 1024] for i in range(2)]
        wqh = [aB[:, 10240 + i * 1024:11264 + i * 1024] for i in range(2)]
        wkh = [aB[:, 12288 + i * 1024:13312 + i * 1024] for i in range(2)]
        woh = [aB[:, 14336 + i * 1024:15360 + i * 1024] for i in range(2)]
        w23 = aB.rearrange("p (hc d) -> p hc d", hc=HC)

        x2 = x_sb[:].rearrange("p (c f) -> p c f", c=NCH)
        hT3 = hT_sb[:].rearrange("p (mc t) -> p mc t", mc=MC)
        v3 = v_sb[:].rearrange("p (c f) -> p c f", c=NCH)
        act3 = v3
        wg3 = wg_sb[:].rearrange("p (mc e) -> p mc e", mc=MC)
        lg3 = lg_sb[:].rearrange("p (c e) -> p c e", c=NCH)
        pr3 = pr_sb[:].rearrange("p (c e) -> p c e", c=NCH)
        m83 = lg_sb[:].rearrange("p (c e) -> p c e", c=NCH)
        c3 = c_sb[:].rearrange("p (c e) -> p c e", c=NCH)

        s_ps = En(nc.psum_tensor("s_ps", [128, 2048], F32))
        mi_ps = En(nc.psum_tensor("mi_ps", [128, 512], F32))
        tp_ps = En(nc.psum_tensor("tp_ps", [128, 512], F32))
        pp_ps = [En(nc.psum_tensor(f"pp{i}", [128, 512], F32)) for i in range(2)]

        dIN = Sem(En(nc.semaphore("dIN")))
        dX = [Sem(En(nc.semaphore(f"dX{i}"))) for i in range(4)]
        dWV = Sem(En(nc.semaphore("dWV")))
        dWG = Sem(En(nc.semaphore("dWG")))
        dWH = [Sem(En(nc.semaphore(f"dWH{i}"))) for i in range(2)]
        dW1 = Sem(En(nc.semaphore("dW1")))
        dW2 = Sem(En(nc.semaphore("dW2")))
        dOUT = Sem(En(nc.semaphore("dOUT")))
        sT = Sem(En(nc.semaphore("sT")))
        sV = Sem(En(nc.semaphore("sV")))
        sA = Sem(En(nc.semaphore("sA")))
        sG = Sem(En(nc.semaphore("sG")))

        P = Plan()
        tp_par = {0: None, 1: None}

        xr = x_d.rearrange("(c p) f -> p c f", p=128)
        m_xg = []
        for g in range(4):
            m_xg.append(P.op("sync", lambda e, g=g: e.dma_start(
                out=x2[:, 4 * g:4 * (g + 1), :], in_=xr[:, 4 * g:4 * (g + 1), :]),
                inc=(dX[g], 16), tag=f"ld_x{g}"))
        m_x = m_xg[3]
        P.op("sync", lambda e: e.dma_start(out=cos_sb[:], in_=cos_d), inc=(dIN, 16))
        P.op("sync", lambda e: e.dma_start(out=sin_sb[:], in_=sin_d), inc=(dIN, 16))
        P.op("sync", lambda e: e.dma_start(out=idn_sb[:], in_=idn_d), inc=(dIN, 16))
        P.op("sync", lambda e: e.dma_start(out=swp_sb[:], in_=swp_d), inc=(dIN, 16))
        m_tbl = P.op("sync", lambda e: e.dma_start(out=tri_sb[:], in_=tri_d), inc=(dIN, 16))
        m_cos = m_sin = m_idn = m_swp = m_tri = m_tbl

        m_wv = P.op("gpsimd", lambda e: e.dma_start(
            out=wv3, in_=wv_d.rearrange("(mc p) h d -> p mc (h d)", p=128)),
            inc=(dWV, 16), tag="ld_wv")
        m_wg = P.op("gpsimd", lambda e: e.dma_start(
            out=wg3, in_=wg_d.rearrange("(mc p) e -> p mc e", p=128)),
            inc=(dWG, 16), tag="ld_wg")

        def load_head_weights(h, waits):
            b = h % 2
            P.op("gpsimd", lambda e, h=h, b=b: e.dma_start(
                out=wqh[b].rearrange("p (mc d) -> p mc d", mc=MC),
                in_=wq_d[:, h, :].rearrange("(mc p) d -> p mc d", p=128)),
                waits=waits, inc=(dWH[b], 16), tag=f"ld_wq{h}")
            P.op("gpsimd", lambda e, h=h, b=b: e.dma_start(
                out=wkh[b].rearrange("p (mc d) -> p mc d", mc=MC),
                in_=wk_d[:, h, :].rearrange("(mc p) d -> p mc d", p=128)),
                inc=(dWH[b], 16), tag=f"ld_wk{h}")
            mo = P.op("gpsimd", lambda e, h=h, b=b: e.dma_start(
                out=woh[b], in_=wo_d[h]), inc=(dWH[b], 16), tag=f"ld_wo{h}")
            return mo

        m_wh = [None] * (H + 1)
        m_wh[0] = load_head_weights(0, [])

        def slot(i, c):
            return ln_st[:, i * NCH + c:i * NCH + c + 1]

        def SS(i):
            return ln_st[:, i * NCH:(i + 1) * NCH]

        def emit_layernorm(wait_src_fn, phase):
            m_sum = m_sq = None
            for c in range(NCH):
                m_sum = P.op("vector", lambda e, c=c: e.reduce_sum(
                    slot(0, c), x2[:, c, :], axis=AX.X),
                    waits=[wait_src_fn(c)], inc=(sV, 1) if c == NCH - 1 else None,
                    tag=f"{phase}_sum{c}")
                m_sq = P.op("scalar", lambda e, c=c: e.activation(
                    sqtrash[:], x2[:, c, :], AF.Square, accum_out=slot(1, c)),
                    waits=[wait_src_fn(c), m_sq], inc=(sA, 1),
                    tag=f"{phase}_sq{c}")
            mm1 = P.op("vector", lambda e: e.tensor_scalar_mul(SS(2), SS(0), 1.0 / D),
                       waits=[m_sum, m_sq], inc=(sV, 1), tag=f"{phase}_m")
            mm2 = P.op("vector", lambda e: e.tensor_scalar_mul(SS(3), SS(2), -1.0),
                       waits=[mm1], inc=(sV, 1))
            mm3 = P.op("vector", lambda e: e.tensor_scalar_mul(SS(4), SS(1), 1.0 / D),
                       inc=(sV, 1))
            mm4 = P.op("vector", lambda e: e.tensor_tensor(SS(7), SS(2), SS(2), ALU.mult),
                       waits=[mm1], inc=(sV, 1))
            mm5 = P.op("vector", lambda e: e.tensor_sub(SS(4), SS(4), SS(7)),
                       waits=[mm3, mm4], inc=(sV, 1))
            m_veps = P.op("vector", lambda e: e.tensor_scalar_add(SS(4), SS(4), 1e-5),
                          waits=[mm5], inc=(sV, 1), tag=f"{phase}_veps")
            m_ln = P.op("scalar", lambda e: e.activation(SS(7), SS(4), AF.Ln),
                        waits=[m_veps], inc=(sA, 1), tag=f"{phase}_ln")
            m_rstd = P.op("scalar", lambda e: e.activation(SS(5), SS(7), AF.Exp, scale=-0.5),
                          waits=[m_ln], inc=(sA, 1), tag=f"{phase}_rstd")
            m_bias = P.op("vector", lambda e: e.tensor_tensor(SS(6), SS(3), SS(5), ALU.mult),
                          waits=[m_rstd], inc=(sV, 1), tag=f"{phase}_bias")
            m_tev = [None] * (2 * NCH)
            m_ttr = [None] * (2 * NCH)
            for c in range(NCH):
                b = c % 2
                m_ap = P.op("vector", lambda e, c=c, b=b: e.tensor_scalar(
                    htok[b], x2[:, c, :], slot(5, c), slot(6, c), ALU.mult, ALU.add),
                    waits=[m_bias, m_rstd, m_ttr[2 * c - 4] if c >= 2 else None],
                    inc=(sV, 1), tag=f"{phase}_apply{c}")
                for g in range(2):
                    half = g
                    mm = None
                    for q in range(4):
                        mc = g * 4 + q
                        mm = P.op("tensor", lambda e, b=b, mc=mc, q=q, half=half: e.matmul(
                            (tp_ps if half == 0 else mi_ps)[:, q * 128:(q + 1) * 128],
                            htok[b][:, mc * 128:(mc + 1) * 128], idn_sb[:],
                            start=True, stop=True),
                            waits=[m_ap if q == 0 else None,
                                   m_idn if c == 0 and g == 0 and q == 0 else None,
                                   tp_par[half] if q == 0 else None],
                            inc=(sT, 1) if q == 3 else None, tag=f"{phase}_tr{c}_{mc}")
                    m_ttr[2 * c + g] = mm
                    m_tev[2 * c + g] = P.op("vector", lambda e, c=c, g=g, half=half: e.tensor_copy(
                        hT3[:, g * 4:(g + 1) * 4, c * 128:(c + 1) * 128],
                        (tp_ps if half == 0 else mi_ps)[:].rearrange(
                            "p (q t) -> p q t", q=4)),
                        waits=[mm], inc=(sV, 1), tag=f"{phase}_tev{c}_{g}")
                    tp_par[half] = m_tev[2 * c + g]
            return m_tev[2 * NCH - 1], m_ttr[2 * NCH - 1], [m_tev[2 * c + 1] for c in range(NCH)]

        mil_h1T, mil_h1T_tr, tev1 = emit_layernorm(lambda c: m_xg[c // 4], "ln1")
        if "lnst" in debug:
            P.op("sync", lambda e: e.dma_start(out=taps["lnst"], in_=ln_st[:]),
                 waits=[mil_h1T], inc=(dOUT, 16))
        if "hT" in debug:
            P.op("sync", lambda e: e.dma_start(out=taps["hT"], in_=hT3),
                 waits=[mil_h1T], inc=(dOUT, 16))

        m_vev = [None, None]
        last_v_mm = None
        for c in range(NCH):
            for nh in range(2):
                pp = (c * 2 + nh) % 2
                mms = None
                for mc in range(MC):
                    mms = P.op("tensor", lambda e, c=c, nh=nh, mc=mc, pp=pp: e.matmul(
                        pp_ps[pp][:], hT3[:, mc, c * 128:(c + 1) * 128],
                        wv3[:, mc, nh * 512:(nh + 1) * 512],
                        start=(mc == 0), stop=(mc == MC - 1)),
                        waits=[tev1[c] if mc == 0 else None,
                               m_wv if mc == 0 and c == 0 else None,
                               m_vev[pp] if mc == 0 else None],
                        inc=(sT, 1) if mc == MC - 1 else None, tag=f"v_mm{c}_{nh}")
                last_v_mm = mms
                m_vev[pp] = P.op("scalar", lambda e, c=c, nh=nh, pp=pp: e.activation(
                    v3[:, c, nh * 512:(nh + 1) * 512], pp_ps[pp][:], AF.Copy),
                    waits=[mms], inc=(sA, 1), tag=f"v_ev{c}_{nh}")
        mil_v_all = (m_vev[0], m_vev[1])
        if "v" in debug:
            P.op("sync", lambda e: e.dma_start(out=taps["v"], in_=v3),
                 waits=[m_vev[0], m_vev[1]], inc=(dOUT, 16))

        m_phB_done = [None] * H
        m_exp_bank = {}
        m_xadd_slab = {}
        m_diag_mm = {"last": None, 0: None, 1: None, 2: None}
        m_xadd = None
        m_xadd_bank = [None, None, None]
        m_oev = [None, None]
        m_proj_mm = [None, None]
        m_rope_done = [None] * H

        def SUM(par, k=None):
            o = par * 4
            return sm_st[:, o + k:o + k + 1] if k is not None else sm_st[:, o:o + 4]
        def SSUM(par):
            return sm_st[:, 8 + par:9 + par]
        def RR(par):
            return sm_st[:, 10 + par:11 + par]

        for h in range(H):
            b = h % 2
            if h + 1 < H:
                m_wh[h + 1] = load_head_weights(
                    h + 1, [m_rope_done[h - 1] if h >= 1 else None,
                            m_proj_mm[0] if h >= 1 else None,
                            m_proj_mm[1] if h >= 1 else None])

            wq3 = wqh[b].rearrange("p (mc d) -> p mc d", mc=MC)
            wk3 = wkh[b].rearrange("p (mc d) -> p mc d", mc=MC)
            raw_slots = [pp_ps[0][:], s_ps[:, 0:512], s_ps[:, 512:1024],
                         s_ps[:, 1024:1536]]
            swp_slots = [pp_ps[1][:], tp_ps[:]]
            m_raw_ev = [None] * 4
            m_swp_ev = [None] * 2
            m_rt1 = [None] * 2
            m_rope = None

            def emit_qk_proj(idx):
                mat, tt = divmod(idx, 4)
                w3 = wq3 if mat == 0 else wk3
                rs = idx % 4
                mms = None
                for mc in range(MC):
                    mms = P.op("tensor", lambda e, w3=w3, tt=tt, mc=mc, rs=rs: e.matmul(
                        raw_slots[rs], w3[:, mc, :], hT3[:, mc, tt * 512:(tt + 1) * 512],
                        start=(mc == 0), stop=(mc == MC - 1)),
                        waits=[m_wh[h] if mc == 0 else None,
                               tev1[tt * 4 + 3] if mc == 0 and h == 0 else None,
                               m_raw_ev[rs] if mc == 0 else None,
                               m_exp_bank.get(rs - 1) if mc == 0 and rs >= 1 else None,
                               m_oev[0] if mc == 0 and idx == 0 else None,
                               m_oev[1] if mc == 0 and idx == 0 else None,
                               m_xadd if mc == 0 and idx == 0 else None,
                               m_phB_done[h - 1] if mc == 0 and idx == 0 and h >= 1 else None,
                               mil_v_all[0] if mc == 0 and idx == 0 and h == 0 else None,
                               mil_v_all[1] if mc == 0 and idx == 0 and h == 0 else None],
                        inc=(sT, 1) if mc == MC - 1 else None, tag=f"h{h}qk{idx}")
                m_raw_ev[rs] = P.op("scalar", lambda e, idx=idx, rs=rs: e.activation(
                    qraw[idx % 2], raw_slots[rs], AF.Copy),
                    waits=[mms, m_rt1[idx % 2],
                           last_v_mm if h == 0 and idx == 0 else None],
                    inc=(sA, 1), tag=f"h{h}qraw{idx}")

            def emit_qk_rope(idx):
                nonlocal m_rope
                mat, tt = divmod(idx, 4)
                dst = q_sb if mat == 0 else k_sb
                rs = idx % 4
                ss = idx % 2
                msw = P.op("tensor", lambda e, ss=ss: e.matmul(
                    swp_slots[ss], swp_sb[:], qraw[ss], start=True, stop=True),
                    waits=[m_raw_ev[rs], m_swp, m_swp_ev[ss],
                           tp_par[0] if ss == 1 and idx == 1 else None,
                           m_xadd if idx == 0 else None],
                    inc=(sT, 1), tag=f"h{h}sw{idx}")
                m_swp_ev[ss] = P.op("vector", lambda e, ss=ss: e.tensor_copy(
                    swb[ss], swp_slots[ss]),
                    waits=[msw], inc=(sV, 1), tag=f"h{h}swev{idx}")
                m_rt1[ss] = P.op("vector", lambda e, ss=ss, tt=tt: e.tensor_mul(
                    rt1, qraw[ss], cos_sb[:, tt * 512:(tt + 1) * 512]),
                    waits=[m_cos, m_sin, m_raw_ev[rs],
                           m_phB_done[h - 1] if h >= 1 and idx == 0 else None,
                           last_v_mm if h == 0 and idx == 0 else None],
                    inc=(sV, 1), tag=f"h{h}rp1_{idx}")
                P.op("vector", lambda e, ss=ss, tt=tt: e.tensor_mul(
                    rt2, swb[ss], sin_sb[:, tt * 512:(tt + 1) * 512]))
                m_rope = P.op("vector", lambda e, dst=dst, tt=tt: e.tensor_add(
                    dst[:, tt * 512:(tt + 1) * 512], rt1, rt2),
                    inc=(sV, 1), tag=f"h{h}rp3_{idx}")

            for idx in range(8):
                emit_qk_proj(idx)
                if idx >= 1:
                    emit_qk_rope(idx - 1)
            emit_qk_rope(7)
            m_rope_done[h] = m_rope
            m_sbank_free = {0: m_raw_ev[1], 1: m_raw_ev[2], 2: m_raw_ev[3]}

            if h == 0 and "q0" in debug:
                P.op("sync", lambda e: e.dma_start(out=taps["q0"], in_=q_sb),
                     waits=[m_rope_done[0]], inc=(dOUT, 16))
            if h == 0 and "k0" in debug:
                P.op("sync", lambda e: e.dma_start(out=taps["k0"], in_=k_sb),
                     waits=[m_rope_done[0]], inc=(dOUT, 16))

            m_phB_local = None
            m_nmx = {}
            m_expm = {}
            m_tail = {}
            m_sumfree = {}
            m_trp_par = {0: None, 1: None}
            m_tgrp = {}
            m_ptev = {}
            m_ptev_last = None

            def emit_scores_bank(ci, k):
                par = ci % 2
                pbuf = ci % 3
                ph = k * 512
                jlen = (ci + 1) * 128
                j0 = k * 512
                j1 = min(j0 + 512, jlen)
                w = j1 - j0
                msc = P.op("tensor", lambda e, ci=ci, j0=j0, j1=j1, ph=ph, w=w: e.matmul(
                    s_ps[:, ph:ph + w], q_sb[:, ci * 128:(ci + 1) * 128],
                    k_sb[:, j0:j1], start=True, stop=True),
                    waits=[m_rope_done[h] if ci == 0 and k == 0 else None,
                           m_exp_bank.get(k),
                           m_sbank_free.get(k)],
                    inc=(sT, 1), tag=f"h{h}s{ci}_{k}")
                last = (j1 == jlen)
                m_msk = None
                if last:
                    lo = ph + (w - 128)
                    m_msk = P.op("vector", lambda e, lo=lo: e.tensor_tensor(
                        s_ps[:, lo:lo + 128], s_ps[:, lo:lo + 128],
                        tri_sb[:], ALU.add),
                        waits=[msc, m_tri], inc=(sV, 1), tag=f"h{h}msk{ci}")
                m_expm[(ci, k)] = P.op("scalar", lambda e, k=k, j0=j0, ph=ph, w=w, par=par, pbuf=pbuf: e.activation(
                    p_sb[pbuf][:, j0:j0 + w], s_ps[:, ph:ph + w], AF.Exp,
                    accum_out=SUM(par, k)),
                    waits=[msc, m_msk, m_diag_mm[pbuf],
                           m_sumfree.get(par),
                           m_rope_done[h] if pbuf == 2 else None],
                    inc=(sA, 1), tag=f"h{h}exp{ci}_{k}")
                m_exp_bank[k] = m_expm[(ci, k)]

            def emit_tail(ci):
                par = ci % 2
                dpar = ci % 2
                K = (ci + 1 + 3) // 4
                t5 = P.op("vector", lambda e, par=par, K=K: e.reduce_sum(
                    SSUM(par), SUM(par)[:, 0:K], axis=AX.X),
                    waits=[m_expm[(ci, K - 1)]], inc=(sV, 1), tag=f"h{h}tl_s{ci}")
                m_sumfree[par] = t5
                t6 = P.op("vector", lambda e, par=par: e.reciprocal(RR(par), SSUM(par)),
                          waits=[t5], inc=(sV, 1), tag=f"h{h}tl_r{ci}")
                m_tail[ci] = P.op("vector", lambda e, dpar=dpar: e.tensor_scalar_mul(
                    diag4_sb[:, dpar * 128:(dpar + 1) * 128], idn_sb[:], RR(dpar)),
                    waits=[t6, m_trp_par[dpar], m_idn],
                    inc=(sV, 1), tag=f"h{h}tl_dg{ci}")

            def emit_transposes(ci):
                nonlocal m_ptev_last
                pbuf = ci % 3
                cl = ci % 4
                sl = ci // 4
                njc = ci + 1
                for g0 in range(0, njc, 4):
                    g1 = min(g0 + 4, njc)
                    k = g0 // 4
                    half = k % 2
                    dst_ps = tp_ps if half == 0 else mi_ps
                    mmg = None
                    for jc in range(g0, g1):
                        mmg = P.op("tensor", lambda e, pbuf=pbuf, jc=jc, g0=g0, k=k, dst_ps=dst_ps: e.matmul(
                            dst_ps[:, (jc - g0) * 128:(jc - g0 + 1) * 128],
                            p_sb[pbuf][:, jc * 128:(jc + 1) * 128],
                            diag4_sb[:, (ci % 2) * 128:(ci % 2 + 1) * 128],
                            start=True, stop=True),
                            waits=[m_tail[ci] if jc == g0 else None,
                                   tp_par[half] if jc == g0 else None,
                                   m_swp_ev[1] if jc == g0 and half == 0 else None,
                                   m_phB_local if jc == g0 and g0 == 0 and cl == 0 else None,
                                   m_oev[0] if jc == g0 and half == 1 else None,
                                   m_oev[1] if jc == g0 and half == 1 else None,
                                   m_phB_done[h - 1] if jc == g0 and g0 == 0 and ci == 0 and h >= 1 else None,
                                   last_v_mm if jc == g0 and g0 == 0 and ci == 0 and h == 0 else None],
                            inc=(sT, 1) if jc == g1 - 1 else None,
                            tag=f"h{h}pt{ci}_{jc}")
                    m_diag_mm[pbuf] = mmg
                    m_diag_mm["last"] = mmg
                    m_trp_par[ci % 2] = mmg
                    if half == 0:
                        m_ptev_last = P.op("scalar", lambda e, cl=cl, g0=g0, g1=g1: e.activation(
                            pT3[:, g0:g1, cl * 128:(cl + 1) * 128],
                            tp_ps[:, 0:(g1 - g0) * 128].rearrange(
                                "p (g i) -> p g i", g=g1 - g0), AF.Copy),
                            waits=[mmg], inc=(sA, 1), tag=f"h{h}ptev{ci}_{g0}")
                    else:
                        m_ptev_last = P.op("vector", lambda e, cl=cl, g0=g0, g1=g1: e.tensor_copy(
                            pT3[:, g0:g1, cl * 128:(cl + 1) * 128],
                            mi_ps[:, 0:(g1 - g0) * 128].rearrange(
                                "p (g i) -> p g i", g=g1 - g0)),
                            waits=[mmg], inc=(sV, 1), tag=f"h{h}ptev{ci}_{g0}")
                    tp_par[half] = m_ptev_last

            def emit_phaseB(sl):
                nonlocal m_phB_local, m_xadd
                par = sl % 2
                m_omm = None
                for jc in range(sl * 4 + 4):
                    lo = max(0, jc - sl * 4) * 128
                    m_omm = P.op("tensor", lambda e, jc=jc, lo=lo, h=h: e.matmul(
                        mi_ps[:, lo:512] if lo else mi_ps[:],
                        v3[:, jc, h * 128:(h + 1) * 128],
                        pT3[:, jc, lo:512] if lo else pT3[:, jc, :],
                        start=(jc == 0), stop=(jc == sl * 4 + 3)),
                        waits=[m_ptev_last if jc == 0 else None,
                               m_oev[1 - par] if jc == 0 else None,
                               tp_par[0] if jc == 0 else None,
                               tp_par[1] if jc == 0 else None],
                        inc=(sT, 1) if jc == sl * 4 + 3 else None, tag=f"h{h}o{sl}_{jc}")
                m_phB_local = m_omm
                m_phB_done[h] = m_omm
                m_oev[par] = P.op("vector", lambda e, par=par: e.tensor_copy(
                    o_sb[par][:], mi_ps[:]),
                    waits=[m_omm, m_proj_mm[par]], inc=(sV, 1), tag=f"h{h}oev{sl}")
                tp_par[1] = m_oev[par]
                pbanks = [pp_ps[1], pp_ps[0], tp_ps]
                for cl in range(4):
                    for mh in range(2):
                        bank = (cl * 2 + mh) % 3
                        mpj = P.op("tensor", lambda e, par=par, cl=cl, mh=mh, b=b, bank=bank, pbanks=pbanks: e.matmul(
                            pbanks[bank][:], o_sb[par][:, cl * 128:(cl + 1) * 128],
                            woh[b][:, mh * 512:(mh + 1) * 512],
                            start=True, stop=True),
                            waits=[m_oev[par], m_xadd_bank[bank],
                                   tp_par[0] if bank == 2 else None,
                                   m_raw_ev[0] if cl == 0 and mh == 1 else None,
                                   m_swp_ev[0] if cl == 0 and mh == 0 else None],
                            inc=(sT, 1), tag=f"h{h}pj{sl}_{cl}_{mh}")
                        m_proj_mm[par] = mpj
                        m_xadd = P.op("vector", lambda e, sl=sl, cl=cl, mh=mh, bank=bank, pbanks=pbanks: e.tensor_tensor(
                            x2[:, sl * 4 + cl, mh * 512:(mh + 1) * 512],
                            x2[:, sl * 4 + cl, mh * 512:(mh + 1) * 512],
                            pbanks[bank][:], ALU.add),
                            waits=[mpj], inc=(sV, 1), tag=f"h{h}xadd{sl}_{cl}_{mh}")
                        m_xadd_bank[bank] = m_xadd
                        if bank == 2:
                            tp_par[0] = m_xadd
                if h == H - 1:
                    m_xadd_slab[sl] = m_xadd

            for step in range(18):
                if step < 16:
                    ci = step
                    K = ((ci + 1) * 128 + 511) // 512
                    for k in range(K):
                        emit_scores_bank(ci, k)
                if 1 <= step <= 16:
                    emit_tail(step - 1)
                if step >= 6 and (step - 6) % 4 == 0 and step < 18:
                    emit_phaseB((step - 6) // 4)
                if step >= 2:
                    emit_transposes(step - 2)
            emit_phaseB(3)

            if h == 0 and "pT0" in debug:
                P.op("sync", lambda e: e.dma_start(out=taps["pT0"], in_=pT3),
                     waits=[m_ptev_last], inc=(dOUT, 16))
            if h == 0 and "p15" in debug:
                P.op("sync", lambda e: e.dma_start(out=taps["p15"], in_=p_sb[1]),
                     waits=[m_expm[(15, 3)], m_tail[15]], inc=(dOUT, 16))
            if h == 0 and "smst" in debug:
                P.op("sync", lambda e: e.dma_start(out=taps["smst"], in_=sm_st[:]),
                     waits=[m_tail[15]], inc=(dOUT, 16))

        if "xattn" in debug:
            P.op("sync", lambda e: e.dma_start(out=taps["xattn"], in_=x2),
                 waits=[m_xadd], inc=(dOUT, 16))

        mil_h2T, mil_h2T_tr, tev2 = emit_layernorm(lambda c: m_xadd_slab.get(c // 4, m_xadd), "ln2")
        if "h2T" in debug:
            P.op("sync", lambda e: e.dma_start(out=taps["h2T"], in_=hT3),
                 waits=[mil_h2T], inc=(dOUT, 16))

        m_rmm = None
        for c in range(NCH):
            for mc in range(MC):
                m_rmm = P.op("tensor", lambda e, c=c, mc=mc: e.matmul(
                    s_ps[:, c * 8:(c + 1) * 8], hT3[:, mc, c * 128:(c + 1) * 128],
                    wg3[:, mc, :], start=(mc == 0), stop=(mc == MC - 1)),
                    waits=[tev2[c] if mc == 0 else None,
                           m_exp_bank.get(0) if c == 0 and mc == 0 else None,
                           m_wg if c == 0 and mc == 0 else None],
                    inc=(sT, 1) if mc == MC - 1 else None, tag=f"rt_mm{c}")
        m_lg = P.op("vector", lambda e: e.tensor_copy(lg_sb[:], s_ps[:, 0:NCH * E]),
                    waits=[m_rmm], inc=(sV, 1), tag="rt_ev")
        R = lambda i, c: rt_sb[:, i * NCH + c:i * NCH + c + 1]
        m_rnm = [None] * NCH
        for c in range(NCH):
            m_rnm[c] = P.op("vector", lambda e, c=c: e.reduce_max(
                R(0, c), lg3[:, c, :], axis=AX.X, negate=True),
                waits=[m_lg], inc=(sV, 1), tag=f"rt_mx{c}")
        m_rex = [None] * NCH
        for c in range(NCH):
            m_rex[c] = P.op("scalar", lambda e, c=c: e.activation(
                pr3[:, c, :], lg3[:, c, :], AF.Exp, bias=R(0, c), accum_out=R(1, c)),
                waits=[m_rnm[c]], inc=(sA, 1), tag=f"rt_exp{c}")
        m_rrc = P.op("vector", lambda e: e.reciprocal(
            rt_sb[:, 2 * NCH:3 * NCH], rt_sb[:, NCH:2 * NCH]),
            waits=[m_rex[NCH - 1]], inc=(sV, 1), tag="rt_recip")
        m_cw = None
        for c in range(NCH):
            e1 = P.op("vector", lambda e, c=c: e.tensor_scalar_mul(
                pr3[:, c, :], pr3[:, c, :], R(2, c)),
                waits=[m_rrc], inc=(sV, 1), tag=f"rt_pr{c}")
            m_m8 = P.op("vector", lambda e, c=c: e.max(out=m83[:, c, :], in_=pr3[:, c, :]),
                        waits=[e1], inc=(sV, 1), tag=f"rt_m8{c}")
            e2 = P.op("vector", lambda e, c=c: e.tensor_tensor(
                R(3, c), m83[:, c, 0:1], m83[:, c, 1:2], ALU.add),
                waits=[m_m8], inc=(sV, 1), tag=f"rt_dn{c}")
            e3 = P.op("vector", lambda e, c=c: e.tensor_scalar_add(R(3, c), R(3, c), 1e-8),
                      waits=[e2], inc=(sV, 1))
            m_rd = P.op("vector", lambda e, c=c: e.reciprocal(R(3, c), R(3, c)),
                        waits=[e3], inc=(sV, 1), tag=f"rt_rd{c}")
            e4 = P.op("vector", lambda e, c=c: e.tensor_single_scalar(
                c3[:, c, :], pr3[:, c, :], m83[:, c, 1:2], ALU.is_ge),
                waits=[m_rd], inc=(sV, 1), tag=f"rt_ind{c}")
            e5 = P.op("vector", lambda e, c=c: e.tensor_mul(
                c3[:, c, :], c3[:, c, :], pr3[:, c, :]),
                waits=[e4], inc=(sV, 1), tag=f"rt_ct{c}")
            m_cw = P.op("vector", lambda e, c=c: e.tensor_scalar_mul(
                c3[:, c, :], c3[:, c, :], R(3, c)),
                waits=[e5], inc=(sV, 1), tag=f"rt_cw{c}")
        if "c" in debug:
            P.op("sync", lambda e: e.dma_start(out=taps["c"], in_=c3),
                 waits=[m_cw], inc=(dOUT, 16))
        if "lg" in debug:
            P.op("sync", lambda e: e.dma_start(out=taps["lg"], in_=lg_sb[:]),
                 waits=[m_cw], inc=(dOUT, 16))
        if "pr" in debug:
            P.op("sync", lambda e: e.dma_start(out=taps["pr"], in_=pr_sb[:]),
                 waits=[m_cw], inc=(dOUT, 16))
        if "m8" in debug:
            P.op("sync", lambda e: e.dma_start(out=taps["m8"], in_=lg_sb[:]),
                 waits=[m_cw], inc=(dOUT, 16))

        m_w1 = [None] * (E + 1)
        m_w2 = [None] * (E + 1)
        m_w1[0] = P.op("gpsimd", lambda e: e.dma_start(
            out=w13, in_=w1_d[0].rearrange("(mc p) hh -> p mc hh", p=128)),
            waits=[m_phB_done[H - 1], m_rope_done[H - 1]], inc=(dW1, 16), tag="ld_w1_0")
        m_w2[0] = P.op("gpsimd", lambda e: e.dma_start(
            out=w23, in_=w2_d[0].rearrange("(hc p) d -> p hc d", p=128)),
            waits=[mil_h2T_tr, m_xadd], inc=(dW2, 16), tag="ld_w2_0")

        fbanks = [pp_ps[0], pp_ps[1], mi_ps]
        m_relu = [None, None, None]
        m_yev = [None, None, None]
        m_out_slab = {}
        m_u_last = None
        m_y_last = None
        m_y_slab = None
        for ex in range(E):
            m_sq_last = None
            for ts in range(2):
                for hc in range(HC):
                    for tt in range(2):
                        pp = (hc * 2 + tt) % 3
                        t0 = ts * 1024 + tt * 512
                        mmu = None
                        for mc in range(MC):
                            mmu = P.op("tensor", lambda e, hc=hc, t0=t0, mc=mc, pp=pp: e.matmul(
                                fbanks[pp][:], w13[:, mc, hc * 128:(hc + 1) * 128],
                                hT3[:, mc, t0:t0 + 512],
                                start=(mc == 0), stop=(mc == MC - 1)),
                                waits=[m_w1[ex] if mc == 0 else None,
                                       tev2[ts * 8 + 7] if mc == 0 and ex == 0 and hc == 0 and tt == 0 else None,
                                       m_relu[pp] if mc == 0 else None,
                                       m_rmm if mc == 0 and ex == 0 and ts == 0 and hc == 0 and tt == 0 else None,
                                       m_xadd_bank[0] if mc == 0 and ex == 0 and ts == 0 and hc == 0 and tt == 0 else None,
                                       m_xadd_bank[1] if mc == 0 and ex == 0 and ts == 0 and hc == 0 and tt == 0 else None,
                                       m_phB_done[H - 1] if mc == 0 and ex == 0 and ts == 0 and hc == 0 and tt == 0 else None,
                                       m_lg if mc == 0 and ex == 0 and ts == 0 and hc <= 1 else None,
                                       tp_par[0] if mc == 0 and ex == 0 and ts == 0 and hc <= 1 else None,
                                       m_yev[pp] if mc == 0 else None],
                                inc=(sT, 1) if mc == MC - 1 else None,
                                tag=f"e{ex}u{ts}_{hc}_{tt}")
                        m_u_last = mmu
                        m_relu[pp] = P.op("scalar", lambda e, pp=pp: e.activation(
                            o_sb[pp % 2][:], fbanks[pp][:], AF.Relu),
                            waits=[mmu, m_sq_last], inc=(sA, 1),
                            tag=f"e{ex}relu{ts}_{hc}_{tt}")
                        m_sq_last = P.op("vector", lambda e, pp=pp, hc=hc, tt=tt: e.tensor_mul(
                            act3[:, hc, tt * 512:(tt + 1) * 512],
                            o_sb[pp % 2][:], o_sb[pp % 2][:]),
                            waits=[m_relu[pp],
                                   m_y_slab if hc == 0 and tt == 0 else None],
                            inc=(sV, 1), tag=f"e{ex}sq{ts}_{hc}_{tt}")
                for tc in range(8):
                    ch = ts * 8 + tc
                    for dh in range(2):
                        pp = (tc * 2 + dh) % 3
                        mmy = None
                        for hc in range(HC):
                            mmy = P.op("tensor", lambda e, hc=hc, tc=tc, dh=dh, pp=pp: e.matmul(
                                fbanks[pp][:],
                                act3[:, hc, tc * 128:(tc + 1) * 128],
                                w23[:, hc, dh * 512:(dh + 1) * 512],
                                start=(hc == 0), stop=(hc == HC - 1)),
                                waits=[m_sq_last if hc == 0 else None,
                                       m_w2[ex] if hc == 0 else None,
                                       m_relu[pp] if hc == 0 else None,
                                       m_yev[pp] if hc == 0 else None],
                                inc=(sT, 1) if hc == HC - 1 else None,
                                tag=f"e{ex}y{ts}_{tc}_{dh}")
                        m_y_last = mmy
                        m_yev[pp] = P.op("vector", lambda e, ch=ch, dh=dh, ex=ex, pp=pp: e.scalar_tensor_tensor(
                            x2[:, ch, dh * 512:(dh + 1) * 512],
                            fbanks[pp][:], c3[:, ch, ex:ex + 1],
                            x2[:, ch, dh * 512:(dh + 1) * 512],
                            ALU.mult, ALU.add),
                            waits=[mmy, m_cw if ex == 0 else None],
                            inc=(sV, 1), tag=f"e{ex}yev{ts}_{tc}_{dh}")
                m_y_slab = m_y_last
                if ex == E - 1:
                    m_out_slab[ts] = (m_yev[0], m_yev[1], m_yev[2])
            if ex + 1 < E:
                m_w1[ex + 1] = P.op("gpsimd", lambda e, ex=ex: e.dma_start(
                    out=w13, in_=w1_d[ex + 1].rearrange("(mc p) hh -> p mc hh", p=128)),
                    waits=[m_u_last], inc=(dW1, 16), tag=f"ld_w1_{ex + 1}")
                m_w2[ex + 1] = P.op("gpsimd", lambda e, ex=ex: e.dma_start(
                    out=w23, in_=w2_d[ex + 1].rearrange("(hc p) d -> p hc d", p=128)),
                    waits=[m_y_last], inc=(dW2, 16), tag=f"ld_w2_{ex + 1}")

        yr = y_d.rearrange("(c p) f -> p c f", p=128)
        for ts in range(2):
            P.op("sync", lambda e, ts=ts: e.dma_start(
                out=yr[:, ts * 8:(ts + 1) * 8, :], in_=x2[:, ts * 8:(ts + 1) * 8, :]),
                waits=list(m_out_slab[ts]), inc=(dOUT, 16), tag=f"store_y{ts}")
        P.op("sync", lambda e: e.wait_ge(dOUT.h, dOUT.n), tag="final_wait")

        block = En(nc.Block())
        block.sync(lambda e: P.replay("sync", e))
        block.gpsimd(lambda e: P.replay("gpsimd", e))
        block.tensor(lambda e: P.replay("tensor", e))
        block.vector(lambda e: P.replay("vector", e))
        block.scalar(lambda e: P.replay("scalar", e))

    return nc


_CACHE = {}


def _get_program(debug=()):
    key = tuple(sorted(debug))
    if key not in _CACHE:
        _CACHE[key] = _build(debug=key)
    return _CACHE[key]


def _make_in_maps(inputs, n_cores=8):
    x = np.ascontiguousarray(np.asarray(inputs["L0_attn_x"], dtype=np.float32))
    cos2, ssin2 = _rope_tables()
    common = {
        "wq": np.ascontiguousarray(np.asarray(inputs["L0_attn_wq"], dtype=np.float32)),
        "wk": np.ascontiguousarray(np.asarray(inputs["L0_attn_wk"], dtype=np.float32)),
        "wv": np.ascontiguousarray(np.asarray(inputs["L0_attn_wv"], dtype=np.float32)),
        "wo": np.ascontiguousarray(np.asarray(inputs["L0_attn_wo"], dtype=np.float32)),
        "wg": np.ascontiguousarray(np.asarray(inputs["L0_ffn_Wg"], dtype=np.float32)),
        "w1": np.ascontiguousarray(np.asarray(inputs["L0_ffn_W1"], dtype=np.float32)),
        "w2": np.ascontiguousarray(np.asarray(inputs["L0_ffn_W2"], dtype=np.float32)),
        "cos2": cos2, "ssin2": ssin2,
        "idn": np.eye(128, dtype=ml_dtypes.bfloat16),
        "swp": _swap_matrix(),
        "tri": _trimask(),
    }
    return [dict(common, x=np.ascontiguousarray(x[b])) for b in range(n_cores)]


def kernel(**inputs):
    nc = _get_program()
    in_maps = _make_in_maps(inputs)
    res = run_bass_kernel_spmd(nc, in_maps, core_ids=list(range(8)))
    return np.stack([np.asarray(r["y"], dtype=np.float32) for r in res.results], axis=0)


# version 16 (final)
# speedup vs baseline: 1.2103x; 1.0112x over previous
import numpy as np
import ml_dtypes
from contextlib import ExitStack
import concourse.bass as bass
import concourse.mybir as mybir
from concourse.bass_utils import run_bass_kernel_spmd

F32 = mybir.dt.float32
BF16 = mybir.dt.bfloat16
AF = mybir.ActivationFunctionType
ALU = mybir.AluOpType
AX = mybir.AxisListType

T = 2048
D = 1024
H = 8
DH = 128
E = 8
HID = 2048
NCH = T // 128
MC = D // 128
HC = HID // 128
NSLAB = 4
ENGINES = ("sync", "gpsimd", "tensor", "vector", "scalar")


class Sem:
    __slots__ = ("h", "n")
    def __init__(self, h):
        self.h = h
        self.n = 0


class Plan:
    def __init__(self):
        self.prog = {e: [] for e in ENGINES}
        self.seen = {e: {} for e in ENGINES}

    def op(self, eng, fn, waits=(), inc=None, tag=""):
        ws = []
        seen = self.seen[eng]
        for w in waits:
            if w is None:
                continue
            s, v = w
            if seen.get(s, 0) >= v:
                continue
            seen[s] = v
            ws.append((s, v))
        mil = None
        if inc is not None:
            s, k = inc
            s.n += k
            mil = (s, s.n)
        self.prog[eng].append((fn, ws, inc, tag))
        return mil

    def replay(self, eng_name, eng):
        for fn, ws, inc, tag in self.prog[eng_name]:
            for s, v in ws:
                eng.wait_ge(s.h, v)
            ins = fn(eng)
            if inc is not None:
                s, k = inc
                ins.then_inc(s.h, k)


def _rope_tables():
    half = DH // 2
    pos = np.arange(T, dtype=np.float64)[None, :]
    freqs = np.power(10000.0, -np.arange(half, dtype=np.float64) / half)[:, None]
    rad = freqs * pos
    s = 1.0 / np.sqrt(float(DH))
    cos = np.cos(rad) * s
    sin = np.sin(rad) * s
    cos2 = np.concatenate([cos, cos], axis=0).astype(ml_dtypes.bfloat16)
    ssin2 = np.concatenate([-sin, sin], axis=0).astype(ml_dtypes.bfloat16)
    return np.ascontiguousarray(cos2), np.ascontiguousarray(ssin2)


def _swap_matrix():
    sw = np.zeros((128, 128), dtype=ml_dtypes.bfloat16)
    for d in range(128):
        sw[(d + 64) % 128, d] = 1.0
    return sw


def _trimask():
    m = np.zeros((128, 128), dtype=ml_dtypes.bfloat16)
    i = np.arange(128)
    m[i[:, None] < i[None, :]] = -1e30
    return m


def _build(debug=()):
    nc = bass.Bass("TRN2", target_bir_lowering=False)

    x_d = nc.dram_tensor("x", [T, D], F32, kind="ExternalInput").ap()
    wq_d = nc.dram_tensor("wq", [D, H, DH], F32, kind="ExternalInput").ap()
    wk_d = nc.dram_tensor("wk", [D, H, DH], F32, kind="ExternalInput").ap()
    wv_d = nc.dram_tensor("wv", [D, H, DH], F32, kind="ExternalInput").ap()
    wo_d = nc.dram_tensor("wo", [H, DH, D], F32, kind="ExternalInput").ap()
    wg_d = nc.dram_tensor("wg", [D, E], F32, kind="ExternalInput").ap()
    w1_d = nc.dram_tensor("w1", [E, D, HID], F32, kind="ExternalInput").ap()
    w2_d = nc.dram_tensor("w2", [E, HID, D], F32, kind="ExternalInput").ap()
    cos_d = nc.dram_tensor("cos2", [128, T], BF16, kind="ExternalInput").ap()
    sin_d = nc.dram_tensor("ssin2", [128, T], BF16, kind="ExternalInput").ap()
    idn_d = nc.dram_tensor("idn", [128, 128], BF16, kind="ExternalInput").ap()
    swp_d = nc.dram_tensor("swp", [128, 128], BF16, kind="ExternalInput").ap()
    tri_d = nc.dram_tensor("tri", [128, 128], BF16, kind="ExternalInput").ap()
    y_d = nc.dram_tensor("y", [T, D], F32, kind="ExternalOutput").ap()

    taps = {}
    def tapdecl(name, shape, dtype):
        if name in debug:
            taps[name] = nc.dram_tensor("tap_" + name, shape, dtype,
                                        kind="ExternalOutput").ap()
    tapdecl("hT", [128, MC, T], BF16)
    tapdecl("lnst", [128, 8 * NCH], F32)
    tapdecl("lg", [128, NCH * E], F32)
    tapdecl("pr", [128, NCH * E], F32)
    tapdecl("m8", [128, NCH * E], F32)
    tapdecl("v", [128, NCH, D], BF16)
    tapdecl("q0", [128, T], BF16)
    tapdecl("k0", [128, T], BF16)
    tapdecl("pT0", [128, 16, 512], BF16)
    tapdecl("p15", [128, T], BF16)
    tapdecl("smst", [128, 48], F32)
    tapdecl("xattn", [128, NCH, D], F32)
    tapdecl("h2T", [128, MC, T], BF16)
    tapdecl("c", [128, NCH, E], F32)

    with ExitStack() as ctx:
        En = ctx.enter_context

        x_sb = En(nc.sbuf_tensor("x_sb", [128, NCH * D], F32))
        hT_sb = En(nc.sbuf_tensor("hT_sb", [128, MC * T], BF16))
        v_sb = En(nc.sbuf_tensor("v_sb", [128, NCH * D], BF16))
        arenaA = En(nc.sbuf_tensor("arenaA", [128, 16384], BF16))
        arenaB = En(nc.sbuf_tensor("arenaB", [128, 16384], BF16))
        cos_sb = En(nc.sbuf_tensor("cos_sb", [128, T], BF16))
        sin_sb = En(nc.sbuf_tensor("sin_sb", [128, T], BF16))
        o_sb = [En(nc.sbuf_tensor(f"o_sb{i}", [128, 512], BF16)) for i in range(2)]
        idn_sb = En(nc.sbuf_tensor("idn_sb", [128, 128], BF16))
        swp_sb = En(nc.sbuf_tensor("swp_sb", [128, 128], BF16))
        tri_sb = En(nc.sbuf_tensor("tri_sb", [128, 128], BF16))
        diag4_sb = En(nc.sbuf_tensor("diag4_sb", [128, 256], BF16))
        sqtrash = En(nc.sbuf_tensor("sqtrash", [128, 1024], BF16))
        wg_sb = En(nc.sbuf_tensor("wg_sb", [128, MC * E], BF16))
        ln_st = En(nc.sbuf_tensor("ln_st", [128, 8 * NCH], F32))
        sm_st = En(nc.sbuf_tensor("sm_st", [128, 12], F32))
        lg_sb = En(nc.sbuf_tensor("lg_sb", [128, NCH * E], F32))
        pr_sb = En(nc.sbuf_tensor("pr_sb", [128, NCH * E], F32))
        c_sb = En(nc.sbuf_tensor("c_sb", [128, NCH * E], F32))
        rt_sb = En(nc.sbuf_tensor("rt_sb", [128, 4 * NCH], F32))

        aA = arenaA[:]
        pT3 = aA[:, 0:8192].rearrange("p (jc i) -> p jc i", jc=16)
        wv3 = aA[:, 8192:16384].rearrange("p (mc d) -> p mc d", mc=MC)
        qraw = [aA[:, 8192 + i * 512:8704 + i * 512] for i in range(2)]
        swb = [aA[:, 9216 + i * 512:9728 + i * 512] for i in range(2)]
        rt1 = aA[:, 10240:10752]
        rt2 = aA[:, 10752:11264]
        w13 = aA.rearrange("p (mc hh) -> p mc hh", mc=MC)

        aB = arenaB[:]
        q_sb = aB[:, 0:2048]
        k_sb = aB[:, 2048:4096]
        p_sb = [aB[:, 4096:6144], aB[:, 6144:8192], aA[:, 8192:10240]]
        htok = [aB[:, 8192 + i * 1024:9216 + i * 1024] for i in range(2)]
        wqh = [aB[:, 10240 + i * 1024:11264 + i * 1024] for i in range(2)]
        wkh = [aB[:, 12288 + i * 1024:13312 + i * 1024] for i in range(2)]
        woh = [aB[:, 14336 + i * 1024:15360 + i * 1024] for i in range(2)]
        w23 = aB.rearrange("p (hc d) -> p hc d", hc=HC)

        x2 = x_sb[:].rearrange("p (c f) -> p c f", c=NCH)
        hT3 = hT_sb[:].rearrange("p (mc t) -> p mc t", mc=MC)
        v3 = v_sb[:].rearrange("p (c f) -> p c f", c=NCH)
        act3 = v3
        wg3 = wg_sb[:].rearrange("p (mc e) -> p mc e", mc=MC)
        lg3 = lg_sb[:].rearrange("p (c e) -> p c e", c=NCH)
        pr3 = pr_sb[:].rearrange("p (c e) -> p c e", c=NCH)
        m83 = lg_sb[:].rearrange("p (c e) -> p c e", c=NCH)
        c3 = c_sb[:].rearrange("p (c e) -> p c e", c=NCH)

        s_ps = En(nc.psum_tensor("s_ps", [128, 2048], F32))
        mi_ps = En(nc.psum_tensor("mi_ps", [128, 512], F32))
        tp_ps = En(nc.psum_tensor("tp_ps", [128, 512], F32))
        pp_ps = [En(nc.psum_tensor(f"pp{i}", [128, 512], F32)) for i in range(2)]

        dIN = Sem(En(nc.semaphore("dIN")))
        dX = [Sem(En(nc.semaphore(f"dX{i}"))) for i in range(4)]
        dWV = Sem(En(nc.semaphore("dWV")))
        dWG = Sem(En(nc.semaphore("dWG")))
        dWH = [Sem(En(nc.semaphore(f"dWH{i}"))) for i in range(2)]
        dW1 = Sem(En(nc.semaphore("dW1")))
        dW2 = Sem(En(nc.semaphore("dW2")))
        dOUT = Sem(En(nc.semaphore("dOUT")))
        sT = Sem(En(nc.semaphore("sT")))
        sV = Sem(En(nc.semaphore("sV")))
        sA = Sem(En(nc.semaphore("sA")))
        sG = Sem(En(nc.semaphore("sG")))

        P = Plan()
        tp_par = {0: None, 1: None}

        xr = x_d.rearrange("(c p) f -> p c f", p=128)
        m_xg = []
        for g in range(4):
            m_xg.append(P.op("sync", lambda e, g=g: e.dma_start(
                out=x2[:, 4 * g:4 * (g + 1), :], in_=xr[:, 4 * g:4 * (g + 1), :]),
                inc=(dX[g], 16), tag=f"ld_x{g}"))
        m_x = m_xg[3]
        P.op("sync", lambda e: e.dma_start(out=cos_sb[:], in_=cos_d), inc=(dIN, 16))
        P.op("sync", lambda e: e.dma_start(out=sin_sb[:], in_=sin_d), inc=(dIN, 16))
        P.op("sync", lambda e: e.dma_start(out=idn_sb[:], in_=idn_d), inc=(dIN, 16))
        P.op("sync", lambda e: e.dma_start(out=swp_sb[:], in_=swp_d), inc=(dIN, 16))
        m_tbl = P.op("sync", lambda e: e.dma_start(out=tri_sb[:], in_=tri_d), inc=(dIN, 16))
        m_cos = m_sin = m_idn = m_swp = m_tri = m_tbl

        m_wv = P.op("gpsimd", lambda e: e.dma_start(
            out=wv3, in_=wv_d.rearrange("(mc p) h d -> p mc (h d)", p=128)),
            inc=(dWV, 16), tag="ld_wv")
        m_wg = P.op("gpsimd", lambda e: e.dma_start(
            out=wg3, in_=wg_d.rearrange("(mc p) e -> p mc e", p=128)),
            inc=(dWG, 16), tag="ld_wg")

        def load_head_weights(h, waits):
            b = h % 2
            P.op("gpsimd", lambda e, h=h, b=b: e.dma_start(
                out=wqh[b].rearrange("p (mc d) -> p mc d", mc=MC),
                in_=wq_d[:, h, :].rearrange("(mc p) d -> p mc d", p=128)),
                waits=waits, inc=(dWH[b], 16), tag=f"ld_wq{h}")
            P.op("gpsimd", lambda e, h=h, b=b: e.dma_start(
                out=wkh[b].rearrange("p (mc d) -> p mc d", mc=MC),
                in_=wk_d[:, h, :].rearrange("(mc p) d -> p mc d", p=128)),
                inc=(dWH[b], 16), tag=f"ld_wk{h}")
            mo = P.op("gpsimd", lambda e, h=h, b=b: e.dma_start(
                out=woh[b], in_=wo_d[h]), inc=(dWH[b], 16), tag=f"ld_wo{h}")
            return mo

        m_wh = [None] * (H + 1)
        m_wh[0] = load_head_weights(0, [])

        def slot(i, c):
            return ln_st[:, i * NCH + c:i * NCH + c + 1]

        def SS(i):
            return ln_st[:, i * NCH:(i + 1) * NCH]

        def emit_layernorm(wait_src_fn, phase):
            m_sum = m_sq = None
            for c in range(NCH):
                m_sum = P.op("vector", lambda e, c=c: e.reduce_sum(
                    slot(0, c), x2[:, c, :], axis=AX.X),
                    waits=[wait_src_fn(c)], inc=(sV, 1) if c == NCH - 1 else None,
                    tag=f"{phase}_sum{c}")
                m_sq = P.op("scalar", lambda e, c=c: e.activation(
                    sqtrash[:], x2[:, c, :], AF.Square, accum_out=slot(1, c)),
                    waits=[wait_src_fn(c), m_sq], inc=(sA, 1),
                    tag=f"{phase}_sq{c}")
            mm1 = P.op("vector", lambda e: e.tensor_scalar_mul(SS(2), SS(0), 1.0 / D),
                       waits=[m_sum, m_sq], inc=(sV, 1), tag=f"{phase}_m")
            mm2 = P.op("vector", lambda e: e.tensor_scalar_mul(SS(3), SS(2), -1.0),
                       waits=[mm1], inc=(sV, 1))
            mm3 = P.op("vector", lambda e: e.tensor_scalar_mul(SS(4), SS(1), 1.0 / D),
                       inc=(sV, 1))
            mm4 = P.op("vector", lambda e: e.tensor_tensor(SS(7), SS(2), SS(2), ALU.mult),
                       waits=[mm1], inc=(sV, 1))
            mm5 = P.op("vector", lambda e: e.tensor_sub(SS(4), SS(4), SS(7)),
                       waits=[mm3, mm4], inc=(sV, 1))
            m_veps = P.op("vector", lambda e: e.tensor_scalar_add(SS(4), SS(4), 1e-5),
                          waits=[mm5], inc=(sV, 1), tag=f"{phase}_veps")
            m_ln = P.op("scalar", lambda e: e.activation(SS(7), SS(4), AF.Ln),
                        waits=[m_veps], inc=(sA, 1), tag=f"{phase}_ln")
            m_rstd = P.op("scalar", lambda e: e.activation(SS(5), SS(7), AF.Exp, scale=-0.5),
                          waits=[m_ln], inc=(sA, 1), tag=f"{phase}_rstd")
            m_bias = P.op("vector", lambda e: e.tensor_tensor(SS(6), SS(3), SS(5), ALU.mult),
                          waits=[m_rstd], inc=(sV, 1), tag=f"{phase}_bias")
            m_tev = [None] * (2 * NCH)
            m_ttr = [None] * (2 * NCH)
            for c in range(NCH):
                b = c % 2
                m_ap = P.op("vector", lambda e, c=c, b=b: e.tensor_scalar(
                    htok[b], x2[:, c, :], slot(5, c), slot(6, c), ALU.mult, ALU.add),
                    waits=[m_bias, m_rstd, m_ttr[2 * c - 4] if c >= 2 else None],
                    inc=(sV, 1), tag=f"{phase}_apply{c}")
                for g in range(2):
                    half = g
                    mm = None
                    for q in range(4):
                        mc = g * 4 + q
                        mm = P.op("tensor", lambda e, b=b, mc=mc, q=q, half=half: e.matmul(
                            (tp_ps if half == 0 else mi_ps)[:, q * 128:(q + 1) * 128],
                            htok[b][:, mc * 128:(mc + 1) * 128], idn_sb[:],
                            start=True, stop=True),
                            waits=[m_ap if q == 0 else None,
                                   m_idn if c == 0 and g == 0 and q == 0 else None,
                                   tp_par[half] if q == 0 else None],
                            inc=(sT, 1) if q == 3 else None, tag=f"{phase}_tr{c}_{mc}")
                    m_ttr[2 * c + g] = mm
                    m_tev[2 * c + g] = P.op("vector", lambda e, c=c, g=g, half=half: e.tensor_copy(
                        hT3[:, g * 4:(g + 1) * 4, c * 128:(c + 1) * 128],
                        (tp_ps if half == 0 else mi_ps)[:].rearrange(
                            "p (q t) -> p q t", q=4)),
                        waits=[mm], inc=(sV, 1), tag=f"{phase}_tev{c}_{g}")
                    tp_par[half] = m_tev[2 * c + g]
            return m_tev[2 * NCH - 1], m_ttr[2 * NCH - 1], [m_tev[2 * c + 1] for c in range(NCH)]

        mil_h1T, mil_h1T_tr, tev1 = emit_layernorm(lambda c: m_xg[c // 4], "ln1")
        if "lnst" in debug:
            P.op("sync", lambda e: e.dma_start(out=taps["lnst"], in_=ln_st[:]),
                 waits=[mil_h1T], inc=(dOUT, 16))
        if "hT" in debug:
            P.op("sync", lambda e: e.dma_start(out=taps["hT"], in_=hT3),
                 waits=[mil_h1T], inc=(dOUT, 16))

        m_vev = [None, None]
        last_v_mm = None
        for c in range(NCH):
            for nh in range(2):
                pp = (c * 2 + nh) % 2
                mms = None
                for mc in range(MC):
                    mms = P.op("tensor", lambda e, c=c, nh=nh, mc=mc, pp=pp: e.matmul(
                        pp_ps[pp][:], hT3[:, mc, c * 128:(c + 1) * 128],
                        wv3[:, mc, nh * 512:(nh + 1) * 512],
                        start=(mc == 0), stop=(mc == MC - 1)),
                        waits=[tev1[c] if mc == 0 else None,
                               m_wv if mc == 0 and c == 0 else None,
                               m_vev[pp] if mc == 0 else None],
                        inc=(sT, 1) if mc == MC - 1 else None, tag=f"v_mm{c}_{nh}")
                last_v_mm = mms
                m_vev[pp] = P.op("scalar", lambda e, c=c, nh=nh, pp=pp: e.activation(
                    v3[:, c, nh * 512:(nh + 1) * 512], pp_ps[pp][:], AF.Copy),
                    waits=[mms], inc=(sA, 1), tag=f"v_ev{c}_{nh}")
        mil_v_all = (m_vev[0], m_vev[1])
        if "v" in debug:
            P.op("sync", lambda e: e.dma_start(out=taps["v"], in_=v3),
                 waits=[m_vev[0], m_vev[1]], inc=(dOUT, 16))

        m_phB_done = [None] * H
        m_exp_bank = {}
        m_xadd_slab = {}
        m_diag_mm = {"last": None, 0: None, 1: None, 2: None}
        m_xadd = None
        m_xadd_bank = [None, None, None]
        m_oev = [None, None]
        m_proj_mm = [None, None]
        m_rope_done = [None] * H

        def SUM(par, k=None):
            o = par * 4
            return sm_st[:, o + k:o + k + 1] if k is not None else sm_st[:, o:o + 4]
        def SSUM(par):
            return sm_st[:, 8 + par:9 + par]
        def RR(par):
            return sm_st[:, 10 + par:11 + par]

        for h in range(H):
            b = h % 2
            if h + 1 < H:
                m_wh[h + 1] = load_head_weights(
                    h + 1, [m_rope_done[h - 1] if h >= 1 else None,
                            m_proj_mm[0] if h >= 1 else None,
                            m_proj_mm[1] if h >= 1 else None])

            wq3 = wqh[b].rearrange("p (mc d) -> p mc d", mc=MC)
            wk3 = wkh[b].rearrange("p (mc d) -> p mc d", mc=MC)
            raw_slots = [pp_ps[0][:], s_ps[:, 0:512], s_ps[:, 512:1024],
                         s_ps[:, 1024:1536]]
            swp_slots = [pp_ps[1][:], tp_ps[:]]
            m_raw_ev = [None] * 4
            m_swp_ev = [None] * 2
            m_rt1 = [None] * 2
            m_rope = None

            def emit_qk_proj(idx):
                mat, tt = divmod(idx, 4)
                w3 = wq3 if mat == 0 else wk3
                rs = idx % 4
                mms = None
                for mc in range(MC):
                    mms = P.op("tensor", lambda e, w3=w3, tt=tt, mc=mc, rs=rs: e.matmul(
                        raw_slots[rs], w3[:, mc, :], hT3[:, mc, tt * 512:(tt + 1) * 512],
                        start=(mc == 0), stop=(mc == MC - 1)),
                        waits=[m_wh[h] if mc == 0 else None,
                               tev1[tt * 4 + 3] if mc == 0 and h == 0 else None,
                               m_raw_ev[rs] if mc == 0 else None,
                               m_exp_bank.get(rs - 1) if mc == 0 and rs >= 1 else None,
                               m_oev[0] if mc == 0 and idx == 0 else None,
                               m_oev[1] if mc == 0 and idx == 0 else None,
                               m_xadd if mc == 0 and idx == 0 else None,
                               m_phB_done[h - 1] if mc == 0 and idx == 0 and h >= 1 else None,
                               mil_v_all[0] if mc == 0 and idx == 0 and h == 0 else None,
                               mil_v_all[1] if mc == 0 and idx == 0 and h == 0 else None],
                        inc=(sT, 1) if mc == MC - 1 else None, tag=f"h{h}qk{idx}")
                m_raw_ev[rs] = P.op("scalar", lambda e, idx=idx, rs=rs: e.activation(
                    qraw[idx % 2], raw_slots[rs], AF.Copy),
                    waits=[mms, m_rt1[idx % 2],
                           last_v_mm if h == 0 and idx == 0 else None],
                    inc=(sA, 1), tag=f"h{h}qraw{idx}")

            def emit_qk_rope(idx):
                nonlocal m_rope
                mat, tt = divmod(idx, 4)
                dst = q_sb if mat == 0 else k_sb
                rs = idx % 4
                ss = idx % 2
                msw = P.op("tensor", lambda e, ss=ss: e.matmul(
                    swp_slots[ss], swp_sb[:], qraw[ss], start=True, stop=True),
                    waits=[m_raw_ev[rs], m_swp, m_swp_ev[ss],
                           tp_par[0] if ss == 1 and idx == 1 else None,
                           m_xadd if idx == 0 else None],
                    inc=(sT, 1), tag=f"h{h}sw{idx}")
                m_swp_ev[ss] = P.op("vector", lambda e, ss=ss: e.tensor_copy(
                    swb[ss], swp_slots[ss]),
                    waits=[msw], inc=(sV, 1), tag=f"h{h}swev{idx}")
                m_rt1[ss] = P.op("vector", lambda e, ss=ss, tt=tt: e.tensor_mul(
                    rt1, qraw[ss], cos_sb[:, tt * 512:(tt + 1) * 512]),
                    waits=[m_cos, m_sin, m_raw_ev[rs],
                           m_phB_done[h - 1] if h >= 1 and idx == 0 else None,
                           last_v_mm if h == 0 and idx == 0 else None],
                    inc=(sV, 1), tag=f"h{h}rp1_{idx}")
                P.op("vector", lambda e, ss=ss, tt=tt: e.tensor_mul(
                    rt2, swb[ss], sin_sb[:, tt * 512:(tt + 1) * 512]))
                m_rope = P.op("vector", lambda e, dst=dst, tt=tt: e.tensor_add(
                    dst[:, tt * 512:(tt + 1) * 512], rt1, rt2),
                    inc=(sV, 1), tag=f"h{h}rp3_{idx}")

            for idx in range(8):
                emit_qk_proj(idx)
                if idx >= 1:
                    emit_qk_rope(idx - 1)
            emit_qk_rope(7)
            m_rope_done[h] = m_rope
            m_sbank_free = {0: m_raw_ev[1], 1: m_raw_ev[2], 2: m_raw_ev[3]}

            if h == 0 and "q0" in debug:
                P.op("sync", lambda e: e.dma_start(out=taps["q0"], in_=q_sb),
                     waits=[m_rope_done[0]], inc=(dOUT, 16))
            if h == 0 and "k0" in debug:
                P.op("sync", lambda e: e.dma_start(out=taps["k0"], in_=k_sb),
                     waits=[m_rope_done[0]], inc=(dOUT, 16))

            m_phB_local = None
            m_nmx = {}
            m_expm = {}
            m_tail = {}
            m_sumfree = {}
            m_trp_par = {0: None, 1: None}
            m_tgrp = {}
            m_ptev = {}
            m_ptev_last = None

            def emit_scores_bank(ci, k):
                par = ci % 2
                pbuf = ci % 3
                ph = k * 512
                jlen = (ci + 1) * 128
                j0 = k * 512
                j1 = min(j0 + 512, jlen)
                w = j1 - j0
                msc = P.op("tensor", lambda e, ci=ci, j0=j0, j1=j1, ph=ph, w=w: e.matmul(
                    s_ps[:, ph:ph + w], q_sb[:, ci * 128:(ci + 1) * 128],
                    k_sb[:, j0:j1], start=True, stop=True),
                    waits=[m_rope_done[h] if ci == 0 and k == 0 else None,
                           m_exp_bank.get(k),
                           m_sbank_free.get(k)],
                    inc=(sT, 1), tag=f"h{h}s{ci}_{k}")
                last = (j1 == jlen)
                m_msk = None
                if last:
                    lo = ph + (w - 128)
                    m_msk = P.op("vector", lambda e, lo=lo: e.tensor_tensor(
                        s_ps[:, lo:lo + 128], s_ps[:, lo:lo + 128],
                        tri_sb[:], ALU.add),
                        waits=[msc, m_tri], inc=(sV, 1), tag=f"h{h}msk{ci}")
                m_expm[(ci, k)] = P.op("scalar", lambda e, k=k, j0=j0, ph=ph, w=w, par=par, pbuf=pbuf: e.activation(
                    p_sb[pbuf][:, j0:j0 + w], s_ps[:, ph:ph + w], AF.Exp,
                    accum_out=SUM(par, k)),
                    waits=[msc, m_msk, m_diag_mm[pbuf],
                           m_sumfree.get(par),
                           m_rope_done[h] if pbuf == 2 else None],
                    inc=(sA, 1), tag=f"h{h}exp{ci}_{k}")
                m_exp_bank[k] = m_expm[(ci, k)]

            def emit_tail(ci):
                par = ci % 2
                dpar = ci % 2
                K = (ci + 1 + 3) // 4
                t5 = P.op("vector", lambda e, par=par, K=K: e.reduce_sum(
                    SSUM(par), SUM(par)[:, 0:K], axis=AX.X),
                    waits=[m_expm[(ci, K - 1)]], inc=(sV, 1), tag=f"h{h}tl_s{ci}")
                m_sumfree[par] = t5
                t6 = P.op("vector", lambda e, par=par: e.reciprocal(RR(par), SSUM(par)),
                          waits=[t5], inc=(sV, 1), tag=f"h{h}tl_r{ci}")
                m_tail[ci] = P.op("vector", lambda e, dpar=dpar: e.tensor_scalar_mul(
                    diag4_sb[:, dpar * 128:(dpar + 1) * 128], idn_sb[:], RR(dpar)),
                    waits=[t6, m_trp_par[dpar], m_idn],
                    inc=(sV, 1), tag=f"h{h}tl_dg{ci}")

            def emit_transposes(ci):
                nonlocal m_ptev_last
                pbuf = ci % 3
                cl = ci % 4
                sl = ci // 4
                njc = ci + 1
                for g0 in range(0, njc, 4):
                    g1 = min(g0 + 4, njc)
                    k = g0 // 4
                    half = k % 2
                    dst_ps = tp_ps if half == 0 else mi_ps
                    mmg = None
                    for jc in range(g0, g1):
                        mmg = P.op("tensor", lambda e, pbuf=pbuf, jc=jc, g0=g0, k=k, dst_ps=dst_ps: e.matmul(
                            dst_ps[:, (jc - g0) * 128:(jc - g0 + 1) * 128],
                            p_sb[pbuf][:, jc * 128:(jc + 1) * 128],
                            diag4_sb[:, (ci % 2) * 128:(ci % 2 + 1) * 128],
                            start=True, stop=True),
                            waits=[m_tail[ci] if jc == g0 else None,
                                   tp_par[half] if jc == g0 else None,
                                   m_swp_ev[1] if jc == g0 and half == 0 else None,
                                   m_phB_local if jc == g0 and g0 == 0 and cl == 0 else None,
                                   m_oev[0] if jc == g0 and half == 1 else None,
                                   m_oev[1] if jc == g0 and half == 1 else None,
                                   m_phB_done[h - 1] if jc == g0 and g0 == 0 and ci == 0 and h >= 1 else None,
                                   last_v_mm if jc == g0 and g0 == 0 and ci == 0 and h == 0 else None],
                            inc=(sT, 1) if jc == g1 - 1 else None,
                            tag=f"h{h}pt{ci}_{jc}")
                    m_diag_mm[pbuf] = mmg
                    m_diag_mm["last"] = mmg
                    m_trp_par[ci % 2] = mmg
                    if half == 0:
                        m_ptev_last = P.op("scalar", lambda e, cl=cl, g0=g0, g1=g1: e.activation(
                            pT3[:, g0:g1, cl * 128:(cl + 1) * 128],
                            tp_ps[:, 0:(g1 - g0) * 128].rearrange(
                                "p (g i) -> p g i", g=g1 - g0), AF.Copy),
                            waits=[mmg], inc=(sA, 1), tag=f"h{h}ptev{ci}_{g0}")
                    else:
                        m_ptev_last = P.op("vector", lambda e, cl=cl, g0=g0, g1=g1: e.tensor_copy(
                            pT3[:, g0:g1, cl * 128:(cl + 1) * 128],
                            mi_ps[:, 0:(g1 - g0) * 128].rearrange(
                                "p (g i) -> p g i", g=g1 - g0)),
                            waits=[mmg], inc=(sV, 1), tag=f"h{h}ptev{ci}_{g0}")
                    tp_par[half] = m_ptev_last

            def emit_phaseB(sl):
                nonlocal m_phB_local, m_xadd
                par = sl % 2
                m_omm = None
                for jc in range(sl * 4 + 4):
                    lo = max(0, jc - sl * 4) * 128
                    m_omm = P.op("tensor", lambda e, jc=jc, lo=lo, h=h: e.matmul(
                        mi_ps[:, lo:512] if lo else mi_ps[:],
                        v3[:, jc, h * 128:(h + 1) * 128],
                        pT3[:, jc, lo:512] if lo else pT3[:, jc, :],
                        start=(jc == 0), stop=(jc == sl * 4 + 3)),
                        waits=[m_ptev_last if jc == 0 else None,
                               m_oev[1 - par] if jc == 0 else None,
                               tp_par[0] if jc == 0 else None,
                               tp_par[1] if jc == 0 else None],
                        inc=(sT, 1) if jc == sl * 4 + 3 else None, tag=f"h{h}o{sl}_{jc}")
                m_phB_local = m_omm
                m_phB_done[h] = m_omm
                m_oev[par] = P.op("vector", lambda e, par=par: e.tensor_copy(
                    o_sb[par][:], mi_ps[:]),
                    waits=[m_omm, m_proj_mm[par]], inc=(sV, 1), tag=f"h{h}oev{sl}")
                tp_par[1] = m_oev[par]
                pbanks = [pp_ps[1], pp_ps[0], tp_ps]
                for cl in range(4):
                    for mh in range(2):
                        bank = (cl * 2 + mh) % 3
                        mpj = P.op("tensor", lambda e, par=par, cl=cl, mh=mh, b=b, bank=bank, pbanks=pbanks: e.matmul(
                            pbanks[bank][:], o_sb[par][:, cl * 128:(cl + 1) * 128],
                            woh[b][:, mh * 512:(mh + 1) * 512],
                            start=True, stop=True),
                            waits=[m_oev[par], m_xadd_bank[bank],
                                   tp_par[0] if bank == 2 else None,
                                   m_raw_ev[0] if cl == 0 and mh == 1 else None,
                                   m_swp_ev[0] if cl == 0 and mh == 0 else None],
                            inc=(sT, 1), tag=f"h{h}pj{sl}_{cl}_{mh}")
                        m_proj_mm[par] = mpj
                        m_xadd = P.op("vector", lambda e, sl=sl, cl=cl, mh=mh, bank=bank, pbanks=pbanks: e.tensor_tensor(
                            x2[:, sl * 4 + cl, mh * 512:(mh + 1) * 512],
                            x2[:, sl * 4 + cl, mh * 512:(mh + 1) * 512],
                            pbanks[bank][:], ALU.add),
                            waits=[mpj], inc=(sV, 1), tag=f"h{h}xadd{sl}_{cl}_{mh}")
                        m_xadd_bank[bank] = m_xadd
                        if bank == 2:
                            tp_par[0] = m_xadd
                if h == H - 1:
                    m_xadd_slab[sl] = m_xadd

            for step in range(18):
                if step < 16:
                    ci = step
                    K = ((ci + 1) * 128 + 511) // 512
                    for k in range(K):
                        emit_scores_bank(ci, k)
                if 1 <= step <= 16:
                    emit_tail(step - 1)
                if step >= 6 and (step - 6) % 4 == 0 and step < 18:
                    emit_phaseB((step - 6) // 4)
                if step >= 2:
                    emit_transposes(step - 2)
            emit_phaseB(3)

            if h == 0 and "pT0" in debug:
                P.op("sync", lambda e: e.dma_start(out=taps["pT0"], in_=pT3),
                     waits=[m_ptev_last], inc=(dOUT, 16))
            if h == 0 and "p15" in debug:
                P.op("sync", lambda e: e.dma_start(out=taps["p15"], in_=p_sb[1]),
                     waits=[m_expm[(15, 3)], m_tail[15]], inc=(dOUT, 16))
            if h == 0 and "smst" in debug:
                P.op("sync", lambda e: e.dma_start(out=taps["smst"], in_=sm_st[:]),
                     waits=[m_tail[15]], inc=(dOUT, 16))

        if "xattn" in debug:
            P.op("sync", lambda e: e.dma_start(out=taps["xattn"], in_=x2),
                 waits=[m_xadd], inc=(dOUT, 16))

        mil_h2T, mil_h2T_tr, tev2 = emit_layernorm(lambda c: m_xadd_slab.get(c // 4, m_xadd), "ln2")
        if "h2T" in debug:
            P.op("sync", lambda e: e.dma_start(out=taps["h2T"], in_=hT3),
                 waits=[mil_h2T], inc=(dOUT, 16))

        m_rmm = None
        for c in range(NCH):
            for mc in range(MC):
                m_rmm = P.op("tensor", lambda e, c=c, mc=mc: e.matmul(
                    s_ps[:, c * 8:(c + 1) * 8], hT3[:, mc, c * 128:(c + 1) * 128],
                    wg3[:, mc, :], start=(mc == 0), stop=(mc == MC - 1)),
                    waits=[tev2[c] if mc == 0 else None,
                           m_exp_bank.get(0) if c == 0 and mc == 0 else None,
                           m_wg if c == 0 and mc == 0 else None],
                    inc=(sT, 1) if mc == MC - 1 else None, tag=f"rt_mm{c}")
        m_lg = P.op("vector", lambda e: e.tensor_copy(lg_sb[:], s_ps[:, 0:NCH * E]),
                    waits=[m_rmm], inc=(sV, 1), tag="rt_ev")
        R = lambda i, c: rt_sb[:, i * NCH + c:i * NCH + c + 1]
        m_rnm = [None] * NCH
        for c in range(NCH):
            m_rnm[c] = P.op("vector", lambda e, c=c: e.reduce_max(
                R(0, c), lg3[:, c, :], axis=AX.X, negate=True),
                waits=[m_lg], inc=(sV, 1), tag=f"rt_mx{c}")
        m_rex = [None] * NCH
        for c in range(NCH):
            m_rex[c] = P.op("scalar", lambda e, c=c: e.activation(
                pr3[:, c, :], lg3[:, c, :], AF.Exp, bias=R(0, c), accum_out=R(1, c)),
                waits=[m_rnm[c]], inc=(sA, 1), tag=f"rt_exp{c}")
        m_rrc = P.op("vector", lambda e: e.reciprocal(
            rt_sb[:, 2 * NCH:3 * NCH], rt_sb[:, NCH:2 * NCH]),
            waits=[m_rex[NCH - 1]], inc=(sV, 1), tag="rt_recip")
        m_cw = None
        for c in range(NCH):
            e1 = P.op("vector", lambda e, c=c: e.tensor_scalar_mul(
                pr3[:, c, :], pr3[:, c, :], R(2, c)),
                waits=[m_rrc], inc=(sV, 1), tag=f"rt_pr{c}")
            m_m8 = P.op("vector", lambda e, c=c: e.max(out=m83[:, c, :], in_=pr3[:, c, :]),
                        waits=[e1], inc=(sV, 1), tag=f"rt_m8{c}")
            e2 = P.op("vector", lambda e, c=c: e.tensor_tensor(
                R(3, c), m83[:, c, 0:1], m83[:, c, 1:2], ALU.add),
                waits=[m_m8], inc=(sV, 1), tag=f"rt_dn{c}")
            e3 = P.op("vector", lambda e, c=c: e.tensor_scalar_add(R(3, c), R(3, c), 1e-8),
                      waits=[e2], inc=(sV, 1))
            m_rd = P.op("vector", lambda e, c=c: e.reciprocal(R(3, c), R(3, c)),
                        waits=[e3], inc=(sV, 1), tag=f"rt_rd{c}")
            e4 = P.op("vector", lambda e, c=c: e.tensor_single_scalar(
                c3[:, c, :], pr3[:, c, :], m83[:, c, 1:2], ALU.is_ge),
                waits=[m_rd], inc=(sV, 1), tag=f"rt_ind{c}")
            e5 = P.op("vector", lambda e, c=c: e.tensor_mul(
                c3[:, c, :], c3[:, c, :], pr3[:, c, :]),
                waits=[e4], inc=(sV, 1), tag=f"rt_ct{c}")
            m_cw = P.op("vector", lambda e, c=c: e.tensor_scalar_mul(
                c3[:, c, :], c3[:, c, :], R(3, c)),
                waits=[e5], inc=(sV, 1), tag=f"rt_cw{c}")
        if "c" in debug:
            P.op("sync", lambda e: e.dma_start(out=taps["c"], in_=c3),
                 waits=[m_cw], inc=(dOUT, 16))
        if "lg" in debug:
            P.op("sync", lambda e: e.dma_start(out=taps["lg"], in_=lg_sb[:]),
                 waits=[m_cw], inc=(dOUT, 16))
        if "pr" in debug:
            P.op("sync", lambda e: e.dma_start(out=taps["pr"], in_=pr_sb[:]),
                 waits=[m_cw], inc=(dOUT, 16))
        if "m8" in debug:
            P.op("sync", lambda e: e.dma_start(out=taps["m8"], in_=lg_sb[:]),
                 waits=[m_cw], inc=(dOUT, 16))

        m_w1 = [None] * (E + 1)
        m_w2 = [None] * (E + 1)
        m_w1[0] = P.op("gpsimd", lambda e: e.dma_start(
            out=w13, in_=w1_d[0].rearrange("(mc p) hh -> p mc hh", p=128)),
            waits=[m_phB_done[H - 1], m_rope_done[H - 1]], inc=(dW1, 16), tag="ld_w1_0")
        m_w2[0] = P.op("gpsimd", lambda e: e.dma_start(
            out=w23, in_=w2_d[0].rearrange("(hc p) d -> p hc d", p=128)),
            waits=[mil_h2T_tr, m_xadd], inc=(dW2, 16), tag="ld_w2_0")

        fbanks = [pp_ps[0][:], pp_ps[1][:], mi_ps[:],
                  s_ps[:, 0:512], s_ps[:, 512:1024], s_ps[:, 1024:1536]]
        m_relu = [None] * 6
        m_yev = [None] * 6
        m_out_slab = {}
        m_u_last = None
        m_y_last = None
        m_y_slab = None
        for ex in range(E):
            m_sq_last = None
            for ts in range(2):
                for hc in range(HC):
                    for tt in range(2):
                        pp = (hc * 2 + tt) % 6
                        t0 = ts * 1024 + tt * 512
                        mmu = None
                        for mc in range(MC):
                            mmu = P.op("tensor", lambda e, hc=hc, t0=t0, mc=mc, pp=pp: e.matmul(
                                fbanks[pp], w13[:, mc, hc * 128:(hc + 1) * 128],
                                hT3[:, mc, t0:t0 + 512],
                                start=(mc == 0), stop=(mc == MC - 1)),
                                waits=[m_w1[ex] if mc == 0 else None,
                                       tev2[ts * 8 + 7] if mc == 0 and ex == 0 and hc == 0 and tt == 0 else None,
                                       m_relu[pp] if mc == 0 else None,
                                       m_rmm if mc == 0 and ex == 0 and ts == 0 and hc == 0 and tt == 0 else None,
                                       m_xadd_bank[0] if mc == 0 and ex == 0 and ts == 0 and hc == 0 and tt == 0 else None,
                                       m_xadd_bank[1] if mc == 0 and ex == 0 and ts == 0 and hc == 0 and tt == 0 else None,
                                       m_phB_done[H - 1] if mc == 0 and ex == 0 and ts == 0 and hc == 0 and tt == 0 else None,
                                       m_lg if mc == 0 and ex == 0 and ts == 0 and hc <= 2 else None,
                                       m_exp_bank.get(pp - 3) if mc == 0 and ex == 0 and ts == 0 and pp >= 3 else None,
                                       tp_par[0] if mc == 0 and ex == 0 and ts == 0 and hc <= 1 else None,
                                       m_yev[pp] if mc == 0 else None],
                                inc=(sT, 1) if mc == MC - 1 else None,
                                tag=f"e{ex}u{ts}_{hc}_{tt}")
                        m_u_last = mmu
                        m_relu[pp] = P.op("scalar", lambda e, pp=pp: e.activation(
                            o_sb[pp % 2][:], fbanks[pp], AF.Relu),
                            waits=[mmu, m_sq_last], inc=(sA, 1),
                            tag=f"e{ex}relu{ts}_{hc}_{tt}")
                        m_sq_last = P.op("vector", lambda e, pp=pp, hc=hc, tt=tt: e.tensor_mul(
                            act3[:, hc, tt * 512:(tt + 1) * 512],
                            o_sb[pp % 2][:], o_sb[pp % 2][:]),
                            waits=[m_relu[pp],
                                   m_y_slab if hc == 0 and tt == 0 else None],
                            inc=(sV, 1), tag=f"e{ex}sq{ts}_{hc}_{tt}")
                for tc in range(8):
                    ch = ts * 8 + tc
                    for dh in range(2):
                        pp = (tc * 2 + dh) % 6
                        mmy = None
                        for hc in range(HC):
                            mmy = P.op("tensor", lambda e, hc=hc, tc=tc, dh=dh, pp=pp: e.matmul(
                                fbanks[pp],
                                act3[:, hc, tc * 128:(tc + 1) * 128],
                                w23[:, hc, dh * 512:(dh + 1) * 512],
                                start=(hc == 0), stop=(hc == HC - 1)),
                                waits=[m_sq_last if hc == 0 else None,
                                       m_w2[ex] if hc == 0 else None,
                                       m_relu[pp] if hc == 0 else None,
                                       m_yev[pp] if hc == 0 else None],
                                inc=(sT, 1) if hc == HC - 1 else None,
                                tag=f"e{ex}y{ts}_{tc}_{dh}")
                        m_y_last = mmy
                        m_yev[pp] = P.op("vector", lambda e, ch=ch, dh=dh, ex=ex, pp=pp: e.scalar_tensor_tensor(
                            x2[:, ch, dh * 512:(dh + 1) * 512],
                            fbanks[pp], c3[:, ch, ex:ex + 1],
                            x2[:, ch, dh * 512:(dh + 1) * 512],
                            ALU.mult, ALU.add),
                            waits=[mmy, m_cw if ex == 0 else None],
                            inc=(sV, 1), tag=f"e{ex}yev{ts}_{tc}_{dh}")
                m_y_slab = m_y_last
                if ex == E - 1:
                    m_out_slab[ts] = tuple(m for m in m_yev if m is not None)
            if ex + 1 < E:
                m_w1[ex + 1] = P.op("gpsimd", lambda e, ex=ex: e.dma_start(
                    out=w13, in_=w1_d[ex + 1].rearrange("(mc p) hh -> p mc hh", p=128)),
                    waits=[m_u_last], inc=(dW1, 16), tag=f"ld_w1_{ex + 1}")
                m_w2[ex + 1] = P.op("gpsimd", lambda e, ex=ex: e.dma_start(
                    out=w23, in_=w2_d[ex + 1].rearrange("(hc p) d -> p hc d", p=128)),
                    waits=[m_y_last], inc=(dW2, 16), tag=f"ld_w2_{ex + 1}")

        yr = y_d.rearrange("(c p) f -> p c f", p=128)
        for ts in range(2):
            P.op("sync", lambda e, ts=ts: e.dma_start(
                out=yr[:, ts * 8:(ts + 1) * 8, :], in_=x2[:, ts * 8:(ts + 1) * 8, :]),
                waits=list(m_out_slab[ts]), inc=(dOUT, 16), tag=f"store_y{ts}")
        P.op("sync", lambda e: e.wait_ge(dOUT.h, dOUT.n), tag="final_wait")

        block = En(nc.Block())
        block.sync(lambda e: P.replay("sync", e))
        block.gpsimd(lambda e: P.replay("gpsimd", e))
        block.tensor(lambda e: P.replay("tensor", e))
        block.vector(lambda e: P.replay("vector", e))
        block.scalar(lambda e: P.replay("scalar", e))

    return nc


_CACHE = {}


def _get_program(debug=()):
    key = tuple(sorted(debug))
    if key not in _CACHE:
        _CACHE[key] = _build(debug=key)
    return _CACHE[key]


def _make_in_maps(inputs, n_cores=8):
    x = np.ascontiguousarray(np.asarray(inputs["L0_attn_x"], dtype=np.float32))
    cos2, ssin2 = _rope_tables()
    common = {
        "wq": np.ascontiguousarray(np.asarray(inputs["L0_attn_wq"], dtype=np.float32)),
        "wk": np.ascontiguousarray(np.asarray(inputs["L0_attn_wk"], dtype=np.float32)),
        "wv": np.ascontiguousarray(np.asarray(inputs["L0_attn_wv"], dtype=np.float32)),
        "wo": np.ascontiguousarray(np.asarray(inputs["L0_attn_wo"], dtype=np.float32)),
        "wg": np.ascontiguousarray(np.asarray(inputs["L0_ffn_Wg"], dtype=np.float32)),
        "w1": np.ascontiguousarray(np.asarray(inputs["L0_ffn_W1"], dtype=np.float32)),
        "w2": np.ascontiguousarray(np.asarray(inputs["L0_ffn_W2"], dtype=np.float32)),
        "cos2": cos2, "ssin2": ssin2,
        "idn": np.eye(128, dtype=ml_dtypes.bfloat16),
        "swp": _swap_matrix(),
        "tri": _trimask(),
    }
    return [dict(common, x=np.ascontiguousarray(x[b])) for b in range(n_cores)]


def kernel(**inputs):
    nc = _get_program()
    in_maps = _make_in_maps(inputs)
    res = run_bass_kernel_spmd(nc, in_maps, core_ids=list(range(8)))
    return np.stack([np.asarray(r["y"], dtype=np.float32) for r in res.results], axis=0)


# version 17
# speedup vs baseline: 1.2115x; 1.0010x over previous
import numpy as np
import ml_dtypes
from contextlib import ExitStack
import concourse.bass as bass
import concourse.mybir as mybir
from concourse.bass_utils import run_bass_kernel_spmd

F32 = mybir.dt.float32
BF16 = mybir.dt.bfloat16
AF = mybir.ActivationFunctionType
ALU = mybir.AluOpType
AX = mybir.AxisListType

T = 2048
D = 1024
H = 8
DH = 128
E = 8
HID = 2048
NCH = T // 128
MC = D // 128
HC = HID // 128
NSLAB = 4
ENGINES = ("sync", "gpsimd", "tensor", "vector", "scalar")


class Sem:
    __slots__ = ("h", "n")
    def __init__(self, h):
        self.h = h
        self.n = 0


class Plan:
    def __init__(self):
        self.prog = {e: [] for e in ENGINES}
        self.seen = {e: {} for e in ENGINES}

    def op(self, eng, fn, waits=(), inc=None, tag=""):
        ws = []
        seen = self.seen[eng]
        for w in waits:
            if w is None:
                continue
            s, v = w
            if seen.get(s, 0) >= v:
                continue
            seen[s] = v
            ws.append((s, v))
        mil = None
        if inc is not None:
            s, k = inc
            s.n += k
            mil = (s, s.n)
        self.prog[eng].append((fn, ws, inc, tag))
        return mil

    def replay(self, eng_name, eng):
        for fn, ws, inc, tag in self.prog[eng_name]:
            for s, v in ws:
                eng.wait_ge(s.h, v)
            ins = fn(eng)
            if inc is not None:
                s, k = inc
                ins.then_inc(s.h, k)


def _rope_tables():
    half = DH // 2
    pos = np.arange(T, dtype=np.float64)[None, :]
    freqs = np.power(10000.0, -np.arange(half, dtype=np.float64) / half)[:, None]
    rad = freqs * pos
    s = 1.0 / np.sqrt(float(DH))
    cos = np.cos(rad) * s
    sin = np.sin(rad) * s
    cos2 = np.concatenate([cos, cos], axis=0).astype(ml_dtypes.bfloat16)
    ssin2 = np.concatenate([-sin, sin], axis=0).astype(ml_dtypes.bfloat16)
    return np.ascontiguousarray(cos2), np.ascontiguousarray(ssin2)


def _swap_matrix():
    sw = np.zeros((128, 128), dtype=ml_dtypes.bfloat16)
    for d in range(128):
        sw[(d + 64) % 128, d] = 1.0
    return sw


def _trimask():
    m = np.zeros((128, 128), dtype=ml_dtypes.bfloat16)
    i = np.arange(128)
    m[i[:, None] < i[None, :]] = -1e30
    return m


def _build(debug=()):
    nc = bass.Bass("TRN2", target_bir_lowering=False)

    x_d = nc.dram_tensor("x", [T, D], F32, kind="ExternalInput").ap()
    wq_d = nc.dram_tensor("wq", [D, H, DH], F32, kind="ExternalInput").ap()
    wk_d = nc.dram_tensor("wk", [D, H, DH], F32, kind="ExternalInput").ap()
    wv_d = nc.dram_tensor("wv", [D, H, DH], F32, kind="ExternalInput").ap()
    wo_d = nc.dram_tensor("wo", [H, DH, D], F32, kind="ExternalInput").ap()
    wg_d = nc.dram_tensor("wg", [D, E], F32, kind="ExternalInput").ap()
    w1_d = nc.dram_tensor("w1", [E, D, HID], F32, kind="ExternalInput").ap()
    w2_d = nc.dram_tensor("w2", [E, HID, D], F32, kind="ExternalInput").ap()
    cos_d = nc.dram_tensor("cos2", [128, T], BF16, kind="ExternalInput").ap()
    sin_d = nc.dram_tensor("ssin2", [128, T], BF16, kind="ExternalInput").ap()
    idn_d = nc.dram_tensor("idn", [128, 128], BF16, kind="ExternalInput").ap()
    swp_d = nc.dram_tensor("swp", [128, 128], BF16, kind="ExternalInput").ap()
    tri_d = nc.dram_tensor("tri", [128, 128], BF16, kind="ExternalInput").ap()
    y_d = nc.dram_tensor("y", [T, D], F32, kind="ExternalOutput").ap()

    taps = {}
    def tapdecl(name, shape, dtype):
        if name in debug:
            taps[name] = nc.dram_tensor("tap_" + name, shape, dtype,
                                        kind="ExternalOutput").ap()
    tapdecl("hT", [128, MC, T], BF16)
    tapdecl("lnst", [128, 8 * NCH], F32)
    tapdecl("lg", [128, NCH * E], F32)
    tapdecl("pr", [128, NCH * E], F32)
    tapdecl("m8", [128, NCH * E], F32)
    tapdecl("v", [128, NCH, D], BF16)
    tapdecl("q0", [128, T], BF16)
    tapdecl("k0", [128, T], BF16)
    tapdecl("pT0", [128, 16, 512], BF16)
    tapdecl("p15", [128, T], BF16)
    tapdecl("smst", [128, 48], F32)
    tapdecl("xattn", [128, NCH, D], F32)
    tapdecl("h2T", [128, MC, T], BF16)
    tapdecl("c", [128, NCH, E], F32)

    with ExitStack() as ctx:
        En = ctx.enter_context

        x_sb = En(nc.sbuf_tensor("x_sb", [128, NCH * D], F32))
        hT_sb = En(nc.sbuf_tensor("hT_sb", [128, MC * T], BF16))
        v_sb = En(nc.sbuf_tensor("v_sb", [128, NCH * D], BF16))
        arenaA = En(nc.sbuf_tensor("arenaA", [128, 16384], BF16))
        arenaB = En(nc.sbuf_tensor("arenaB", [128, 16384], BF16))
        cos_sb = En(nc.sbuf_tensor("cos_sb", [128, T], BF16))
        sin_sb = En(nc.sbuf_tensor("sin_sb", [128, T], BF16))
        o_sb = [En(nc.sbuf_tensor(f"o_sb{i}", [128, 512], BF16)) for i in range(2)]
        idn_sb = En(nc.sbuf_tensor("idn_sb", [128, 128], BF16))
        swp_sb = En(nc.sbuf_tensor("swp_sb", [128, 128], BF16))
        tri_sb = En(nc.sbuf_tensor("tri_sb", [128, 128], BF16))
        diag4_sb = En(nc.sbuf_tensor("diag4_sb", [128, 256], BF16))
        sqtrash = En(nc.sbuf_tensor("sqtrash", [128, 1024], BF16))
        wg_sb = En(nc.sbuf_tensor("wg_sb", [128, MC * E], BF16))
        ln_st = En(nc.sbuf_tensor("ln_st", [128, 8 * NCH], F32))
        sm_st = En(nc.sbuf_tensor("sm_st", [128, 12], F32))
        lg_sb = En(nc.sbuf_tensor("lg_sb", [128, NCH * E], F32))
        pr_sb = En(nc.sbuf_tensor("pr_sb", [128, NCH * E], F32))
        c_sb = En(nc.sbuf_tensor("c_sb", [128, NCH * E], F32))
        rt_sb = En(nc.sbuf_tensor("rt_sb", [128, 4 * NCH], F32))

        aA = arenaA[:]
        pT3 = aA[:, 0:8192].rearrange("p (jc i) -> p jc i", jc=16)
        wv3 = aA[:, 8192:16384].rearrange("p (mc d) -> p mc d", mc=MC)
        qraw = [aA[:, 8192 + i * 512:8704 + i * 512] for i in range(2)]
        swb = [aA[:, 9216 + i * 512:9728 + i * 512] for i in range(2)]
        rt1 = aA[:, 10240:10752]
        rt2 = aA[:, 10752:11264]
        w13 = aA.rearrange("p (mc hh) -> p mc hh", mc=MC)

        aB = arenaB[:]
        q_sb = aB[:, 0:2048]
        k_sb = aB[:, 2048:4096]
        p_sb = [aB[:, 4096:6144], aB[:, 6144:8192], aA[:, 8192:10240]]
        htok = [aB[:, 8192 + i * 1024:9216 + i * 1024] for i in range(2)]
        wqh = [aB[:, 10240 + i * 1024:11264 + i * 1024] for i in range(2)]
        wkh = [aB[:, 12288 + i * 1024:13312 + i * 1024] for i in range(2)]
        woh = [aB[:, 14336 + i * 1024:15360 + i * 1024] for i in range(2)]
        w23 = aB.rearrange("p (hc d) -> p hc d", hc=HC)

        x2 = x_sb[:].rearrange("p (c f) -> p c f", c=NCH)
        hT3 = hT_sb[:].rearrange("p (mc t) -> p mc t", mc=MC)
        v3 = v_sb[:].rearrange("p (c f) -> p c f", c=NCH)
        act3 = v3
        wg3 = wg_sb[:].rearrange("p (mc e) -> p mc e", mc=MC)
        lg3 = lg_sb[:].rearrange("p (c e) -> p c e", c=NCH)
        pr3 = pr_sb[:].rearrange("p (c e) -> p c e", c=NCH)
        m83 = lg_sb[:].rearrange("p (c e) -> p c e", c=NCH)
        c3 = c_sb[:].rearrange("p (c e) -> p c e", c=NCH)

        s_ps = En(nc.psum_tensor("s_ps", [128, 2048], F32))
        mi_ps = En(nc.psum_tensor("mi_ps", [128, 512], F32))
        tp_ps = En(nc.psum_tensor("tp_ps", [128, 512], F32))
        pp_ps = [En(nc.psum_tensor(f"pp{i}", [128, 512], F32)) for i in range(2)]

        dIN = Sem(En(nc.semaphore("dIN")))
        dX = [Sem(En(nc.semaphore(f"dX{i}"))) for i in range(4)]
        dWV = Sem(En(nc.semaphore("dWV")))
        dWG = Sem(En(nc.semaphore("dWG")))
        dWH = [Sem(En(nc.semaphore(f"dWH{i}"))) for i in range(2)]
        dW1 = Sem(En(nc.semaphore("dW1")))
        dW2 = Sem(En(nc.semaphore("dW2")))
        dOUT = Sem(En(nc.semaphore("dOUT")))
        sT = Sem(En(nc.semaphore("sT")))
        sV = Sem(En(nc.semaphore("sV")))
        sA = Sem(En(nc.semaphore("sA")))
        sG = Sem(En(nc.semaphore("sG")))

        P = Plan()
        tp_par = {0: None, 1: None}

        xr = x_d.rearrange("(c p) f -> p c f", p=128)
        m_xg = []
        for g in range(4):
            m_xg.append(P.op("sync", lambda e, g=g: e.dma_start(
                out=x2[:, 4 * g:4 * (g + 1), :], in_=xr[:, 4 * g:4 * (g + 1), :]),
                inc=(dX[g], 16), tag=f"ld_x{g}"))
        m_x = m_xg[3]
        P.op("sync", lambda e: e.dma_start(out=cos_sb[:], in_=cos_d), inc=(dIN, 16))
        P.op("sync", lambda e: e.dma_start(out=sin_sb[:], in_=sin_d), inc=(dIN, 16))
        P.op("sync", lambda e: e.dma_start(out=idn_sb[:], in_=idn_d), inc=(dIN, 16))
        P.op("sync", lambda e: e.dma_start(out=swp_sb[:], in_=swp_d), inc=(dIN, 16))
        m_tbl = P.op("sync", lambda e: e.dma_start(out=tri_sb[:], in_=tri_d), inc=(dIN, 16))
        m_cos = m_sin = m_idn = m_swp = m_tri = m_tbl

        m_wv = P.op("gpsimd", lambda e: e.dma_start(
            out=wv3, in_=wv_d.rearrange("(mc p) h d -> p mc (h d)", p=128)),
            inc=(dWV, 16), tag="ld_wv")
        m_wg = P.op("gpsimd", lambda e: e.dma_start(
            out=wg3, in_=wg_d.rearrange("(mc p) e -> p mc e", p=128)),
            inc=(dWG, 16), tag="ld_wg")

        def load_head_weights(h, waits):
            b = h % 2
            P.op("gpsimd", lambda e, h=h, b=b: e.dma_start(
                out=wqh[b].rearrange("p (mc d) -> p mc d", mc=MC),
                in_=wq_d[:, h, :].rearrange("(mc p) d -> p mc d", p=128)),
                waits=waits, inc=(dWH[b], 16), tag=f"ld_wq{h}")
            P.op("gpsimd", lambda e, h=h, b=b: e.dma_start(
                out=wkh[b].rearrange("p (mc d) -> p mc d", mc=MC),
                in_=wk_d[:, h, :].rearrange("(mc p) d -> p mc d", p=128)),
                inc=(dWH[b], 16), tag=f"ld_wk{h}")
            mo = P.op("gpsimd", lambda e, h=h, b=b: e.dma_start(
                out=woh[b], in_=wo_d[h]), inc=(dWH[b], 16), tag=f"ld_wo{h}")
            return mo

        m_wh = [None] * (H + 1)
        m_wh[0] = load_head_weights(0, [])

        def slot(i, c):
            return ln_st[:, i * NCH + c:i * NCH + c + 1]

        def SS(i):
            return ln_st[:, i * NCH:(i + 1) * NCH]

        def emit_layernorm(wait_src_fn, phase):
            m_sum = m_sq = None
            for c in range(NCH):
                m_sum = P.op("vector", lambda e, c=c: e.reduce_sum(
                    slot(0, c), x2[:, c, :], axis=AX.X),
                    waits=[wait_src_fn(c)], inc=(sV, 1) if c == NCH - 1 else None,
                    tag=f"{phase}_sum{c}")
                m_sq = P.op("scalar", lambda e, c=c: e.activation(
                    sqtrash[:], x2[:, c, :], AF.Square, accum_out=slot(1, c)),
                    waits=[wait_src_fn(c), m_sq], inc=(sA, 1),
                    tag=f"{phase}_sq{c}")
            mm1 = P.op("vector", lambda e: e.tensor_scalar_mul(SS(2), SS(0), 1.0 / D),
                       waits=[m_sum, m_sq], inc=(sV, 1), tag=f"{phase}_m")
            mm2 = P.op("vector", lambda e: e.tensor_scalar_mul(SS(3), SS(2), -1.0),
                       waits=[mm1], inc=(sV, 1))
            mm3 = P.op("vector", lambda e: e.tensor_scalar_mul(SS(4), SS(1), 1.0 / D),
                       inc=(sV, 1))
            mm4 = P.op("vector", lambda e: e.tensor_tensor(SS(7), SS(2), SS(2), ALU.mult),
                       waits=[mm1], inc=(sV, 1))
            mm5 = P.op("vector", lambda e: e.tensor_sub(SS(4), SS(4), SS(7)),
                       waits=[mm3, mm4], inc=(sV, 1))
            m_veps = P.op("vector", lambda e: e.tensor_scalar_add(SS(4), SS(4), 1e-5),
                          waits=[mm5], inc=(sV, 1), tag=f"{phase}_veps")
            m_ln = P.op("scalar", lambda e: e.activation(SS(7), SS(4), AF.Ln),
                        waits=[m_veps], inc=(sA, 1), tag=f"{phase}_ln")
            m_rstd = P.op("scalar", lambda e: e.activation(SS(5), SS(7), AF.Exp, scale=-0.5),
                          waits=[m_ln], inc=(sA, 1), tag=f"{phase}_rstd")
            m_bias = P.op("vector", lambda e: e.tensor_tensor(SS(6), SS(3), SS(5), ALU.mult),
                          waits=[m_rstd], inc=(sV, 1), tag=f"{phase}_bias")
            m_tev = [None] * (2 * NCH)
            m_ttr = [None] * (2 * NCH)
            for c in range(NCH):
                b = c % 2
                m_ap = P.op("vector", lambda e, c=c, b=b: e.tensor_scalar(
                    htok[b], x2[:, c, :], slot(5, c), slot(6, c), ALU.mult, ALU.add),
                    waits=[m_bias, m_rstd, m_ttr[2 * c - 4] if c >= 2 else None],
                    inc=(sV, 1), tag=f"{phase}_apply{c}")
                for g in range(2):
                    half = g
                    mm = None
                    for q in range(4):
                        mc = g * 4 + q
                        mm = P.op("tensor", lambda e, b=b, mc=mc, q=q, half=half: e.matmul(
                            (tp_ps if half == 0 else mi_ps)[:, q * 128:(q + 1) * 128],
                            htok[b][:, mc * 128:(mc + 1) * 128], idn_sb[:],
                            start=True, stop=True),
                            waits=[m_ap if q == 0 else None,
                                   m_idn if c == 0 and g == 0 and q == 0 else None,
                                   tp_par[half] if q == 0 else None],
                            inc=(sT, 1) if q == 3 else None, tag=f"{phase}_tr{c}_{mc}")
                    m_ttr[2 * c + g] = mm
                    m_tev[2 * c + g] = P.op("vector", lambda e, c=c, g=g, half=half: e.tensor_copy(
                        hT3[:, g * 4:(g + 1) * 4, c * 128:(c + 1) * 128],
                        (tp_ps if half == 0 else mi_ps)[:].rearrange(
                            "p (q t) -> p q t", q=4)),
                        waits=[mm], inc=(sV, 1), tag=f"{phase}_tev{c}_{g}")
                    tp_par[half] = m_tev[2 * c + g]
            return m_tev[2 * NCH - 1], m_ttr[2 * NCH - 1], [m_tev[2 * c + 1] for c in range(NCH)]

        mil_h1T, mil_h1T_tr, tev1 = emit_layernorm(lambda c: m_xg[c // 4], "ln1")
        if "lnst" in debug:
            P.op("sync", lambda e: e.dma_start(out=taps["lnst"], in_=ln_st[:]),
                 waits=[mil_h1T], inc=(dOUT, 16))
        if "hT" in debug:
            P.op("sync", lambda e: e.dma_start(out=taps["hT"], in_=hT3),
                 waits=[mil_h1T], inc=(dOUT, 16))

        m_vev = [None, None]
        last_v_mm = None
        for c in range(NCH):
            for nh in range(2):
                pp = (c * 2 + nh) % 2
                mms = None
                for mc in range(MC):
                    mms = P.op("tensor", lambda e, c=c, nh=nh, mc=mc, pp=pp: e.matmul(
                        pp_ps[pp][:], hT3[:, mc, c * 128:(c + 1) * 128],
                        wv3[:, mc, nh * 512:(nh + 1) * 512],
                        start=(mc == 0), stop=(mc == MC - 1)),
                        waits=[tev1[c] if mc == 0 else None,
                               m_wv if mc == 0 and c == 0 else None,
                               m_vev[pp] if mc == 0 else None],
                        inc=(sT, 1) if mc == MC - 1 else None, tag=f"v_mm{c}_{nh}")
                last_v_mm = mms
                m_vev[pp] = P.op("scalar", lambda e, c=c, nh=nh, pp=pp: e.activation(
                    v3[:, c, nh * 512:(nh + 1) * 512], pp_ps[pp][:], AF.Copy),
                    waits=[mms], inc=(sA, 1), tag=f"v_ev{c}_{nh}")
        mil_v_all = (m_vev[0], m_vev[1])
        if "v" in debug:
            P.op("sync", lambda e: e.dma_start(out=taps["v"], in_=v3),
                 waits=[m_vev[0], m_vev[1]], inc=(dOUT, 16))

        m_phB_done = [None] * H
        m_exp_bank = {}
        m_xadd_slab = {}
        m_diag_mm = {"last": None, 0: None, 1: None, 2: None}
        m_xadd = None
        m_xadd_bank = [None, None, None]
        m_oev = [None, None]
        m_proj_mm = [None, None]
        m_rope_done = [None] * H

        def SUM(par, k=None):
            o = par * 4
            return sm_st[:, o + k:o + k + 1] if k is not None else sm_st[:, o:o + 4]
        def SSUM(par):
            return sm_st[:, 8 + par:9 + par]
        def RR(par):
            return sm_st[:, 10 + par:11 + par]

        for h in range(H):
            b = h % 2
            if h + 1 < H:
                m_wh[h + 1] = load_head_weights(
                    h + 1, [m_rope_done[h - 1] if h >= 1 else None,
                            m_proj_mm[0] if h >= 1 else None,
                            m_proj_mm[1] if h >= 1 else None])

            wq3 = wqh[b].rearrange("p (mc d) -> p mc d", mc=MC)
            wk3 = wkh[b].rearrange("p (mc d) -> p mc d", mc=MC)
            raw_slots = [pp_ps[0][:], s_ps[:, 0:512], s_ps[:, 512:1024],
                         s_ps[:, 1024:1536]]
            swp_slots = [pp_ps[1][:], tp_ps[:]]
            m_raw_ev = [None] * 4
            m_swp_ev = [None] * 2
            m_rt1 = [None] * 2
            m_rope = None

            def emit_qk_proj(idx):
                mat, tt = divmod(idx, 4)
                w3 = wq3 if mat == 0 else wk3
                rs = idx % 4
                mms = None
                for mc in range(MC):
                    mms = P.op("tensor", lambda e, w3=w3, tt=tt, mc=mc, rs=rs: e.matmul(
                        raw_slots[rs], w3[:, mc, :], hT3[:, mc, tt * 512:(tt + 1) * 512],
                        start=(mc == 0), stop=(mc == MC - 1)),
                        waits=[m_wh[h] if mc == 0 else None,
                               tev1[tt * 4 + 3] if mc == 0 and h == 0 else None,
                               m_raw_ev[rs] if mc == 0 else None,
                               m_exp_bank.get(rs - 1) if mc == 0 and rs >= 1 else None,
                               m_oev[0] if mc == 0 and idx == 0 else None,
                               m_oev[1] if mc == 0 and idx == 0 else None,
                               m_xadd if mc == 0 and idx == 0 else None,
                               m_phB_done[h - 1] if mc == 0 and idx == 0 and h >= 1 else None,
                               mil_v_all[0] if mc == 0 and idx == 0 and h == 0 else None,
                               mil_v_all[1] if mc == 0 and idx == 0 and h == 0 else None],
                        inc=(sT, 1) if mc == MC - 1 else None, tag=f"h{h}qk{idx}")
                m_raw_ev[rs] = P.op("scalar", lambda e, idx=idx, rs=rs: e.activation(
                    qraw[idx % 2], raw_slots[rs], AF.Copy),
                    waits=[mms, m_rt1[idx % 2],
                           last_v_mm if h == 0 and idx == 0 else None],
                    inc=(sA, 1), tag=f"h{h}qraw{idx}")

            def emit_qk_rope(idx):
                nonlocal m_rope
                mat, tt = divmod(idx, 4)
                dst = q_sb if mat == 0 else k_sb
                rs = idx % 4
                ss = idx % 2
                msw = P.op("tensor", lambda e, ss=ss: e.matmul(
                    swp_slots[ss], swp_sb[:], qraw[ss], start=True, stop=True),
                    waits=[m_raw_ev[rs], m_swp, m_swp_ev[ss],
                           tp_par[0] if ss == 1 and idx == 1 else None,
                           m_xadd if idx == 0 else None],
                    inc=(sT, 1), tag=f"h{h}sw{idx}")
                m_swp_ev[ss] = P.op("vector", lambda e, ss=ss: e.tensor_copy(
                    swb[ss], swp_slots[ss]),
                    waits=[msw], inc=(sV, 1), tag=f"h{h}swev{idx}")
                m_rt1[ss] = P.op("vector", lambda e, ss=ss, tt=tt: e.tensor_mul(
                    rt1, qraw[ss], cos_sb[:, tt * 512:(tt + 1) * 512]),
                    waits=[m_cos, m_sin, m_raw_ev[rs],
                           m_phB_done[h - 1] if h >= 1 and idx == 0 else None,
                           last_v_mm if h == 0 and idx == 0 else None],
                    inc=(sV, 1), tag=f"h{h}rp1_{idx}")
                P.op("vector", lambda e, ss=ss, tt=tt: e.tensor_mul(
                    rt2, swb[ss], sin_sb[:, tt * 512:(tt + 1) * 512]))
                m_rope = P.op("vector", lambda e, dst=dst, tt=tt: e.tensor_add(
                    dst[:, tt * 512:(tt + 1) * 512], rt1, rt2),
                    inc=(sV, 1), tag=f"h{h}rp3_{idx}")

            for idx in range(8):
                emit_qk_proj(idx)
                if idx >= 1:
                    emit_qk_rope(idx - 1)
            emit_qk_rope(7)
            m_rope_done[h] = m_rope
            m_sbank_free = {0: m_raw_ev[1], 1: m_raw_ev[2], 2: m_raw_ev[3]}

            if h == 0 and "q0" in debug:
                P.op("sync", lambda e: e.dma_start(out=taps["q0"], in_=q_sb),
                     waits=[m_rope_done[0]], inc=(dOUT, 16))
            if h == 0 and "k0" in debug:
                P.op("sync", lambda e: e.dma_start(out=taps["k0"], in_=k_sb),
                     waits=[m_rope_done[0]], inc=(dOUT, 16))

            m_phB_local = None
            m_nmx = {}
            m_expm = {}
            m_tail = {}
            m_sumfree = {}
            m_trp_par = {0: None, 1: None}
            m_tgrp = {}
            m_ptev = {}
            m_ptev_last = None

            def emit_scores_bank(ci, k):
                par = ci % 2
                pbuf = ci % 3
                ph = k * 512
                jlen = (ci + 1) * 128
                j0 = k * 512
                j1 = min(j0 + 512, jlen)
                w = j1 - j0
                msc = P.op("tensor", lambda e, ci=ci, j0=j0, j1=j1, ph=ph, w=w: e.matmul(
                    s_ps[:, ph:ph + w], q_sb[:, ci * 128:(ci + 1) * 128],
                    k_sb[:, j0:j1], start=True, stop=True),
                    waits=[m_rope_done[h] if ci == 0 and k == 0 else None,
                           m_exp_bank.get(k),
                           m_sbank_free.get(k)],
                    inc=(sT, 1), tag=f"h{h}s{ci}_{k}")
                last = (j1 == jlen)
                m_msk = None
                if last:
                    lo = ph + (w - 128)
                    m_msk = P.op("vector", lambda e, lo=lo: e.tensor_tensor(
                        s_ps[:, lo:lo + 128], s_ps[:, lo:lo + 128],
                        tri_sb[:], ALU.add),
                        waits=[msc, m_tri], inc=(sV, 1), tag=f"h{h}msk{ci}")
                m_expm[(ci, k)] = P.op("scalar", lambda e, k=k, j0=j0, ph=ph, w=w, par=par, pbuf=pbuf: e.activation(
                    p_sb[pbuf][:, j0:j0 + w], s_ps[:, ph:ph + w], AF.Exp,
                    accum_out=SUM(par, k)),
                    waits=[msc, m_msk, m_diag_mm[pbuf],
                           m_sumfree.get(par),
                           m_rope_done[h] if pbuf == 2 else None],
                    inc=(sA, 1), tag=f"h{h}exp{ci}_{k}")
                m_exp_bank[k] = m_expm[(ci, k)]

            def emit_tail(ci):
                par = ci % 2
                dpar = ci % 2
                K = (ci + 1 + 3) // 4
                t5 = P.op("vector", lambda e, par=par, K=K: e.reduce_sum(
                    SSUM(par), SUM(par)[:, 0:K], axis=AX.X),
                    waits=[m_expm[(ci, K - 1)]], inc=(sV, 1), tag=f"h{h}tl_s{ci}")
                m_sumfree[par] = t5
                t6 = P.op("vector", lambda e, par=par: e.reciprocal(RR(par), SSUM(par)),
                          waits=[t5], inc=(sV, 1), tag=f"h{h}tl_r{ci}")
                m_tail[ci] = P.op("vector", lambda e, dpar=dpar: e.tensor_scalar_mul(
                    diag4_sb[:, dpar * 128:(dpar + 1) * 128], idn_sb[:], RR(dpar)),
                    waits=[t6, m_trp_par[dpar], m_idn],
                    inc=(sV, 1), tag=f"h{h}tl_dg{ci}")

            def emit_transposes(ci):
                nonlocal m_ptev_last
                pbuf = ci % 3
                cl = ci % 4
                sl = ci // 4
                njc = ci + 1
                for g0 in range(0, njc, 4):
                    g1 = min(g0 + 4, njc)
                    k = g0 // 4
                    half = k % 2
                    dst_ps = tp_ps if half == 0 else mi_ps
                    mmg = None
                    for jc in range(g0, g1):
                        mmg = P.op("tensor", lambda e, pbuf=pbuf, jc=jc, g0=g0, k=k, dst_ps=dst_ps: e.matmul(
                            dst_ps[:, (jc - g0) * 128:(jc - g0 + 1) * 128],
                            p_sb[pbuf][:, jc * 128:(jc + 1) * 128],
                            diag4_sb[:, (ci % 2) * 128:(ci % 2 + 1) * 128],
                            start=True, stop=True),
                            waits=[m_tail[ci] if jc == g0 else None,
                                   tp_par[half] if jc == g0 else None,
                                   m_swp_ev[1] if jc == g0 and half == 0 else None,
                                   m_phB_local if jc == g0 and g0 == 0 and cl == 0 else None,
                                   m_oev[0] if jc == g0 and half == 1 else None,
                                   m_oev[1] if jc == g0 and half == 1 else None,
                                   m_phB_done[h - 1] if jc == g0 and g0 == 0 and ci == 0 and h >= 1 else None,
                                   last_v_mm if jc == g0 and g0 == 0 and ci == 0 and h == 0 else None],
                            inc=(sT, 1) if jc == g1 - 1 else None,
                            tag=f"h{h}pt{ci}_{jc}")
                    m_diag_mm[pbuf] = mmg
                    m_diag_mm["last"] = mmg
                    m_trp_par[ci % 2] = mmg
                    if half == 0:
                        m_ptev_last = P.op("scalar", lambda e, cl=cl, g0=g0, g1=g1: e.activation(
                            pT3[:, g0:g1, cl * 128:(cl + 1) * 128],
                            tp_ps[:, 0:(g1 - g0) * 128].rearrange(
                                "p (g i) -> p g i", g=g1 - g0), AF.Copy),
                            waits=[mmg], inc=(sA, 1), tag=f"h{h}ptev{ci}_{g0}")
                    else:
                        m_ptev_last = P.op("vector", lambda e, cl=cl, g0=g0, g1=g1: e.tensor_copy(
                            pT3[:, g0:g1, cl * 128:(cl + 1) * 128],
                            mi_ps[:, 0:(g1 - g0) * 128].rearrange(
                                "p (g i) -> p g i", g=g1 - g0)),
                            waits=[mmg], inc=(sV, 1), tag=f"h{h}ptev{ci}_{g0}")
                    tp_par[half] = m_ptev_last

            def emit_phaseB(sl):
                nonlocal m_phB_local, m_xadd
                par = sl % 2
                m_omm = None
                for jc in range(sl * 4 + 4):
                    lo = max(0, jc - sl * 4) * 128
                    m_omm = P.op("tensor", lambda e, jc=jc, lo=lo, h=h: e.matmul(
                        mi_ps[:, lo:512] if lo else mi_ps[:],
                        v3[:, jc, h * 128:(h + 1) * 128],
                        pT3[:, jc, lo:512] if lo else pT3[:, jc, :],
                        start=(jc == 0), stop=(jc == sl * 4 + 3)),
                        waits=[m_ptev_last if jc == 0 else None,
                               m_oev[1 - par] if jc == 0 else None,
                               tp_par[0] if jc == 0 else None,
                               tp_par[1] if jc == 0 else None],
                        inc=(sT, 1) if jc == sl * 4 + 3 else None, tag=f"h{h}o{sl}_{jc}")
                m_phB_local = m_omm
                m_phB_done[h] = m_omm
                m_oev[par] = P.op("vector", lambda e, par=par: e.tensor_copy(
                    o_sb[par][:], mi_ps[:]),
                    waits=[m_omm, m_proj_mm[par]], inc=(sV, 1), tag=f"h{h}oev{sl}")
                tp_par[1] = m_oev[par]
                pbanks = [pp_ps[1], pp_ps[0], tp_ps]
                for cl in range(4):
                    for mh in range(2):
                        bank = (cl * 2 + mh) % 3
                        mpj = P.op("tensor", lambda e, par=par, cl=cl, mh=mh, b=b, bank=bank, pbanks=pbanks: e.matmul(
                            pbanks[bank][:], o_sb[par][:, cl * 128:(cl + 1) * 128],
                            woh[b][:, mh * 512:(mh + 1) * 512],
                            start=True, stop=True),
                            waits=[m_oev[par], m_xadd_bank[bank],
                                   tp_par[0] if bank == 2 else None,
                                   m_raw_ev[0] if cl == 0 and mh == 1 else None,
                                   m_swp_ev[0] if cl == 0 and mh == 0 else None],
                            inc=(sT, 1), tag=f"h{h}pj{sl}_{cl}_{mh}")
                        m_proj_mm[par] = mpj
                        m_xadd = P.op("vector", lambda e, sl=sl, cl=cl, mh=mh, bank=bank, pbanks=pbanks: e.tensor_tensor(
                            x2[:, sl * 4 + cl, mh * 512:(mh + 1) * 512],
                            x2[:, sl * 4 + cl, mh * 512:(mh + 1) * 512],
                            pbanks[bank][:], ALU.add),
                            waits=[mpj], inc=(sV, 1), tag=f"h{h}xadd{sl}_{cl}_{mh}")
                        m_xadd_bank[bank] = m_xadd
                        if bank == 2:
                            tp_par[0] = m_xadd
                if h == H - 1:
                    m_xadd_slab[sl] = m_xadd

            for step in range(18):
                if step < 16:
                    ci = step
                    K = ((ci + 1) * 128 + 511) // 512
                    for k in range(K):
                        emit_scores_bank(ci, k)
                if 1 <= step <= 16:
                    emit_tail(step - 1)
                if step >= 6 and (step - 6) % 4 == 0 and step < 18:
                    emit_phaseB((step - 6) // 4)
                if step >= 2:
                    emit_transposes(step - 2)
            emit_phaseB(3)

            if h == 0 and "pT0" in debug:
                P.op("sync", lambda e: e.dma_start(out=taps["pT0"], in_=pT3),
                     waits=[m_ptev_last], inc=(dOUT, 16))
            if h == 0 and "p15" in debug:
                P.op("sync", lambda e: e.dma_start(out=taps["p15"], in_=p_sb[1]),
                     waits=[m_expm[(15, 3)], m_tail[15]], inc=(dOUT, 16))
            if h == 0 and "smst" in debug:
                P.op("sync", lambda e: e.dma_start(out=taps["smst"], in_=sm_st[:]),
                     waits=[m_tail[15]], inc=(dOUT, 16))

        if "xattn" in debug:
            P.op("sync", lambda e: e.dma_start(out=taps["xattn"], in_=x2),
                 waits=[m_xadd], inc=(dOUT, 16))

        mil_h2T, mil_h2T_tr, tev2 = emit_layernorm(lambda c: m_xadd_slab.get(c // 4, m_xadd), "ln2")
        if "h2T" in debug:
            P.op("sync", lambda e: e.dma_start(out=taps["h2T"], in_=hT3),
                 waits=[mil_h2T], inc=(dOUT, 16))

        m_rmm = None
        for c in range(NCH):
            for mc in range(MC):
                m_rmm = P.op("tensor", lambda e, c=c, mc=mc: e.matmul(
                    s_ps[:, c * 8:(c + 1) * 8], hT3[:, mc, c * 128:(c + 1) * 128],
                    wg3[:, mc, :], start=(mc == 0), stop=(mc == MC - 1)),
                    waits=[tev2[c] if mc == 0 else None,
                           m_exp_bank.get(0) if c == 0 and mc == 0 else None,
                           m_wg if c == 0 and mc == 0 else None],
                    inc=(sT, 1) if mc == MC - 1 else None, tag=f"rt_mm{c}")
        m_lg = P.op("vector", lambda e: e.tensor_copy(lg_sb[:], s_ps[:, 0:NCH * E]),
                    waits=[m_rmm], inc=(sV, 1), tag="rt_ev")
        R = lambda i, c: rt_sb[:, i * NCH + c:i * NCH + c + 1]
        m_rnm = [None] * NCH
        for c in range(NCH):
            m_rnm[c] = P.op("vector", lambda e, c=c: e.reduce_max(
                R(0, c), lg3[:, c, :], axis=AX.X, negate=True),
                waits=[m_lg], inc=(sV, 1), tag=f"rt_mx{c}")
        m_rex = [None] * NCH
        for c in range(NCH):
            m_rex[c] = P.op("scalar", lambda e, c=c: e.activation(
                pr3[:, c, :], lg3[:, c, :], AF.Exp, bias=R(0, c), accum_out=R(1, c)),
                waits=[m_rnm[c]], inc=(sA, 1), tag=f"rt_exp{c}")
        m_rrc = P.op("vector", lambda e: e.reciprocal(
            rt_sb[:, 2 * NCH:3 * NCH], rt_sb[:, NCH:2 * NCH]),
            waits=[m_rex[NCH - 1]], inc=(sV, 1), tag="rt_recip")
        m_cw = None
        for c in range(NCH):
            e1 = P.op("vector", lambda e, c=c: e.tensor_scalar_mul(
                pr3[:, c, :], pr3[:, c, :], R(2, c)),
                waits=[m_rrc], inc=(sV, 1), tag=f"rt_pr{c}")
            m_m8 = P.op("vector", lambda e, c=c: e.max(out=m83[:, c, :], in_=pr3[:, c, :]),
                        waits=[e1], inc=(sV, 1), tag=f"rt_m8{c}")
            e2 = P.op("vector", lambda e, c=c: e.tensor_tensor(
                R(3, c), m83[:, c, 0:1], m83[:, c, 1:2], ALU.add),
                waits=[m_m8], inc=(sV, 1), tag=f"rt_dn{c}")
            e3 = P.op("vector", lambda e, c=c: e.tensor_scalar_add(R(3, c), R(3, c), 1e-8),
                      waits=[e2], inc=(sV, 1))
            m_rd = P.op("vector", lambda e, c=c: e.reciprocal(R(3, c), R(3, c)),
                        waits=[e3], inc=(sV, 1), tag=f"rt_rd{c}")
            e4 = P.op("vector", lambda e, c=c: e.tensor_single_scalar(
                c3[:, c, :], pr3[:, c, :], m83[:, c, 1:2], ALU.is_ge),
                waits=[m_rd], inc=(sV, 1), tag=f"rt_ind{c}")
            e5 = P.op("vector", lambda e, c=c: e.tensor_mul(
                c3[:, c, :], c3[:, c, :], pr3[:, c, :]),
                waits=[e4], inc=(sV, 1), tag=f"rt_ct{c}")
            m_cw = P.op("vector", lambda e, c=c: e.tensor_scalar_mul(
                c3[:, c, :], c3[:, c, :], R(3, c)),
                waits=[e5], inc=(sV, 1), tag=f"rt_cw{c}")
        if "c" in debug:
            P.op("sync", lambda e: e.dma_start(out=taps["c"], in_=c3),
                 waits=[m_cw], inc=(dOUT, 16))
        if "lg" in debug:
            P.op("sync", lambda e: e.dma_start(out=taps["lg"], in_=lg_sb[:]),
                 waits=[m_cw], inc=(dOUT, 16))
        if "pr" in debug:
            P.op("sync", lambda e: e.dma_start(out=taps["pr"], in_=pr_sb[:]),
                 waits=[m_cw], inc=(dOUT, 16))
        if "m8" in debug:
            P.op("sync", lambda e: e.dma_start(out=taps["m8"], in_=lg_sb[:]),
                 waits=[m_cw], inc=(dOUT, 16))

        m_w1 = [None] * (E + 1)
        m_w2 = [None] * (E + 1)
        m_w1[0] = P.op("gpsimd", lambda e: e.dma_start(
            out=w13, in_=w1_d[0].rearrange("(mc p) hh -> p mc hh", p=128)),
            waits=[m_phB_done[H - 1], m_rope_done[H - 1]], inc=(dW1, 16), tag="ld_w1_0")
        m_w2[0] = P.op("gpsimd", lambda e: e.dma_start(
            out=w23, in_=w2_d[0].rearrange("(hc p) d -> p hc d", p=128)),
            waits=[mil_h2T_tr, m_xadd], inc=(dW2, 16), tag="ld_w2_0")

        fbanks = [pp_ps[0][:], pp_ps[1][:], mi_ps[:],
                  s_ps[:, 0:512], s_ps[:, 512:1024], s_ps[:, 1024:1536]]
        rscr = [o_sb[0][:], o_sb[1][:], sqtrash[:, 0:512]]
        m_relu = [None] * 6
        m_yev = [None] * 6
        m_out_slab = {}
        m_u_last = None
        m_y_last = None
        m_y_slab = None
        for ex in range(E):
            m_sq_last = None
            for ts in range(2):
                for hc in range(HC):
                    for tt in range(2):
                        pp = (hc * 2 + tt) % 6
                        t0 = ts * 1024 + tt * 512
                        mmu = None
                        for mc in range(MC):
                            mmu = P.op("tensor", lambda e, hc=hc, t0=t0, mc=mc, pp=pp: e.matmul(
                                fbanks[pp], w13[:, mc, hc * 128:(hc + 1) * 128],
                                hT3[:, mc, t0:t0 + 512],
                                start=(mc == 0), stop=(mc == MC - 1)),
                                waits=[m_w1[ex] if mc == 0 else None,
                                       tev2[ts * 8 + 7] if mc == 0 and ex == 0 and hc == 0 and tt == 0 else None,
                                       m_relu[pp] if mc == 0 else None,
                                       m_rmm if mc == 0 and ex == 0 and ts == 0 and hc == 0 and tt == 0 else None,
                                       m_xadd_bank[0] if mc == 0 and ex == 0 and ts == 0 and hc == 0 and tt == 0 else None,
                                       m_xadd_bank[1] if mc == 0 and ex == 0 and ts == 0 and hc == 0 and tt == 0 else None,
                                       m_phB_done[H - 1] if mc == 0 and ex == 0 and ts == 0 and hc == 0 and tt == 0 else None,
                                       m_lg if mc == 0 and ex == 0 and ts == 0 and hc <= 2 else None,
                                       m_exp_bank.get(pp - 3) if mc == 0 and ex == 0 and ts == 0 and pp >= 3 else None,
                                       tp_par[0] if mc == 0 and ex == 0 and ts == 0 and hc <= 1 else None,
                                       m_yev[pp] if mc == 0 else None],
                                inc=(sT, 1) if mc == MC - 1 else None,
                                tag=f"e{ex}u{ts}_{hc}_{tt}")
                        m_u_last = mmu
                        m_relu[pp] = P.op("scalar", lambda e, pp=pp: e.activation(
                            rscr[pp % 3], fbanks[pp], AF.Relu),
                            waits=[mmu, m_sq_last], inc=(sA, 1),
                            tag=f"e{ex}relu{ts}_{hc}_{tt}")
                        m_sq_last = P.op("vector", lambda e, pp=pp, hc=hc, tt=tt: e.tensor_mul(
                            act3[:, hc, tt * 512:(tt + 1) * 512],
                            rscr[pp % 3], rscr[pp % 3]),
                            waits=[m_relu[pp],
                                   m_y_slab if hc == 0 and tt == 0 else None],
                            inc=(sV, 1), tag=f"e{ex}sq{ts}_{hc}_{tt}")
                for tc in range(8):
                    ch = ts * 8 + tc
                    for dh in range(2):
                        pp = (tc * 2 + dh) % 6
                        mmy = None
                        for hc in range(HC):
                            mmy = P.op("tensor", lambda e, hc=hc, tc=tc, dh=dh, pp=pp: e.matmul(
                                fbanks[pp],
                                act3[:, hc, tc * 128:(tc + 1) * 128],
                                w23[:, hc, dh * 512:(dh + 1) * 512],
                                start=(hc == 0), stop=(hc == HC - 1)),
                                waits=[m_sq_last if hc == 0 else None,
                                       m_w2[ex] if hc == 0 else None,
                                       m_relu[pp] if hc == 0 else None,
                                       m_yev[pp] if hc == 0 else None],
                                inc=(sT, 1) if hc == HC - 1 else None,
                                tag=f"e{ex}y{ts}_{tc}_{dh}")
                        m_y_last = mmy
                        m_yev[pp] = P.op("vector", lambda e, ch=ch, dh=dh, ex=ex, pp=pp: e.scalar_tensor_tensor(
                            x2[:, ch, dh * 512:(dh + 1) * 512],
                            fbanks[pp], c3[:, ch, ex:ex + 1],
                            x2[:, ch, dh * 512:(dh + 1) * 512],
                            ALU.mult, ALU.add),
                            waits=[mmy, m_cw if ex == 0 else None],
                            inc=(sV, 1), tag=f"e{ex}yev{ts}_{tc}_{dh}")
                m_y_slab = m_y_last
                if ex == E - 1:
                    m_out_slab[ts] = tuple(m for m in m_yev if m is not None)
            if ex + 1 < E:
                m_w1[ex + 1] = P.op("gpsimd", lambda e, ex=ex: e.dma_start(
                    out=w13, in_=w1_d[ex + 1].rearrange("(mc p) hh -> p mc hh", p=128)),
                    waits=[m_u_last], inc=(dW1, 16), tag=f"ld_w1_{ex + 1}")
                m_w2[ex + 1] = P.op("gpsimd", lambda e, ex=ex: e.dma_start(
                    out=w23, in_=w2_d[ex + 1].rearrange("(hc p) d -> p hc d", p=128)),
                    waits=[m_y_last], inc=(dW2, 16), tag=f"ld_w2_{ex + 1}")

        yr = y_d.rearrange("(c p) f -> p c f", p=128)
        for ts in range(2):
            P.op("sync", lambda e, ts=ts: e.dma_start(
                out=yr[:, ts * 8:(ts + 1) * 8, :], in_=x2[:, ts * 8:(ts + 1) * 8, :]),
                waits=list(m_out_slab[ts]), inc=(dOUT, 16), tag=f"store_y{ts}")
        P.op("sync", lambda e: e.wait_ge(dOUT.h, dOUT.n), tag="final_wait")

        block = En(nc.Block())
        block.sync(lambda e: P.replay("sync", e))
        block.gpsimd(lambda e: P.replay("gpsimd", e))
        block.tensor(lambda e: P.replay("tensor", e))
        block.vector(lambda e: P.replay("vector", e))
        block.scalar(lambda e: P.replay("scalar", e))

    return nc


_CACHE = {}


def _get_program(debug=()):
    key = tuple(sorted(debug))
    if key not in _CACHE:
        _CACHE[key] = _build(debug=key)
    return _CACHE[key]


def _make_in_maps(inputs, n_cores=8):
    x = np.ascontiguousarray(np.asarray(inputs["L0_attn_x"], dtype=np.float32))
    cos2, ssin2 = _rope_tables()
    common = {
        "wq": np.ascontiguousarray(np.asarray(inputs["L0_attn_wq"], dtype=np.float32)),
        "wk": np.ascontiguousarray(np.asarray(inputs["L0_attn_wk"], dtype=np.float32)),
        "wv": np.ascontiguousarray(np.asarray(inputs["L0_attn_wv"], dtype=np.float32)),
        "wo": np.ascontiguousarray(np.asarray(inputs["L0_attn_wo"], dtype=np.float32)),
        "wg": np.ascontiguousarray(np.asarray(inputs["L0_ffn_Wg"], dtype=np.float32)),
        "w1": np.ascontiguousarray(np.asarray(inputs["L0_ffn_W1"], dtype=np.float32)),
        "w2": np.ascontiguousarray(np.asarray(inputs["L0_ffn_W2"], dtype=np.float32)),
        "cos2": cos2, "ssin2": ssin2,
        "idn": np.eye(128, dtype=ml_dtypes.bfloat16),
        "swp": _swap_matrix(),
        "tri": _trimask(),
    }
    return [dict(common, x=np.ascontiguousarray(x[b])) for b in range(n_cores)]


def kernel(**inputs):
    nc = _get_program()
    in_maps = _make_in_maps(inputs)
    res = run_bass_kernel_spmd(nc, in_maps, core_ids=list(range(8)))
    return np.stack([np.asarray(r["y"], dtype=np.float32) for r in res.results], axis=0)
